# version 25
# baseline (speedup 1.0000x reference)
import numpy as np
from contextlib import ExitStack

import concourse.bass as bass
import concourse.mybir as mybir
from concourse.bass_utils import run_bass_kernel_spmd

F32 = mybir.dt.float32
BF16 = mybir.dt.bfloat16
AF = mybir.ActivationFunctionType
ALU = mybir.AluOpType
AXX = mybir.AxisListType.X

SEQ, DM, SD = 4096, 1024, 256
NT, TT = 8, 512
EPS = 1e-5
ENGINES = ("tensor", "vector", "scalar", "sync", "gpsimd")
STAGE = "full"


class P:

    def __init__(self, nc):
        self.nc = nc
        self.th = {e: [] for e in ENGINES}
        self.cnt = {}
        self.hand = {}
        self.waited = {e: {} for e in ENGINES}
        self.own = {"T": "tensor"}

    def mksem(self, stack, *names):
        self.stack = stack
        for n in names:
            self.hand[n] = stack.enter_context(self.nc.semaphore(n))
            self.cnt[n] = 0

    def chan(self, n):
        if n not in self.hand:
            self.hand[n] = self.stack.enter_context(self.nc.semaphore(n))
            self.cnt[n] = 0

    def emit(self, eng, fn, waits=(), inc=None, n=1):
        if eng == "scalar":
            waits = list(waits) + [("S", self.cnt["S"])]
        elif eng == "vector":
            waits = list(waits) + [("V", self.cnt["V"])]
        clean = []
        for wv in waits:
            if wv is None:
                continue
            s, v = wv
            if v is None or v <= 0:
                continue
            if self.own.get(s) == eng:
                continue
            assert v <= self.cnt[s], (eng, s, v, self.cnt[s])
            clean.append((s, v))
        tok = None
        if inc is not None:
            self.cnt[inc] += n
            tok = self.cnt[inc]
        wd = self.waited[eng]
        real = []
        for s, v in clean:
            if wd.get(s, 0) < v:
                real.append((s, v))
                wd[s] = v
        hand = self.hand

        def thunk(e, fn=fn, real=tuple(real), inc=inc, n=n):
            for s, v in real:
                e.wait_ge(hand[s], v)
            inst = fn(e)
            if inc is not None:
                inst.then_inc(hand[inc], n)

        self.th[eng].append(thunk)
        return tok

    def all_waits(self):
        return [(s, self.cnt[s]) for s in self.cnt]


def build_nc(stage=STAGE):
    nc = bass.Bass("TRN2", target_bir_lowering=False)

    x_in = nc.dram_tensor("x_in", [SEQ, DM], F32, kind="ExternalInput")
    cvec = nc.dram_tensor("cvec", [DM], F32, kind="ExternalInput")
    wmod_d = nc.dram_tensor("wmod", [DM, 6 * DM], F32, kind="ExternalInput")
    wd_d = nc.dram_tensor("wd", [2, DM, SD], F32, kind="ExternalInput")
    bd_d = nc.dram_tensor("bd", [2, SD], F32, kind="ExternalInput")
    wi_d = nc.dram_tensor("wi", [2, DM, SD], F32, kind="ExternalInput")
    bi_d = nc.dram_tensor("bi", [2, SD], F32, kind="ExternalInput")
    al_d = nc.dram_tensor("alog", [2, SD], F32, kind="ExternalInput")
    wo_d = nc.dram_tensor("wo", [2, SD, DM], F32, kind="ExternalInput")
    bo_d = nc.dram_tensor("bo", [2, DM], F32, kind="ExternalInput")
    ws_d = nc.dram_tensor("wsm", [2, SD, DM], F32, kind="ExternalInput")
    bs_d = nc.dram_tensor("bsv", [2, DM], F32, kind="ExternalInput")
    gw_d = nc.dram_tensor("gw", [DM, 8], F32, kind="ExternalInput")
    gbias_d = nc.dram_tensor("gbias", [1, 8], F32, kind="ExternalInput")
    wg_d = nc.dram_tensor("wg", [8, DM, 512], F32, kind="ExternalInput")
    wu_d = nc.dram_tensor("wu", [8, DM, 512], F32, kind="ExternalInput")
    wn_d = nc.dram_tensor("wdn", [8, 512, DM], F32, kind="ExternalInput")
    sg_d = nc.dram_tensor("swg", [DM, DM], F32, kind="ExternalInput")
    su_d = nc.dram_tensor("swu", [DM, DM], F32, kind="ExternalInput")
    sd_d = nc.dram_tensor("swd", [DM, DM], F32, kind="ExternalInput")
    cst_d = nc.dram_tensor("cst_hbm", [128, 257], F32, kind="ExternalInput")

    scr_row = nc.dram_tensor("scr_row", [6 * DM], F32)
    scr_w8 = nc.dram_tensor("scr_w8", [8, 1024], F32)
    scr_e = nc.dram_tensor("scr_e", [2, 128, SEQ], F32)
    scr_f = nc.dram_tensor("scr_f", [2, 128, SEQ], F32)
    scr_xfm = nc.dram_tensor("scr_xfm", [8, 128, SEQ], F32)
    scr_h2 = nc.dram_tensor("scr_h2", [8, 128, SEQ], F32)
    scr_xtm = nc.dram_tensor("scr_xtm", [SEQ, DM], F32)
    scr_gh = nc.dram_tensor("scr_gh", [10, 128, 8, 512], BF16)
    scr_gl = nc.dram_tensor("scr_gl", [10, 128, 8, 512], BF16)
    scr_uh = nc.dram_tensor("scr_uh", [10, 128, 8, 512], BF16)
    scr_ul = nc.dram_tensor("scr_ul", [10, 128, 8, 512], BF16)
    scr_dh = nc.dram_tensor("scr_dh", [10, 128, 4, 2, 512], BF16)
    scr_dl = nc.dram_tensor("scr_dl", [10, 128, 4, 2, 512], BF16)
    out_d = nc.dram_tensor("out", [SEQ, DM], F32, kind="ExternalOutput")

    p = P(nc)

    with ExitStack() as top:
        en = top.enter_context
        p.mksem(top, "T", "V", "S")

        cst = en(nc.sbuf_tensor("cst", [128, 257], F32))
        ident = cst[:, 0:128]
        onesb = cst[:, 128:256]
        invd = cst[:, 256:257]
        rowtmp = en(nc.sbuf_tensor("rowtmp", [1, 2, 512], F32))
        mod_cols = en(nc.sbuf_tensor("mod_cols", [128, 48], F32))
        oneca = en(nc.sbuf_tensor("oneca", [128, 8], F32))
        onecb = en(nc.sbuf_tensor("onecb", [128, 8], F32))
        gacl = en(nc.sbuf_tensor("gacl", [128, 8], F32))
        gbcl = en(nc.sbuf_tensor("gbcl", [128, 8], F32))
        gb_bc = en(nc.sbuf_tensor("gb_bc", [128, 2, 512], F32))
        negsA = en(nc.sbuf_tensor("negsA", [128, 4], F32))
        bo_cols = en(nc.sbuf_tensor("bo_cols", [128, 16], F32))
        bd_cols = en(nc.sbuf_tensor("bd_cols", [128, 4], F32))
        bi_cols = en(nc.sbuf_tensor("bi_cols", [128, 4], F32))
        bias_d = en(nc.sbuf_tensor("bias_d", [128, 4], F32))
        bias_i = en(nc.sbuf_tensor("bias_i", [128, 4], F32))
        nsd_row = en(nc.sbuf_tensor("nsd_row", [1, SD], F32))
        nsi_row = en(nc.sbuf_tensor("nsi_row", [1, SD], F32))
        cs_cols = en(nc.sbuf_tensor("cs_cols", [128, 8], F32))
        gc1 = en(nc.sbuf_tensor("gc1", [128, 8], F32))
        gw_sb = en(nc.sbuf_tensor("gw_sb", [128, 8, 8], F32))
        gbias_row = en(nc.sbuf_tensor("gbias_row", [1, 8], F32))
        gbias_bc = en(nc.sbuf_tensor("gbias_bc", [128, 4, 8], F32))
        wtok = en(nc.sbuf_tensor("wtok", [128, 32, 8], F32))
        rep2 = en(nc.sbuf_tensor("rep2", [128, 8], F32))
        ccols = en(nc.sbuf_tensor("ccols", [128, 8], F32))
        scc = en(nc.sbuf_tensor("scc", [128, 8], F32))
        trtmp = en(nc.sbuf_tensor("trtmp", [48, 128], F32))
        rt4 = en(nc.sbuf_tensor("rt4", [128, 4, 48], F32))
        m8b4 = en(nc.sbuf_tensor("m8b4", [128, 4, 8], F32))
        rp4 = en(nc.sbuf_tensor("rp4", [128, 4, 8], F32))

        pb = [en(nc.psum_tensor(f"pb{i}", [128, 512], F32)) for i in range(8)]
        bank_cons = [[] for _ in range(8)]

        def bank_waits(i):
            w = bank_cons[i]
            bank_cons[i] = []
            return w

        def note_bank(i, sem, tok):
            bank_cons[i].append((sem, tok))

        def mm(out_ap, lhsT, rhs, start, stop, waits=(), inc=None):
            return p.emit("tensor",
                          lambda e, o=out_ap, l=lhsT, r=rhs, a=start, b=stop:
                          e.matmul(o, l, r, start=a, stop=b),
                          waits=waits, inc=inc)

        def tr(out_ap, in_ap, id_ap, waits=(), inc=None):
            return p.emit("tensor",
                          lambda e, o=out_ap, i=in_ap, d=id_ap: e.transpose(o, i, d),
                          waits=waits, inc=inc)

        def act(out_ap, in_ap, func, bias=0.0, scale=1.0, waits=(), inc="S"):
            return p.emit("scalar",
                          lambda e, o=out_ap, i=in_ap, f=func, b=bias, s=scale:
                          e.activation(o, i, f, bias=b, scale=s),
                          waits=waits, inc=inc)

        def vop(fn, waits=(), inc="V"):
            return p.emit("vector", fn, waits=waits, inc=inc)

        def dma(eng, out_ap, in_ap, ch, waits=(), slow=False):
            p.chan(ch)
            swait = [(ch, p.cnt[ch])]
            tok = p.emit(eng,
                         lambda e, o=out_ap, i=in_ap, sl=slow:
                         e.dma_start(out=o, in_=i, allow_slow_non_contiguous=sl),
                         waits=list(waits) + swait, inc=ch, n=16)
            return (ch, tok)

        tk_cst = dma("sync", cst[:, :], cst_d.ap(), "c_cst")
        tk_c = dma("sync", ccols[:, :], cvec.ap().rearrange("(k p) -> p k", p=128), "c_c", slow=True)
        tk_gw = dma("sync", gw_sb[:, :, :], gw_d.ap().rearrange("(k p) e -> p k e", p=128), "c_gw", slow=True)
        tk_gbr = dma("sync", gbias_row[0:1, :], gbias_d.ap(), "c_gbr")
        tk_al = dma("sync", negsA[:, :], al_d.ap().rearrange("l (k p) -> p (l k)", p=128), "c_al", slow=True)
        tk_bo = dma("sync", bo_cols[:, :], bo_d.ap().rearrange("l (k p) -> p (l k)", p=128), "c_bo", slow=True)
        tk_bd = dma("sync", bd_cols[:, :], bd_d.ap().rearrange("l (m p) -> p (l m)", p=128), "c_bd", slow=True)
        tk_bi = dma("sync", bi_cols[:, :], bi_d.ap().rearrange("l (m p) -> p (l m)", p=128), "c_bi", slow=True)

        for s4 in range(4):
            t = mm(pb[0][:, s4 * 8:(s4 + 1) * 8], onesb[0:1, 0:128],
                   gbias_row[0:1, :], s4 == 0, s4 == 3,
                   waits=[tk_gbr, tk_cst], inc="T" if s4 == 3 else None)
        s_gbias = act(gbias_bc[:, :, :], pb[0][:, 0:32].rearrange(
            "p (s e) -> p s e", e=8), AF.Copy, waits=[("T", t)])
        note_bank(0, "S", s_gbias)

        s_sc = act(scc[:, :], ccols[:, :], AF.Silu, waits=[tk_c])

        with nc.sbuf_tensor("wmodb", [128, 2, 8, 512], F32) as wmodb:
            mm_toks = [None] * 12
            row_st = [None] * 12
            for j in range(12):
                sl = j % 2
                w = [("T", mm_toks[j - 2])] if j >= 2 else []
                wjt = dma("sync", wmodb[:, sl, :, :],
                          wmod_d.ap()[:, j * 512:(j + 1) * 512].rearrange(
                              "(k p) n -> p k n", p=128), f"wm{sl}", waits=w)
                bw = bank_waits(1)
                for k in range(8):
                    t = mm(pb[1][0:1, :], scc[:, k:k + 1], wmodb[:, sl, k, :],
                           k == 0, k == 7,
                           waits=[wjt, ("S", s_sc)] + (bw if k == 0 else []),
                           inc="T" if k == 7 else None)
                mm_toks[j] = t
                s_t = act(rowtmp[0:1, sl, :], pb[1][0:1, :], AF.Copy,
                          waits=[("T", t), row_st[j - 2]] if j >= 2
                          else [("T", t)])
                note_bank(1, "S", s_t)
                row_st[j] = dma("gpsimd", scr_row.ap()[j * 512:(j + 1) * 512][None, :],
                                rowtmp[0:1, sl, :], "gx", waits=[("S", s_t)])
            tk_modrow = row_st[11]

        tk_trt = dma("gpsimd", trtmp[0:48, :],
                     scr_row.ap().rearrange("(a b) -> a b", b=128), "gx",
                     waits=[tk_modrow])
        t = tr(pb[2][:, 0:48], trtmp[0:48, :], ident[0:48, 0:48],
               waits=[tk_trt] + bank_waits(2), inc="T")
        s_modc = act(mod_cols[:, :], pb[2][:, 0:48], AF.Copy, waits=[("T", t)])
        note_bank(2, "S", s_modc)

        v_oneca = vop(lambda e: e.tensor_scalar_add(oneca[:, :], mod_cols[:, 8:16], 1.0),
                      waits=[("S", s_modc)])
        v_onecb = vop(lambda e: e.tensor_scalar_add(onecb[:, :], mod_cols[:, 32:40], 1.0))

        def clamp(dst, lo_col, extra):
            sl = slice(lo_col, lo_col + 8)
            s_a = act(gc1[:, :], mod_cols[:, sl], AF.Abs,
                      waits=[("S", s_modc)] + extra)
            s_s = act(rep2[:, :], mod_cols[:, sl], AF.Sign)
            vop(lambda e: e.tensor_scalar_max(gc1[:, :], gc1[:, :], 0.25),
                waits=[("S", s_a), ("S", s_s)])
            return vop(lambda e, dst=dst: e.tensor_mul(dst, rep2[:, :], gc1[:, :]))

        v_ga = clamp(gacl[:, :], 16, [])
        v_gb = clamp(gbcl[:, :], 40, [("V", v_ga)])


        a1 = act(gc1[:, 0:4], negsA[:, :], AF.Abs, waits=[tk_al, ("V", v_gb)])
        act(gc1[:, 0:4], gc1[:, 0:4], AF.Exp, scale=-1.0)
        a3 = act(gc1[:, 0:4], gc1[:, 0:4], AF.Ln, bias=1.0)
        a4 = act(rep2[:, 0:4], negsA[:, :], AF.Relu)
        v5 = vop(lambda e: e.tensor_add(rep2[:, 0:4], rep2[:, 0:4], gc1[:, 0:4]),
                 waits=[("S", a4)])
        v_nsa = vop(lambda e: e.tensor_scalar_mul(negsA[:, :], rep2[:, 0:4], -1.0))
        v_rep2 = vop(lambda e: e.memset(rp4[:, :, :], 1.0e30))

        with ExitStack() as attn:
            an = attn.enter_context
            A = an(nc.sbuf_tensor("Ab", [128, 2, 512], F32))
            B = an(nc.sbuf_tensor("Bb", [128, 2, 512], F32))
            C = an(nc.sbuf_tensor("Cb", [128, 2, 512], F32))
            D = an(nc.sbuf_tensor("Db", [128, 2, 4096], F32))
            EqB = an(nc.sbuf_tensor("EqB", [128, 2, 2, 512], F32))
            FqB = an(nc.sbuf_tensor("FqB", [128, 2, 2, 512], F32))
            h2r = an(nc.sbuf_tensor("h2r", [128, 2, 512], F32))
            xt = an(nc.sbuf_tensor("xt", [128, 2, 8, 512], F32))
            p1 = [xt[:, s, 0:2, :].rearrange("p a t -> p (a t)") for s in range(2)]
            p1s = [xt[:, s, 2:4, :].rearrange("p a t -> p (a t)") for s in range(2)]
            tr_last = [None, None]
            ev_last = [None] * 32
            ld32 = [None] * 32
            st32 = [None] * 32
            for jt in range(32):
                sl = jt % 2
                w = [("T", tr_last[sl])] if jt >= 2 else []
                ld32[jt] = dma("sync", p1[sl], x_in.ap()[jt * 128:(jt + 1) * 128, :],
                               f"x{sl}", waits=w)
                s_tok = v_tok = None
                pv = st32[jt - 2] if jt >= 2 else None
                for c in range(8):
                    bank = 4 + sl
                    t = tr(pb[bank][:, 0:128], p1[sl][:, c * 128:(c + 1) * 128], ident,
                           waits=[ld32[jt], tk_cst] + bank_waits(bank),
                           inc="T")
                    if c == 7:
                        tr_last[sl] = t
                    if c % 2 == 0:
                        s_tok = act(p1s[sl][:, c * 128:(c + 1) * 128], pb[bank][:, 0:128],
                                    AF.Copy, waits=[("T", t), pv])
                        note_bank(bank, "S", s_tok)
                    else:
                        v_tok = vop(lambda e, sl=sl, c=c, bank=bank:
                                    e.tensor_copy(p1s[sl][:, c * 128:(c + 1) * 128],
                                                  pb[bank][:, 0:128]),
                                    waits=[("T", t), pv])
                        note_bank(bank, "V", v_tok)
                ev_last[jt] = (s_tok, v_tok)
                st32[jt] = dma("gpsimd",
                    scr_xfm.ap()[:, :, jt * 128:(jt + 1) * 128].rearrange("c p t -> p c t"),
                    p1s[sl].rearrange("p (c t) -> p c t", t=128), f"st{sl}",
                    waits=[("S", s_tok), ("V", v_tok)])
            tk_p1 = [st32[30], st32[31]]

            h1 = an(nc.sbuf_tensor("h1", [128, 8, 512], F32))
            sq = an(nc.sbuf_tensor("sq", [128, 2, 512], F32))
            rb2 = an(nc.sbuf_tensor("rb2", [128, 2, 512], F32))
            wdsb = an(nc.sbuf_tensor("wdsb", [128, 8, 2, 128], F32))
            wisb = an(nc.sbuf_tensor("wisb", [128, 8, 2, 128], F32))
            wdsb2 = an(nc.sbuf_tensor("wdsb2", [128, 8, 2, 128], F32))
            wisb2 = an(nc.sbuf_tensor("wisb2", [128, 8, 2, 128], F32))
            wosb = an(nc.sbuf_tensor("wosb", [128, 2, 8, 128], F32))
            wssb = an(nc.sbuf_tensor("wssb", [128, 2, 1024], F32))
            rows = an(nc.sbuf_tensor("rows", [1, 2, 2, 512], F32))
            bs_row = an(nc.sbuf_tensor("bs_row", [1, 2 * DM], F32))
            csrow = an(nc.sbuf_tensor("csrow", [1, DM], F32))
            g8row = an(nc.sbuf_tensor("g8row", [8, 128], F32))

            tk_bs = dma("sync", bs_row[0:1, :],
                        bs_d.ap().rearrange("l f -> (l f)")[None, :], "wt")
            t = tr(pb[2][0:8, 0:128], gbcl[:, :], ident[:, :],
                   waits=[("V", v_gb)] + bank_waits(2), inc="T")
            s_g8 = act(g8row[0:8, :], pb[2][0:8, 0:128], AF.Copy, waits=[("T", t)])
            note_bank(2, "S", s_g8)
            tk_g1 = dma("gpsimd", scr_row.ap()[0:DM].rearrange("(a b) -> a b", b=128),
                        g8row[0:8, :], "gx", waits=[("S", s_g8)])
            tk_gbrow = dma("gpsimd", csrow[0:1, :], scr_row.ap()[0:DM][None, :],
                           "gx", waits=[tk_g1])
            for fh in range(2):
                t = mm(pb[3][:, :], onesb[0:1, 0:128], csrow[0:1, fh * 512:(fh + 1) * 512],
                       True, True, waits=[tk_gbrow] + bank_waits(3), inc="T")
                s_gbbc = act(gb_bc[:, fh, :], pb[3][:, :], AF.Copy, waits=[("T", t)])
                note_bank(3, "S", s_gbbc)

            pg = list(tk_p1)
            tkwd = dma("sync", wdsb[:, :, :, :],
                       wd_d.ap()[0].rearrange("(k p) (m n) -> p k m n", p=128, n=128),
                       "wt", waits=pg)
            tkwi = dma("sync", wisb[:, :, :, :],
                       wi_d.ap()[0].rearrange("(k p) (m n) -> p k m n", p=128, n=128),
                       "wt", waits=pg)
            tkwd2 = dma("sync", wdsb2[:, :, :, :],
                        wd_d.ap()[1].rearrange("(k p) (m n) -> p k m n", p=128, n=128),
                        "wt", waits=pg)
            tkwi2 = dma("sync", wisb2[:, :, :, :],
                        wi_d.ap()[1].rearrange("(k p) (m n) -> p k m n", p=128, n=128),
                        "wt", waits=pg)
            tkwo = dma("sync", wosb[:, :, :, :],
                       wo_d.ap()[0].rearrange("(k p) (m n) -> p k m n", p=128, n=128),
                       "wt", waits=pg)
            tkws = dma("sync", wssb[:, :, :],
                       ws_d.ap()[0].rearrange("(c p) f -> p c f", p=128),
                       "wt", waits=pg)

            def bias_cols(dst2, w_sb, rhs_cols, add_cols, extra_waits):
                last_t = None
                for m in range(2):
                    bank = 5 + m
                    bw = bank_waits(bank)
                    for k in range(8):
                        last_t = mm(pb[bank][:, 0:1], w_sb[:, k, m, :], rhs_cols[:, k:k + 1],
                                    k == 0, k == 7,
                                    waits=extra_waits + (bw if k == 0 else []),
                                    inc="T" if k == 7 else None)
                    s_t = act(dst2[:, m:m + 1], pb[bank][:, 0:1], AF.Copy,
                              waits=[("T", last_t)])
                    note_bank(bank, "S", s_t)
                v_t = vop(lambda e, dst2=dst2, add_cols=add_cols:
                          e.tensor_add(dst2[:, 0:2], dst2[:, 0:2], add_cols),
                          waits=[("S", s_t)])
                return v_t, last_t

            v_b1d, t_bd = bias_cols(bias_d[:, 0:2], wdsb, mod_cols[:, 0:8], bd_cols[:, 0:2],
                                    [tkwd, ("S", s_modc), tk_bd])
            v_b1i, t_bi = bias_cols(bias_i[:, 0:2], wisb, mod_cols[:, 0:8], bi_cols[:, 0:2],
                                    [tkwi, ("S", s_modc), tk_bi])

            v_sc = None
            for k in range(8):
                v_sc = vop(lambda e, k=k: e.tensor_scalar_mul(
                    wdsb[:, k, :, :], wdsb[:, k, :, :], oneca[:, k:k + 1]),
                    waits=[("V", v_oneca), ("T", t_bd), ("T", t_bi)])
                v_sc = vop(lambda e, k=k: e.tensor_scalar_mul(
                    wisb[:, k, :, :], wisb[:, k, :, :], oneca[:, k:k + 1]))

            def colsums(dst_row, w_sb, prev_v):
                bw = bank_waits(7)
                last_t = None
                for k in range(8):
                    last_t = mm(pb[7][0:1, 0:256], onesb[:, 0:1],
                                w_sb[:, k, :, :].rearrange("p m n -> p (m n)"),
                                k == 0, k == 7,
                                waits=[("V", prev_v)] + (bw if k == 0 else []),
                                inc="T" if k == 7 else None)
                v_t = vop(lambda e, dst_row=dst_row:
                          e.tensor_scalar_mul(dst_row, pb[7][0:1, 0:256], -1.0),
                          waits=[("T", last_t)])
                note_bank(7, "V", v_t)
                return v_t

            v_nsd = colsums(nsd_row[0:1, :], wdsb, v_sc)
            v_nsi = colsums(nsi_row[0:1, :], wisb, v_sc)

            def stats_tile(src_tile, slot, negate_mu, src_waits, sq4=None):
                bw7 = bank_waits(7)
                for k in range(8):
                    t_mu = mm(pb[7][0:1, :], invd, src_tile[:, k, :], k == 0, k == 7,
                              waits=src_waits + (bw7 if k == 0 else []),
                              inc="T" if k == 7 else None)
                bw4 = bank_waits(4)
                slots = [sq[:, 0, :], sq[:, 1, :]]
                if sq4 is not None:
                    slots += [sq4[:, 0, :]]
                G = len(slots)
                sq_t = [None] * G
                for k in range(8):
                    g = k % G
                    row = g * 32 if G == 3 else 0
                    st_f = k < G if G == 3 else k == 0
                    sp_f = k >= 8 - G if G == 3 else k == 7
                    s_t = act(slots[g], src_tile[:, k, :], AF.Square,
                              waits=src_waits + ([("T", sq_t[g])] if sq_t[g] else []))
                    t_sq = mm(pb[4][row:row + 1, :], invd, slots[g],
                              st_f, sp_f,
                              waits=[("S", s_t)] + (bw4 if st_f else []),
                              inc="T")
                    sq_t[g] = t_sq
                mu_sl = rows[0:1, slot, 0, :]
                rstd_sl = rows[0:1, slot, 1, :]
                scr_sl = rowtmp[0:1, slot, :]
                v_mu = vop(lambda e, mu_sl=mu_sl, f=(-1.0 if negate_mu else 1.0):
                           e.tensor_scalar_mul(mu_sl, pb[7][0:1, :], f),
                           waits=[("T", t_mu)])
                note_bank(7, "V", v_mu)
                vop(lambda e, scr_sl=scr_sl, mu_sl=mu_sl: e.tensor_mul(scr_sl, mu_sl, mu_sl),
                    waits=src_waits)
                if G == 3:
                    vop(lambda e, rstd_sl=rstd_sl: e.tensor_copy(
                        rstd_sl, pb[4][0:1, :]), waits=[("T", t_sq)])
                    vop(lambda e, rstd_sl=rstd_sl: e.tensor_add(
                        rstd_sl, rstd_sl, pb[4][32:33, :]))
                    v_s4 = vop(lambda e, rstd_sl=rstd_sl: e.tensor_add(
                        rstd_sl, rstd_sl, pb[4][64:65, :]))
                    note_bank(4, "V", v_s4)
                    v_ve = vop(lambda e, scr_sl=scr_sl, rstd_sl=rstd_sl:
                               e.scalar_tensor_tensor(
                                   scr_sl, rstd_sl, EPS, scr_sl, ALU.add, ALU.subtract))
                else:
                    v_ve = vop(lambda e, scr_sl=scr_sl: e.scalar_tensor_tensor(
                        scr_sl, pb[4][0:1, :], EPS, scr_sl, ALU.add, ALU.subtract),
                        waits=[("T", t_sq)])
                    note_bank(4, "V", v_ve)
                act(scr_sl, scr_sl, AF.Ln, waits=[("V", v_ve)])
                s_rstd = act(rstd_sl, scr_sl, AF.Exp, scale=-0.5)
                return v_mu, s_rstd

            xt_tok = [None] * 8
            xt_lastT = [None] * 8
            pre1_spill = {}
            scan0_v = None

            scan_toks = [None] * 8
            zlv = [None] * 8

            def l0_front(j):
                sl = j % 2
                jsl = slice(j * 512, (j + 1) * 512)
                w = list(tk_p1) if j < 2 else [("T", xt_lastT[j - 2])] + list(tk_p1)
                xt_tok[j] = dma("sync", xt[:, sl, :, :],
                                scr_xfm.ap()[:, :, jsl].rearrange("c p t -> p c t"),
                                f"xt{sl}", waits=w)
                v_mu, s_rstd = stats_tile(xt[:, sl, :, :], sl, False,
                                          [xt_tok[j], tk_modrow, tk_trt])
                t = mm(pb[4][:, :], onesb[0:1, 0:128], rows[0:1, sl, 1, :], True, True,
                       waits=[("S", s_rstd)] + bank_waits(4), inc="T")
                s_rb = act(rb2[:, sl, :], pb[4][:, :], AF.Copy,
                           waits=[("T", t), ("V", p.cnt["V"])])
                note_bank(4, "S", s_rb)
                z_last_v = None
                for (w_sb, ns_row, dstbuf, base, v_ns) in (
                        (wdsb, nsd_row, A, 0, v_nsd), (wisb, nsi_row, B, 2, v_nsi)):
                    for m in range(2):
                        bank = base + m
                        bw = bank_waits(bank)
                        for k in range(8):
                            mm(pb[bank][:, :], w_sb[:, k, m, :], xt[:, sl, k, :],
                               k == 0, False,
                               waits=([("V", v_ns), xt_tok[j]] +
                                      (bw if k == 0 else [])))
                        t = mm(pb[bank][:, :], ns_row[0:1, m * 128:(m + 1) * 128],
                               rows[0:1, sl, 0, :], False, True,
                               waits=[("V", v_mu)], inc="T")
                        v_z = vop(lambda e, bank=bank, dstbuf=dstbuf, m=m, sl=sl:
                                  e.tensor_mul(dstbuf[:, m, :],
                                               pb[bank][:, :], rb2[:, sl, :]),
                                  waits=[("T", t), ("S", s_rb)])
                        note_bank(bank, "V", v_z)
                        z_last_v = v_z
                xt_lastT[j] = p.cnt["T"]
                zlv[j] = z_last_v

                for c in range(2):
                    Ah, Bh, Ch = A[:, c, :], B[:, c, :], C[:, c, :]
                    bwt = [("V", z_last_v), ("V", v_b1d), ("V", v_b1i), ("V", v_nsa)]
                    act(Ch, Ah, AF.Abs, bias=bias_d[:, c:c + 1], waits=bwt)
                    act(Ch, Ch, AF.Exp, scale=-1.0)
                    s_ln = act(Ch, Ch, AF.Ln, bias=1.0)
                    s_rl = act(Ah, Ah, AF.Relu, bias=bias_d[:, c:c + 1])
                    v_dl = vop(lambda e, Ah=Ah, Ch=Ch: e.tensor_add(Ah, Ah, Ch),
                               waits=[("S", s_ln), ("S", s_rl)])
                    s_dc = act(Ch, Ah, AF.Exp, scale=negsA[:, c:c + 1],
                               waits=[("V", v_dl)])
                    vop(lambda e, Ah=Ah, Bh=Bh, c=c: e.scalar_tensor_tensor(
                        Bh, Bh, bias_i[:, c:c + 1], Ah, ALU.add, ALU.mult))
                    init = 0.0 if j == 0 else D[:, c, j * 512 - 1:j * 512]
                    scan_toks[j] = vop(lambda e, Ch=Ch, Bh=Bh, c=c, j=j, init=init:
                                       e.tensor_tensor_scan(
                                           D[:, c, j * 512:(j + 1) * 512],
                                           Ch, Bh, init, ALU.mult, ALU.add),
                                       waits=[("S", s_dc)])

            def l0_back(j):
                jsl = slice(j * 512, (j + 1) * 512)
                h1_s = [None] * 8
                for m in range(8):
                    bank = 5 + (m % 2)
                    bw = bank_waits(bank)
                    for k in range(2):
                        t = mm(pb[bank][:, :], wosb[:, k, m, :], D[:, k, jsl],
                               k == 0, k == 1,
                               waits=[("V", scan_toks[j]), tkwo] + (bw if k == 0 else []),
                               inc="T" if k == 1 else None)
                    h1_s[m] = act(h1[:, m, :], pb[bank][:, :], AF.Identity,
                                  bias=bo_cols[:, m:m + 1],
                                  waits=[("T", t), tk_bo])
                    note_bank(bank, "S", h1_s[m])
                for (w_sb2, stgbuf, scr, base, tk_w) in (
                        (wdsb2, EqB, scr_e, 0, tkwd2), (wisb2, FqB, scr_f, 2, tkwi2)):
                    sl = j % 2
                    pre_s = [None, None]
                    for m in range(2):
                        bank = base + m
                        bw = bank_waits(bank)
                        for k in range(8):
                            t = mm(pb[bank][:, :], w_sb2[:, k, m, :], h1[:, k, :],
                                   k == 0, k == 7,
                                   waits=[tk_w, ("S", h1_s[k]),
                                          pre1_spill.get((j - 2, base))] +
                                         (bw if k == 0 else []),
                                   inc="T" if k == 7 else None)
                        pre_s[m] = act(stgbuf[:, sl, m, :], pb[bank][:, :], AF.Copy,
                                       waits=[("T", t)])
                        note_bank(bank, "S", pre_s[m])
                    ch = ("se" if base == 0 else "sf") + str(sl)
                    pre1_spill[(j, base)] = dma(
                        "gpsimd", scr.ap()[:, :, jsl].rearrange("c p t -> p c t"),
                        stgbuf[:, sl, :, :], ch,
                        waits=[("S", pre_s[0]), ("S", pre_s[1])])

            for j in range(9):
                if j < 8:
                    l0_front(j)
                if j >= 1:
                    l0_back(j - 1)
            scan0_v = scan_toks[7]

            for fh in range(2):
                bw5 = bank_waits(5)
                for c in range(2):
                    t = mm(pb[5][0:1, :],
                           D[:, c, 4095:4096], wssb[:, c, fh * 512:(fh + 1) * 512],
                           c == 0, False,
                           waits=[("V", scan0_v), tkws] + (bw5 if c == 0 else []))
                t = mm(pb[5][0:1, :], onesb[0:1, 0:1], bs_row[0:1, fh * 512:(fh + 1) * 512],
                       False, True, waits=[tk_bs], inc="T")
                s_cs = act(csrow[0:1, fh * 512:(fh + 1) * 512], pb[5][0:1, :], AF.Copy,
                           waits=[("T", t), tk_gbrow])
                note_bank(5, "S", s_cs)
            tk_c0a = dma("gpsimd", scr_row.ap()[0:DM][None, :], csrow[0:1, :],
                         "gx", waits=[("S", s_cs)])
            tk_cst0 = dma("gpsimd", trtmp[0:8, :],
                          scr_row.ap()[0:DM].rearrange("(a b) -> a b", b=128),
                          "gx", waits=[tk_c0a, ("S", s_modc)])
            t = tr(pb[6][:, 0:8], trtmp[0:8, :], ident[0:8, 0:8],
                   waits=[tk_cst0] + bank_waits(6), inc="T")
            s_csc = act(cs_cols[:, :], pb[6][:, 0:8], AF.Copy, waits=[("T", t)])
            note_bank(6, "S", s_csc)

            v_b2d, _ = bias_cols(bias_d[:, 2:4], wdsb2, cs_cols[:, :], bd_cols[:, 2:4],
                                 [tkwd2, ("S", s_csc)])
            v_b2i, _ = bias_cols(bias_i[:, 2:4], wisb2, cs_cols[:, :], bi_cols[:, 2:4],
                                 [tkwi2, ("S", s_csc)])

            tkwo1 = dma("sync", wosb[:, :, :, :],
                        wo_d.ap()[1].rearrange("(k p) (m n) -> p k m n", p=128, n=128),
                        "wt", waits=[("T", p.cnt["T"])])
            tkws1 = dma("sync", wssb[:, :, :],
                        ws_d.ap()[1].rearrange("(c p) f -> p c f", p=128),
                        "wt", waits=[("T", p.cnt["T"])])

            y0_all_T = p.cnt["T"]
            eq_tok = [None] * 8
            fq_tok = [None] * 8
            q_done_v = [None] * 8
            xt_wb = [None] * 8
            scan1_v = None
            for j8 in range(8):
                sl = j8 % 2
                qsl = slice(j8 * 512, (j8 + 1) * 512)
                wv = [pre1_spill[(6 + sl, 0)], pre1_spill[(6 + sl, 2)]]
                if j8 >= 2 and q_done_v[j8 - 2] is not None:
                    wv.append(("V", q_done_v[j8 - 2]))
                eq_tok[j8] = dma("sync", EqB[:, sl, :, :],
                                 scr_e.ap()[:, :, qsl].rearrange("c p t -> p c t"),
                                 f"eq{sl}", waits=wv)
                fq_tok[j8] = dma("sync", FqB[:, sl, :, :],
                                 scr_f.ap()[:, :, qsl].rearrange("c p t -> p c t"),
                                 f"fq{sl}", waits=wv)
                for c in range(2):
                    Ehh = EqB[:, sl, c, :]
                    Fhh = FqB[:, sl, c, :]
                    Ch = C[:, c, :]
                    bwt = [("V", v_b2d), ("V", v_b2i), eq_tok[j8], fq_tok[j8]]
                    act(Ch, Ehh, AF.Abs, bias=bias_d[:, 2 + c:3 + c], waits=bwt)
                    act(Ch, Ch, AF.Exp, scale=-1.0)
                    s_ln = act(Ch, Ch, AF.Ln, bias=1.0)
                    s_rl = act(Ehh, Ehh, AF.Relu, bias=bias_d[:, 2 + c:3 + c])
                    v_dl = vop(lambda e, Ehh=Ehh, Ch=Ch: e.tensor_add(Ehh, Ehh, Ch),
                               waits=[("S", s_ln), ("S", s_rl)])
                    s_dc = act(Ch, Ehh, AF.Exp, scale=negsA[:, 2 + c:3 + c],
                               waits=[("V", v_dl)])
                    vop(lambda e, Ehh=Ehh, Fhh=Fhh, c=c: e.scalar_tensor_tensor(
                        Fhh, Fhh, bias_i[:, 2 + c:3 + c], Ehh, ALU.add, ALU.mult))
                    init = 0.0 if j8 == 0 else D[:, c, j8 * 512 - 1:j8 * 512]
                    scan1_v = vop(lambda e, Ch=Ch, Fhh=Fhh, c=c, j8=j8, init=init:
                                  e.tensor_tensor_scan(D[:, c, j8 * 512:(j8 + 1) * 512],
                                                       Ch, Fhh, init, ALU.mult, ALU.add),
                                  waits=[("S", s_dc), ("T", y0_all_T)])
                q_done_v[j8] = scan1_v
                w = [("T", xt_lastT[6]), ("T", xt_lastT[7])] if j8 < 2 else                     [xt_wb[j8 - 2]]
                xt_tok[j8] = dma("sync", xt[:, sl, :, :],
                                 scr_xfm.ap()[:, :, qsl].rearrange("c p t -> p c t"),
                                 f"xt{sl}", waits=w + list(tk_p1))
                v_u1 = None
                for m in range(8):
                    bank = 5 + (m % 2)
                    bw = bank_waits(bank)
                    for k in range(2):
                        t = mm(pb[bank][:, :], wosb[:, k, m, :], D[:, k, qsl],
                               k == 0, k == 1,
                               waits=[("V", scan1_v), tkwo1] + (bw if k == 0 else []),
                               inc="T" if k == 1 else None)
                    v_u1 = vop(lambda e, bank=bank, m=m, sl=sl: e.scalar_tensor_tensor(
                        xt[:, sl, m, :], pb[bank][:, :], gacl[:, m:m + 1], xt[:, sl, m, :],
                        ALU.mult, ALU.add),
                        waits=[("T", t), xt_tok[j8], ("V", v_ga)])
                    note_bank(bank, "V", v_u1)
                xt_wb[j8] = dma("gpsimd",
                                scr_xfm.ap()[:, :, qsl].rearrange("c p t -> p c t"),
                                xt[:, sl, :, :], f"xw{sl}", waits=[("V", v_u1)])

            for fh in range(2):
                bw5 = bank_waits(5)
                for c in range(2):
                    t = mm(pb[5][0:1, :], D[:, c, 4095:4096],
                           wssb[:, c, fh * 512:(fh + 1) * 512], c == 0, False,
                           waits=[("V", scan1_v), tkws1] + (bw5 if c == 0 else []))
                t = mm(pb[5][0:1, :], onesb[0:1, 0:1], bs_row[0:1, DM + fh * 512:DM + (fh + 1) * 512],
                       False, True, inc="T")
                s_cs1 = act(csrow[0:1, fh * 512:(fh + 1) * 512], pb[5][0:1, :], AF.Copy,
                            waits=[("T", t), tk_cst0])
                note_bank(5, "S", s_cs1)
            tk_c1a = dma("gpsimd", scr_row.ap()[0:DM][None, :], csrow[0:1, :],
                         "gx", waits=[("S", s_cs1)])
            tk_cst1 = dma("gpsimd", trtmp[0:8, :],
                          scr_row.ap()[0:DM].rearrange("(a b) -> a b", b=128),
                          "gx", waits=[tk_c1a, ("S", s_csc)])
            t = tr(pb[6][:, 0:8], trtmp[0:8, :], ident[0:8, 0:8],
                   waits=[tk_cst1] + bank_waits(6), inc="T")
            s_cs1c = act(cs_cols[:, :], pb[6][:, 0:8], AF.Copy, waits=[("T", t)])
            note_bank(6, "S", s_cs1c)
            vop(lambda e: e.tensor_add(gc1[:, :], cs_cols[:, :], bo_cols[:, 8:16]),
                waits=[("S", s_cs1c), tk_bo])
            v_gc1 = vop(lambda e: e.tensor_mul(gc1[:, :], gacl[:, :], gc1[:, :]),
                        waits=[("V", v_ga)])

            p4_lastT = [None] * 8
            h2_st_tok = [None] * 8
            h2prev_l = [[None] * 8]
            xtm_st = {}
            v_up_toks = [None] * 8
            stats_out = [None] * 8

            def p4_b1(j):
                sl = j % 2
                jsl = slice(j * 512, (j + 1) * 512)
                if j < 2:
                    w = [xt_wb[6 + sl]]
                else:
                    w = [("T", p4_lastT[j - 2]), h2_st_tok[j - 2]]
                xt_tok[j] = dma("sync", xt[:, sl, :, :],
                                scr_xfm.ap()[:, :, jsl].rearrange("c p t -> p c t"),
                                f"xt{sl}", waits=w)
                v_up = vop(lambda e, sl=sl: e.tensor_tensor(
                    xt[:, sl, :, :], xt[:, sl, :, :],
                    gc1[:, :].rearrange("p (e o) -> p e o", o=1)
                    .broadcast_to([128, 8, 512]), ALU.add),
                    waits=[("V", v_gc1), xt_tok[j]])
                v_up_toks[j] = v_up
                v_mu, s_rstd = stats_tile(xt[:, sl, :, :], sl, True, [("V", v_up)],
                                          sq4=C)
                v_m2r = vop(lambda e, sl=sl: e.tensor_mul(
                    rowtmp[0:1, sl, :], rows[0:1, sl, 0, :], rows[0:1, sl, 1, :]),
                    waits=[("S", s_rstd)])
                stats_out[j] = (v_mu, s_rstd, v_m2r)

            def p4_b2(j):
                sl = j % 2
                jsl = slice(j * 512, (j + 1) * 512)
                v_up = v_up_toks[j]
                s_rstd, v_m2r = stats_out[j][1], stats_out[j][2]
                t_r = mm(pb[0][:, :], onesb[0:1, 0:128], rows[0:1, sl, 1, :], True, True,
                         waits=[("S", s_rstd)] + bank_waits(0), inc="T")
                t_m = mm(pb[1][:, :], onesb[0:1, 0:128], rowtmp[0:1, sl, :], True, True,
                         waits=[("V", v_m2r)] + bank_waits(1), inc="T")
                for s4 in range(4):
                    stg = h1[:, :, :].rearrange("p m t -> p (m t)")[:, s4 * 1024:(s4 + 1) * 1024]
                    s_tok = v_tok = None
                    for c in range(8):
                        bank = 5 + (c % 2)
                        bw = bank_waits(bank)
                        t = tr(pb[bank][:, 0:128], xt[:, sl, c, s4 * 128:(s4 + 1) * 128],
                               ident,
                               waits=[("V", v_up)] + xtm_st.get((j - 1, s4), []) + bw,
                               inc="T")
                        if c % 2 == 0:
                            s_tok = act(stg[:, c * 128:(c + 1) * 128], pb[bank][:, 0:128],
                                        AF.Copy, waits=[("T", t)])
                            note_bank(bank, "S", s_tok)
                        else:
                            v_tok = vop(lambda e, stg=stg, c=c, bank=bank: e.tensor_copy(
                                stg[:, c * 128:(c + 1) * 128], pb[bank][:, 0:128]),
                                waits=[("T", t)])
                            note_bank(bank, "V", v_tok)
                    rsl = slice((j * 4 + s4) * 128, (j * 4 + s4 + 1) * 128)
                    xtm_st[(j, s4)] = [dma("gpsimd", scr_xtm.ap()[rsl, :], stg,
                                           f"xm{s4}",
                                           waits=[("S", s_tok), ("V", v_tok)])]
                    if stage == "p4":
                        xtm_st[(j, s4)].append(
                            dma("gpsimd", out_d.ap()[rsl, :], stg, f"om{s4}",
                                waits=[("S", s_tok), ("V", v_tok)]))
                h2st_c = [None] * 8
                lbank = (2, 3, 5, 6)
                lg_tok = [None] * 4
                lgT = [None] * 8
                for c in range(8):
                    hsl = h2r[:, c % 2, :]
                    wv1 = [("T", t_r)]
                    if c >= 2:
                        wv1 += [h2st_c[c - 2], ("T", lgT[c - 2])]
                    elif j > 0:
                        wv1 += [h2prev_l[0][6 + c]]
                    v1 = vop(lambda e, hsl=hsl, c=c, sl=sl: e.tensor_mul(
                        hsl, xt[:, sl, c, :], pb[0][:, :]), waits=wv1)
                    note_bank(0, "V", v1)
                    v2 = vop(lambda e, hsl=hsl: e.tensor_add(hsl, hsl, pb[1][:, :]),
                             waits=[("T", t_m)])
                    note_bank(1, "V", v2)
                    v_h2 = vop(lambda e, hsl=hsl, c=c: e.tensor_scalar(
                        hsl, hsl, onecb[:, c:c + 1],
                        mod_cols[:, 24 + c:25 + c], ALU.mult, ALU.add),
                        waits=[("V", v_onecb)])
                    for s4 in range(4):
                        bw = bank_waits(lbank[s4]) if c == 0 else []
                        lg_tok[s4] = mm(pb[lbank[s4]][:, 0:8],
                                        hsl[:, s4 * 128:(s4 + 1) * 128],
                                        gw_sb[:, c, :], c == 0, c == 7,
                                        waits=[("V", v_h2), tk_gw] + bw,
                                        inc="T" if s4 == 3 else None)
                    lgT[c] = lg_tok[3]
                    h2st_c[c] = dma("gpsimd",
                                    scr_h2.ap()[c, :, jsl], hsl, f"h2{c % 2}",
                                    waits=[("V", v_h2)])
                h2_st_tok[j] = h2st_c[7]
                h2prev_l[0] = h2st_c
                R = rt4
                for s4 in range(4):
                    s_sg = act(R[:, s4, 0:8], pb[lbank[s4]][:, 0:8], AF.Exp, scale=-1.0,
                               waits=[("T", lgT[7]), ("V", p.cnt["V"])])
                    note_bank(lbank[s4], "S", s_sg)
                vop(lambda e: e.tensor_scalar_add(R[:, :, 0:8], R[:, :, 0:8], 1.0),
                    waits=[("S", s_sg)])
                vop(lambda e: e.reciprocal(R[:, :, 8:16], R[:, :, 0:8]))
                vop(lambda e: e.tensor_add(R[:, :, 8:16], R[:, :, 8:16], gbias_bc[:, :, :]),
                    waits=[("S", s_gbias)])
                vop(lambda e: e.tensor_add(R[:, :, 16:24:2], R[:, :, 8:16:2], R[:, :, 9:16:2]))
                vop(lambda e: e.tensor_copy(R[:, :, 17:24:2], R[:, :, 16:24:2]))
                vop(lambda e: e.tensor_max(R[:, :, 24:25], R[:, :, 16:17], R[:, :, 20:21]))
                vop(lambda e: e.tensor_tensor(R[:, :, 25:26], R[:, :, 16:17], R[:, :, 20:21], ALU.min))
                vop(lambda e: e.tensor_max(R[:, :, 26:27], R[:, :, 18:19], R[:, :, 22:23]))
                vop(lambda e: e.tensor_tensor(R[:, :, 27:28], R[:, :, 18:19], R[:, :, 22:23], ALU.min))
                vop(lambda e: e.tensor_tensor(R[:, :, 28:29], R[:, :, 24:25], R[:, :, 26:27], ALU.is_ge))
                vop(lambda e: e.tensor_mul(R[:, :, 29:30], R[:, :, 28:29], R[:, :, 25:26]))
                vop(lambda e: e.tensor_scalar(R[:, :, 28:29], R[:, :, 28:29], -1.0, 1.0,
                                              ALU.mult, ALU.add))
                vop(lambda e: e.tensor_mul(R[:, :, 28:29], R[:, :, 28:29], R[:, :, 27:28]))
                vop(lambda e: e.tensor_add(R[:, :, 29:30], R[:, :, 29:30], R[:, :, 28:29]))
                vop(lambda e: e.tensor_tensor(R[:, :, 30:31], R[:, :, 24:25], R[:, :, 26:27], ALU.min))
                vop(lambda e: e.tensor_max(R[:, :, 31:32], R[:, :, 30:31], R[:, :, 29:30]))
                vop(lambda e: e.tensor_tensor(
                    R[:, :, 32:40], R[:, :, 16:24],
                    R[:, :, 31:32].broadcast_to([128, 4, 8]), ALU.is_ge))
                vop(lambda e: e.tensor_mul(R[:, :, 40:48], R[:, :, 8:16], R[:, :, 32:40]))
                for s4 in range(4):
                    vop(lambda e, s4=s4: e.max(m8b4[:, s4, :], R[:, s4, 40:48]))
                vop(lambda e: e.tensor_copy(rp4[:, :, 0:2], m8b4[:, :, 0:2]),
                    waits=[("V", v_rep2)])
                for s4 in range(4):
                    vop(lambda e, s4=s4: e.match_replace(
                        R[:, s4, 32:40], rp4[:, s4, :], R[:, s4, 40:48], -1.0e30))
                vop(lambda e: e.tensor_single_scalar(R[:, :, 32:40], R[:, :, 32:40],
                                                     -1.0e29, ALU.is_le))
                vop(lambda e: e.tensor_mul(R[:, :, 32:40], R[:, :, 32:40], R[:, :, 8:16]))
                vop(lambda e: e.reduce_sum(R[:, :, 24:25], R[:, :, 32:40], AXX))
                vop(lambda e: e.tensor_scalar_add(R[:, :, 24:25], R[:, :, 24:25], 1.0e-20))
                vop(lambda e: e.reciprocal(R[:, :, 25:26], R[:, :, 24:25]))
                vop(lambda e: e.tensor_mul(
                    R[:, :, 40:48], R[:, :, 32:40],
                    R[:, :, 25:26].broadcast_to([128, 4, 8])))
                v_w = vop(lambda e, j=j: e.tensor_scalar_mul(
                    wtok[:, j * 4:(j + 1) * 4, :], R[:, :, 40:48], 2.5))
                p4_lastT[j] = p.cnt["T"]

            for j in range(9):
                if j < 8:
                    p4_b1(j)
                if j >= 1:
                    p4_b2(j - 1)

            attn_bar = p.all_waits()

        if stage == "p4":
            pass
        else:
            with ExitStack() as ffn:
                fn_ = ffn.enter_context
                stgb = fn_(nc.sbuf_tensor("stgb", [128, 8, 1024], F32))
                h2hh = fn_(nc.sbuf_tensor("h2hh", [128, 8, 1024], BF16))
                h2hl = fn_(nc.sbuf_tensor("h2hl", [128, 8, 1024], BF16))
                h2st = fn_(nc.sbuf_tensor("h2st", [128, 2, 1024], F32))
                wbgh = fn_(nc.sbuf_tensor("wbgh", [128, 2, 8, 512], BF16))
                wbgl = fn_(nc.sbuf_tensor("wbgl", [128, 2, 8, 512], BF16))
                wbuh = fn_(nc.sbuf_tensor("wbuh", [128, 2, 8, 512], BF16))
                wbul = fn_(nc.sbuf_tensor("wbul", [128, 2, 8, 512], BF16))
                wbdh = fn_(nc.sbuf_tensor("wbdh", [128, 2, 4, 2, 512], BF16))
                wbdl = fn_(nc.sbuf_tensor("wbdl", [128, 2, 4, 2, 512], BF16))
                wst = fn_(nc.sbuf_tensor("wst", [128, 3, 512], F32))
                mbuf = fn_(nc.sbuf_tensor("mbuf", [128, 512], F32))
                mh = fn_(nc.sbuf_tensor("mh", [128, 4, 512], BF16))
                ml = fn_(nc.sbuf_tensor("ml", [128, 4, 512], BF16))
                sgb = fn_(nc.sbuf_tensor("sgb", [128, 2, 512], F32))
                wrh = fn_(nc.sbuf_tensor("wrh", [8, 256], F32))
                wrow = fn_(nc.sbuf_tensor("wrow", [1, 2, 512], F32))

                first = {e: True for e in ENGINES}

                def bw_first(eng):
                    if first[eng]:
                        first[eng] = False
                        return list(attn_bar)
                    return []

                out_toks = []
                out_by_slot = {}
                h2_split = {0: [], 1: []}
                wr_state = {"tk_w8": None, "wrow_tok": None}

                def emit_wrows(hq, bank):
                    for g2 in range(4):
                        bk = bank if g2 % 2 == 0 else 6
                        bw = bank_waits(bk)
                        t = None
                        for ss4 in range(2):
                            ss = g2 * 2 + ss4
                            t = tr(pb[bk][0:8, ss4 * 128:(ss4 + 1) * 128],
                                   wtok[:, hq * 8 + ss, :], ident,
                                   waits=bw_first("tensor") + bw, inc="T")
                            bw = []
                        s_wr = act(wrh[0:8, :], pb[bk][0:8, 0:256],
                                   AF.Copy, waits=bw_first("scalar") +
                                   [("T", t), wr_state["tk_w8"]])
                        note_bank(bk, "S", s_wr)
                        wr_state["tk_w8"] = dma(
                            "gpsimd", scr_w8.ap()[:, g2 * 256:(g2 + 1) * 256],
                            wrh[0:8, :], "w8",
                            waits=bw_first("gpsimd") + [("S", s_wr)] +
                                  ([wr_state["wrow_tok"]] if g2 == 0 else []))

                for h in range(4):
                    emit_wrows(h, 3)
                    tk_w8 = wr_state["tk_w8"]
                    stg_tok = []
                    for jj in range(2):
                        wx = bw_first("sync") + ([out_by_slot[jj]] if jj in out_by_slot else [])
                        stg_tok.append(dma(
                            "sync", stgb[:, jj * 4:(jj + 1) * 4, :],
                            scr_xtm.ap()[h * 1024 + jj * 512: h * 1024 + (jj + 1) * 512, :]
                            .rearrange("(s p) f -> p s f", p=128), f"sgl{jj}", waits=wx))
                    t_snap = p.cnt["T"]
                    h2S = h2V = None
                    for c in range(8):
                        slc = c % 2
                        tkc = dma("sync", h2st[:, slc, :],
                                  scr_h2.ap()[c, :, h * 1024:(h + 1) * 1024],
                                  f"hh{slc}", waits=bw_first("sync") + h2_split[slc])
                        h2S = act(h2hh[:, c, :], h2st[:, slc, :], AF.Copy,
                                  waits=[tkc, ("T", t_snap)])
                        h2V = vop(lambda e, c=c, slc=slc: e.tensor_tensor(
                            h2hl[:, c, :], h2st[:, slc, :], h2hh[:, c, :], ALU.subtract),
                            waits=bw_first("vector") + [tkc, ("S", h2S)])
                        h2_split[slc] = [("S", h2S), ("V", h2V)]

                    def esrc(e10):
                        if e10 < 8:
                            return (wg_d.ap()[e10].rearrange("(k p) q -> p k q", p=128),
                                    wu_d.ap()[e10].rearrange("(k p) q -> p k q", p=128),
                                    wn_d.ap()[e10].rearrange("(k p) q -> p k q", p=128))
                        ih = e10 - 8
                        return (sg_d.ap()[:, ih * 512:(ih + 1) * 512].rearrange(
                                    "(k p) q -> p k q", p=128),
                                su_d.ap()[:, ih * 512:(ih + 1) * 512].rearrange(
                                    "(k p) q -> p k q", p=128),
                                sd_d.ap()[ih * 512:(ih + 1) * 512, :].rearrange(
                                    "(k p) q -> p k q", p=128))

                    def preload(e10):
                        slw = e10 % 2
                        wprev = [("T", slot_T[slw])] if slot_T[slw] else []
                        toks = {}
                        if h == 0:
                            po = list(pair_out[slw])
                            g_src, u_src, d_src = esrc(e10)
                            qctr = [0]

                            def mk_stage(nm, q, srcap, dsth, dstl, sh, sl_,
                                         wprev=wprev, po=po, e10=e10, slw=slw,
                                         toks=toks):
                                fold = nm == "d"
                                if not fold:
                                    sap = srcap[:, q, :]
                                    dh = dsth[:, slw, q, :]
                                    dl = dstl[:, slw, q, :]
                                    oh = sh.ap()[e10][:, q, :]
                                    ol = sl_.ap()[e10][:, q, :]
                                else:
                                    kq, fhh = q // 2, q % 2
                                    sap = srcap[:, kq, fhh * 512:(fhh + 1) * 512]
                                    dh = dsth[:, slw, kq, fhh, :]
                                    dl = dstl[:, slw, kq, fhh, :]
                                    oh = sh.ap()[e10][:, kq, fhh, :]
                                    ol = sl_.ap()[e10][:, kq, fhh, :]

                                def emit():
                                    qs = qctr[0] % 3
                                    qctr[0] += 1
                                    tk = dma("sync", wst[:, qs, :], sap, f"ws{qs}",
                                             waits=bw_first("sync") + stage_guard[qs])
                                    vf = None
                                    if fold:
                                        vf = vop(lambda e, qs=qs, fhh=q % 2:
                                                 e.tensor_mul(wst[:, qs, :],
                                                              wst[:, qs, :],
                                                              gb_bc[:, fhh, :]),
                                                 waits=bw_first("vector") + [tk])
                                    sS = act(dh, wst[:, qs, :], AF.Copy,
                                             waits=[tk, ("V", vf)] + wprev + po)
                                    sV = vop(lambda e, dl=dl, qs=qs, dh=dh:
                                             e.tensor_tensor(dl, wst[:, qs, :], dh,
                                                             ALU.subtract),
                                             waits=[tk, ("S", sS)] + po)
                                    oh_t = dma("gpsimd", oh, dh, "pwoh",
                                               waits=bw_first("gpsimd") + [("S", sS)])
                                    ol_t = dma("gpsimd", ol, dl, "pwol",
                                               waits=[("V", sV)])
                                    stage_guard[qs] = [("S", sS), ("V", sV)]
                                    if q == 7:
                                        toks[nm] = [("S", sS), ("V", sV)]
                                        if nm == "d":
                                            pair_out[slw] = [oh_t, ol_t]
                                return emit

                            for nm, srcap, dsth, dstl, sh, sl_ in (
                                    ("g", g_src, wbgh, wbgl, scr_gh, scr_gl),
                                    ("u", u_src, wbuh, wbul, scr_uh, scr_ul),
                                    ("d", d_src, wbdh, wbdl, scr_dh, scr_dl)):
                                for q in range(8):
                                    pend.append(mk_stage(nm, q, srcap, dsth,
                                                         dstl, sh, sl_))
                        else:
                            pw = [("pwoh", p.cnt["pwoh"]), ("pwol", p.cnt["pwol"])]
                            for nm, dh, dl, sh, sl_ in (
                                    ("g", wbgh, wbgl, scr_gh, scr_gl),
                                    ("u", wbuh, wbul, scr_uh, scr_ul),
                                    ("d", wbdh, wbdl, scr_dh, scr_dl)):
                                dst = dh[:, slw] if nm == "d" else dh[:, slw, :, :]
                                dstl = dl[:, slw] if nm == "d" else dl[:, slw, :, :]
                                dma("sync", dst, sh.ap()[e10], f"w{nm}{slw}",
                                    waits=bw_first("sync") + wprev + pw)
                                tl = dma("sync", dstl, sl_.ap()[e10], f"w{nm}{slw}",
                                         waits=wprev)
                                toks[nm] = [tl]
                        return toks

                    if h == 0:
                        slot_T = [None, None]
                        stage_guard = {0: [], 1: [], 2: []}
                        pair_out = {0: [], 1: []}
                        sgb_rd = {0: None, 1: None}
                        mh_S = [None]
                        wrow_T = None

                    def wrow_load(e10):
                        wr_state["wrow_tok"] = dma(
                            "sync", wrow[0:1, :, :],
                            scr_w8.ap()[e10][None, :].rearrange(
                                "a (j t) -> a j t", j=2), "wr",
                            waits=[tk_w8, ("T", wrow_T)])
                        return wr_state["wrow_tok"]

                    wrow_tok = wrow_load(0)
                    pend = []

                    def drain(n):
                        for _ in range(min(n, len(pend))):
                            pend.pop(0)()

                    pre = {0: preload(0)}
                    ev_tok = {}
                    for e10 in range(10):
                        routed = e10 < 8
                        slw = e10 % 2
                        drain(len(pend))
                        gtk, utk, dtk = pre[e10]["g"], pre[e10]["u"], pre[e10]["d"]
                        if e10 + 1 < 10:
                            pre[e10 + 1] = preload(e10 + 1)
                        for jj in range(2):
                            jsl = slice(jj * 512, (jj + 1) * 512)
                            wbc_T = None
                            if routed:
                                bw = bank_waits(6)
                                wbc_T = mm(pb[6][:, :], onesb[0:1, 0:128],
                                           wrow[0:1, jj, :], True, True,
                                           waits=[wrow_tok] + bw, inc="T")
                                if jj == 1:
                                    wrow_T = wbc_T
                                    if e10 + 1 < 8:
                                        wrow_tok = wrow_load(e10 + 1)
                            s_mh = [None] * 4
                            v_ml = [None] * 4
                            u_T = [None] * 4
                            for m in range(4):
                                gbank = m % 2
                                bw = bank_waits(gbank)
                                fw = gtk + [("S", h2S), ("V", h2V)] + bw
                                for pi, (lw, rh) in enumerate(
                                        ((wbgh, h2hh), (wbgh, h2hl), (wbgl, h2hh))):
                                    for k in range(8):
                                        st = pi == 0 and k == 0
                                        sp = pi == 2 and k == 7
                                        t = mm(pb[gbank][:, :],
                                               lw[:, slw, k, m * 128:(m + 1) * 128],
                                               rh[:, k, jsl], st, sp,
                                               waits=fw if st else [],
                                               inc="T" if sp else None)
                                g_T = t
                                ubank = 2 + m % 2
                                bw = bank_waits(ubank)
                                fw = utk + bw
                                for pi, (lw, rh) in enumerate(
                                        ((wbuh, h2hh), (wbuh, h2hl), (wbul, h2hh))):
                                    for k in range(8):
                                        st = pi == 0 and k == 0
                                        sp = pi == 2 and k == 7
                                        t = mm(pb[ubank][:, :],
                                               lw[:, slw, k, m * 128:(m + 1) * 128],
                                               rh[:, k, jsl], st, sp,
                                               waits=fw if st else [],
                                               inc="T" if sp else None)
                                u_T[m] = t
                                s_si = act(sgb[:, m % 2, :], pb[gbank][:, :], AF.Silu,
                                           waits=[("T", g_T), ("V", sgb_rd[m % 2])])
                                note_bank(gbank, "S", s_si)
                                v1 = vop(lambda e, m=m, ubank=ubank: e.tensor_mul(
                                    mbuf[:, :], sgb[:, m % 2, :], pb[ubank][:, :]),
                                    waits=[("S", s_si), ("T", u_T[m]),
                                           ("S", mh_S[0])])
                                sgb_rd[m % 2] = v1
                                note_bank(ubank, "V", v1)
                                if routed:
                                    v1 = vop(lambda e: e.tensor_mul(
                                        mbuf[:, :], mbuf[:, :], pb[6][:, :]),
                                        waits=[("T", wbc_T)])
                                    note_bank(6, "V", v1)
                                s_mh[m] = act(mh[:, m, :], mbuf[:, :], AF.Copy,
                                              waits=[("V", v1)])
                                mh_S[0] = s_mh[m]
                                v_ml[m] = vop(lambda e, m=m: e.tensor_tensor(
                                    ml[:, m, :], mbuf[:, :], mh[:, m, :], ALU.subtract),
                                    waits=[("S", s_mh[m])])
                                drain(2)
                            for ts in range(4):
                                for fhh in range(2):
                                    bank = 4 + (ts * 2 + fhh) % 2
                                    bw = bank_waits(bank)
                                    fw = [("S", s_mh[3]), ("V", v_ml[3])] + dtk + bw
                                    for pi, (lh, rh) in enumerate(
                                            ((mh, wbdh), (mh, wbdl), (ml, wbdh))):
                                        for k in range(4):
                                            st = pi == 0 and k == 0
                                            sp = pi == 2 and k == 3
                                            t = mm(pb[bank][:, :],
                                                   lh[:, k, ts * 128:(ts + 1) * 128],
                                                   rh[:, slw, k, fhh, :], st, sp,
                                                   waits=fw if st else [],
                                                   inc="T" if sp else None)
                                    v_ev = vop(lambda e, jj=jj, ts=ts, fhh=fhh, bank=bank:
                                               e.tensor_add(
                                                   stgb[:, jj * 4 + ts, fhh * 512:(fhh + 1) * 512],
                                                   stgb[:, jj * 4 + ts, fhh * 512:(fhh + 1) * 512],
                                                   pb[bank][:, :]),
                                               waits=[("T", t), stg_tok[jj]])
                                    note_bank(bank, "V", v_ev)
                                    ev_tok[(jj, ts, fhh)] = v_ev
                                    drain(1)
                        slot_T[slw] = p.cnt["T"]
                    for jj in range(2):
                        wv = [("V", ev_tok[(jj, ts, fhh)]) for ts in range(4) for fhh in range(2)]
                        out_toks.append(dma(
                            "gpsimd",
                            out_d.ap()[h * 1024 + jj * 512: h * 1024 + (jj + 1) * 512, :]
                            .rearrange("(s p) f -> p s f", p=128),
                            stgb[:, jj * 4:(jj + 1) * 4, :], f"ob{jj}",
                            waits=bw_first("gpsimd") + wv))
                        out_by_slot[jj] = out_toks[-1]

        finals = [(s, v) for s, v in p.all_waits() if s not in ("T", "V", "S") and v > 0]

        def _final_wait(e, fs=tuple(finals)):
            for s, v in fs:
                e.wait_ge(p.hand[s], v)
        p.th["gpsimd"].append(_final_wait)

        with nc.Block() as block:

            @block.tensor
            def _(eng):
                for th in p.th["tensor"]:
                    th(eng)

            @block.vector
            def _(eng):
                for th in p.th["vector"]:
                    th(eng)

            @block.scalar
            def _(eng):
                for th in p.th["scalar"]:
                    th(eng)

            @block.sync
            def _(eng):
                for th in p.th["sync"]:
                    th(eng)

            @block.gpsimd
            def _(eng):
                for th in p.th["gpsimd"]:
                    th(eng)

    return nc


def _host_consts():
    cst = np.zeros((128, 257), dtype=np.float32)
    cst[:, 0:128] = np.eye(128, dtype=np.float32)
    cst[:, 128:256] = 1.0
    cst[:, 256] = 1.0 / 1024.0
    return cst


def kernel(**inputs):
    g = lambda k: np.ascontiguousarray(np.asarray(inputs[k], dtype=np.float32))
    xall = g("L0_attn_shot_embeddings")
    cst = _host_consts()
    shared = dict(
        wmod=g("L0_wmod"), wd=g("L0_attn_Wd"), bd=g("L0_attn_bd"),
        wi=g("L0_attn_Wi"), bi=g("L0_attn_bi"), alog=g("L0_attn_A_log"),
        wo=g("L0_attn_Wo"), bo=g("L0_attn_bo"), wsm=g("L0_attn_Ws"),
        bsv=g("L0_attn_bs"), gw=g("L0_ffn_gate_w"),
        gbias=g("L0_ffn_gate_bias").reshape(1, 8),
        wg=g("L0_ffn_w_gate"), wu=g("L0_ffn_w_up"), wdn=g("L0_ffn_w_down"),
        swg=g("L0_ffn_sw_gate"), swu=g("L0_ffn_sw_up"), swd=g("L0_ffn_sw_down"),
        cst_hbm=cst,
    )
    call = g("L0_c")
    nc = build_nc()
    in_maps = [dict(x_in=np.ascontiguousarray(xall[b]),
                    cvec=np.ascontiguousarray(call[b]), **shared)
               for b in range(8)]
    res = run_bass_kernel_spmd(nc, in_maps, core_ids=list(range(8)))
    return np.stack([res.results[b]["out"] for b in range(8)], axis=0)


if __name__ == "__main__":
    import reference as R
    ins = {k: np.asarray(v) for k, v in R.setup_inputs().items()}
    out = kernel(**ins)
    ref = np.asarray(R.reference(**{k: v for k, v in ins.items()}))
    err = np.linalg.norm(out - ref) / np.linalg.norm(ref)
    print("rel err:", err)


# version 26
# speedup vs baseline: 1.0585x; 1.0585x over previous
import numpy as np
from contextlib import ExitStack

import concourse.bass as bass
import concourse.mybir as mybir
from concourse.bass_utils import run_bass_kernel_spmd

F32 = mybir.dt.float32
BF16 = mybir.dt.bfloat16
AF = mybir.ActivationFunctionType
ALU = mybir.AluOpType
AXX = mybir.AxisListType.X

SEQ, DM, SD = 4096, 1024, 256
NT, TT = 8, 512
EPS = 1e-5
ENGINES = ("tensor", "vector", "scalar", "sync", "gpsimd")
STAGE = "full"


class P:

    def __init__(self, nc):
        self.nc = nc
        self.th = {e: [] for e in ENGINES}
        self.cnt = {}
        self.hand = {}
        self.waited = {e: {} for e in ENGINES}
        self.own = {"T": "tensor"}

    def mksem(self, stack, *names):
        self.stack = stack
        for n in names:
            self.hand[n] = stack.enter_context(self.nc.semaphore(n))
            self.cnt[n] = 0

    def chan(self, n):
        if n not in self.hand:
            self.hand[n] = self.stack.enter_context(self.nc.semaphore(n))
            self.cnt[n] = 0

    def emit(self, eng, fn, waits=(), inc=None, n=1):
        if eng == "scalar":
            waits = list(waits) + [("S", self.cnt["S"])]
        elif eng == "vector":
            waits = list(waits) + [("V", self.cnt["V"])]
        clean = []
        for wv in waits:
            if wv is None:
                continue
            s, v = wv
            if v is None or v <= 0:
                continue
            if self.own.get(s) == eng:
                continue
            assert v <= self.cnt[s], (eng, s, v, self.cnt[s])
            clean.append((s, v))
        tok = None
        if inc is not None:
            self.cnt[inc] += n
            tok = self.cnt[inc]
        wd = self.waited[eng]
        real = []
        for s, v in clean:
            if wd.get(s, 0) < v:
                real.append((s, v))
                wd[s] = v
        hand = self.hand

        def thunk(e, fn=fn, real=tuple(real), inc=inc, n=n):
            for s, v in real:
                e.wait_ge(hand[s], v)
            inst = fn(e)
            if inc is not None:
                inst.then_inc(hand[inc], n)

        self.th[eng].append(thunk)
        return tok

    def all_waits(self):
        return [(s, self.cnt[s]) for s in self.cnt]


def build_nc(stage=STAGE):
    nc = bass.Bass("TRN2", target_bir_lowering=False)

    x_in = nc.dram_tensor("x_in", [SEQ, DM], F32, kind="ExternalInput")
    cvec = nc.dram_tensor("cvec", [DM], F32, kind="ExternalInput")
    wmod_d = nc.dram_tensor("wmod", [DM, 6 * DM], F32, kind="ExternalInput")
    wd_d = nc.dram_tensor("wd", [2, DM, SD], F32, kind="ExternalInput")
    bd_d = nc.dram_tensor("bd", [2, SD], F32, kind="ExternalInput")
    wi_d = nc.dram_tensor("wi", [2, DM, SD], F32, kind="ExternalInput")
    bi_d = nc.dram_tensor("bi", [2, SD], F32, kind="ExternalInput")
    al_d = nc.dram_tensor("alog", [2, SD], F32, kind="ExternalInput")
    wo_d = nc.dram_tensor("wo", [2, SD, DM], F32, kind="ExternalInput")
    bo_d = nc.dram_tensor("bo", [2, DM], F32, kind="ExternalInput")
    ws_d = nc.dram_tensor("wsm", [2, SD, DM], F32, kind="ExternalInput")
    bs_d = nc.dram_tensor("bsv", [2, DM], F32, kind="ExternalInput")
    gw_d = nc.dram_tensor("gw", [DM, 8], F32, kind="ExternalInput")
    gbias_d = nc.dram_tensor("gbias", [1, 8], F32, kind="ExternalInput")
    wg_d = nc.dram_tensor("wg", [8, DM, 512], F32, kind="ExternalInput")
    wu_d = nc.dram_tensor("wu", [8, DM, 512], F32, kind="ExternalInput")
    wn_d = nc.dram_tensor("wdn", [8, 512, DM], F32, kind="ExternalInput")
    sg_d = nc.dram_tensor("swg", [DM, DM], F32, kind="ExternalInput")
    su_d = nc.dram_tensor("swu", [DM, DM], F32, kind="ExternalInput")
    sd_d = nc.dram_tensor("swd", [DM, DM], F32, kind="ExternalInput")
    cst_d = nc.dram_tensor("cst_hbm", [128, 257], F32, kind="ExternalInput")

    scr_row = nc.dram_tensor("scr_row", [6 * DM], F32)
    scr_w8 = nc.dram_tensor("scr_w8", [8, 1024], F32)
    scr_e = nc.dram_tensor("scr_e", [2, 128, SEQ], F32)
    scr_f = nc.dram_tensor("scr_f", [2, 128, SEQ], F32)
    scr_xfm = nc.dram_tensor("scr_xfm", [8, 128, SEQ], F32)
    scr_h2 = nc.dram_tensor("scr_h2", [8, 128, SEQ], F32)
    scr_xtm = nc.dram_tensor("scr_xtm", [SEQ, DM], F32)
    scr_gh = nc.dram_tensor("scr_gh", [10, 128, 8, 512], BF16)
    scr_gl = nc.dram_tensor("scr_gl", [10, 128, 8, 512], BF16)
    scr_uh = nc.dram_tensor("scr_uh", [10, 128, 8, 512], BF16)
    scr_ul = nc.dram_tensor("scr_ul", [10, 128, 8, 512], BF16)
    scr_dh = nc.dram_tensor("scr_dh", [10, 128, 4, 2, 512], BF16)
    scr_dl = nc.dram_tensor("scr_dl", [10, 128, 4, 2, 512], BF16)
    out_d = nc.dram_tensor("out", [SEQ, DM], F32, kind="ExternalOutput")

    p = P(nc)

    with ExitStack() as top:
        en = top.enter_context
        p.mksem(top, "T", "V", "S")

        cst = en(nc.sbuf_tensor("cst", [128, 257], F32))
        ident = cst[:, 0:128]
        onesb = cst[:, 128:256]
        invd = cst[:, 256:257]
        rowtmp = en(nc.sbuf_tensor("rowtmp", [1, 2, 512], F32))
        mod_cols = en(nc.sbuf_tensor("mod_cols", [128, 48], F32))
        oneca = en(nc.sbuf_tensor("oneca", [128, 8], F32))
        onecb = en(nc.sbuf_tensor("onecb", [128, 8], F32))
        gacl = en(nc.sbuf_tensor("gacl", [128, 8], F32))
        gbcl = en(nc.sbuf_tensor("gbcl", [128, 8], F32))
        gb_bc = en(nc.sbuf_tensor("gb_bc", [128, 2, 512], F32))
        negsA = en(nc.sbuf_tensor("negsA", [128, 4], F32))
        bo_cols = en(nc.sbuf_tensor("bo_cols", [128, 16], F32))
        bd_cols = en(nc.sbuf_tensor("bd_cols", [128, 4], F32))
        bi_cols = en(nc.sbuf_tensor("bi_cols", [128, 4], F32))
        bias_d = en(nc.sbuf_tensor("bias_d", [128, 4], F32))
        bias_i = en(nc.sbuf_tensor("bias_i", [128, 4], F32))
        nsd_row = en(nc.sbuf_tensor("nsd_row", [1, SD], F32))
        nsi_row = en(nc.sbuf_tensor("nsi_row", [1, SD], F32))
        cs_cols = en(nc.sbuf_tensor("cs_cols", [128, 8], F32))
        gc1 = en(nc.sbuf_tensor("gc1", [128, 8], F32))
        gw_sb = en(nc.sbuf_tensor("gw_sb", [128, 8, 8], F32))
        gbias_row = en(nc.sbuf_tensor("gbias_row", [1, 8], F32))
        gbias_bc = en(nc.sbuf_tensor("gbias_bc", [128, 4, 8], F32))
        wtok = en(nc.sbuf_tensor("wtok", [128, 32, 8], F32))
        rep2 = en(nc.sbuf_tensor("rep2", [128, 8], F32))
        ccols = en(nc.sbuf_tensor("ccols", [128, 8], F32))
        scc = en(nc.sbuf_tensor("scc", [128, 8], F32))
        trtmp = en(nc.sbuf_tensor("trtmp", [48, 128], F32))
        rt4 = en(nc.sbuf_tensor("rt4", [128, 4, 48], F32))
        m8b4 = en(nc.sbuf_tensor("m8b4", [128, 4, 8], F32))
        rp4 = en(nc.sbuf_tensor("rp4", [128, 4, 8], F32))

        pb = [en(nc.psum_tensor(f"pb{i}", [128, 512], F32)) for i in range(8)]
        bank_cons = [[] for _ in range(8)]

        def bank_waits(i):
            w = bank_cons[i]
            bank_cons[i] = []
            return w

        def note_bank(i, sem, tok):
            bank_cons[i].append((sem, tok))

        def mm(out_ap, lhsT, rhs, start, stop, waits=(), inc=None):
            return p.emit("tensor",
                          lambda e, o=out_ap, l=lhsT, r=rhs, a=start, b=stop:
                          e.matmul(o, l, r, start=a, stop=b),
                          waits=waits, inc=inc)

        def tr(out_ap, in_ap, id_ap, waits=(), inc=None):
            return p.emit("tensor",
                          lambda e, o=out_ap, i=in_ap, d=id_ap: e.transpose(o, i, d),
                          waits=waits, inc=inc)

        def act(out_ap, in_ap, func, bias=0.0, scale=1.0, waits=(), inc="S"):
            return p.emit("scalar",
                          lambda e, o=out_ap, i=in_ap, f=func, b=bias, s=scale:
                          e.activation(o, i, f, bias=b, scale=s),
                          waits=waits, inc=inc)

        def vop(fn, waits=(), inc="V"):
            return p.emit("vector", fn, waits=waits, inc=inc)

        def dma(eng, out_ap, in_ap, ch, waits=(), slow=False):
            p.chan(ch)
            swait = [(ch, p.cnt[ch])]
            tok = p.emit(eng,
                         lambda e, o=out_ap, i=in_ap, sl=slow:
                         e.dma_start(out=o, in_=i, allow_slow_non_contiguous=sl),
                         waits=list(waits) + swait, inc=ch, n=16)
            return (ch, tok)

        tk_cst = dma("sync", cst[:, :], cst_d.ap(), "c_cst")
        tk_c = dma("sync", ccols[:, :], cvec.ap().rearrange("(k p) -> p k", p=128), "c_c", slow=True)
        tk_gw = dma("sync", gw_sb[:, :, :], gw_d.ap().rearrange("(k p) e -> p k e", p=128), "c_gw", slow=True)
        tk_gbr = dma("sync", gbias_row[0:1, :], gbias_d.ap(), "c_gbr")
        tk_al = dma("sync", negsA[:, :], al_d.ap().rearrange("l (k p) -> p (l k)", p=128), "c_al", slow=True)
        tk_bo = dma("sync", bo_cols[:, :], bo_d.ap().rearrange("l (k p) -> p (l k)", p=128), "c_bo", slow=True)
        tk_bd = dma("sync", bd_cols[:, :], bd_d.ap().rearrange("l (m p) -> p (l m)", p=128), "c_bd", slow=True)
        tk_bi = dma("sync", bi_cols[:, :], bi_d.ap().rearrange("l (m p) -> p (l m)", p=128), "c_bi", slow=True)

        for s4 in range(4):
            t = mm(pb[0][:, s4 * 8:(s4 + 1) * 8], onesb[0:1, 0:128],
                   gbias_row[0:1, :], s4 == 0, s4 == 3,
                   waits=[tk_gbr, tk_cst], inc="T" if s4 == 3 else None)
        s_gbias = act(gbias_bc[:, :, :], pb[0][:, 0:32].rearrange(
            "p (s e) -> p s e", e=8), AF.Copy, waits=[("T", t)])
        note_bank(0, "S", s_gbias)

        s_sc = act(scc[:, :], ccols[:, :], AF.Silu, waits=[tk_c])

        with nc.sbuf_tensor("wmodb", [128, 2, 8, 512], F32) as wmodb:
            mm_toks = [None] * 12
            row_st = [None] * 12
            for j in range(12):
                sl = j % 2
                w = [("T", mm_toks[j - 2])] if j >= 2 else []
                wjt = dma("sync", wmodb[:, sl, :, :],
                          wmod_d.ap()[:, j * 512:(j + 1) * 512].rearrange(
                              "(k p) n -> p k n", p=128), f"wm{sl}", waits=w)
                bw = bank_waits(1)
                for k in range(8):
                    t = mm(pb[1][0:1, :], scc[:, k:k + 1], wmodb[:, sl, k, :],
                           k == 0, k == 7,
                           waits=[wjt, ("S", s_sc)] + (bw if k == 0 else []),
                           inc="T" if k == 7 else None)
                mm_toks[j] = t
                s_t = act(rowtmp[0:1, sl, :], pb[1][0:1, :], AF.Copy,
                          waits=[("T", t), row_st[j - 2]] if j >= 2
                          else [("T", t)])
                note_bank(1, "S", s_t)
                row_st[j] = dma("gpsimd", scr_row.ap()[j * 512:(j + 1) * 512][None, :],
                                rowtmp[0:1, sl, :], "gx", waits=[("S", s_t)])
            tk_modrow = row_st[11]

        tk_trt = dma("gpsimd", trtmp[0:48, :],
                     scr_row.ap().rearrange("(a b) -> a b", b=128), "gx",
                     waits=[tk_modrow])
        t = tr(pb[2][:, 0:48], trtmp[0:48, :], ident[0:48, 0:48],
               waits=[tk_trt] + bank_waits(2), inc="T")
        s_modc = act(mod_cols[:, :], pb[2][:, 0:48], AF.Copy, waits=[("T", t)])
        note_bank(2, "S", s_modc)

        v_oneca = vop(lambda e: e.tensor_scalar_add(oneca[:, :], mod_cols[:, 8:16], 1.0),
                      waits=[("S", s_modc)])
        v_onecb = vop(lambda e: e.tensor_scalar_add(onecb[:, :], mod_cols[:, 32:40], 1.0))

        def clamp(dst, lo_col, extra):
            sl = slice(lo_col, lo_col + 8)
            s_a = act(gc1[:, :], mod_cols[:, sl], AF.Abs,
                      waits=[("S", s_modc)] + extra)
            s_s = act(rep2[:, :], mod_cols[:, sl], AF.Sign)
            vop(lambda e: e.tensor_scalar_max(gc1[:, :], gc1[:, :], 0.25),
                waits=[("S", s_a), ("S", s_s)])
            return vop(lambda e, dst=dst: e.tensor_mul(dst, rep2[:, :], gc1[:, :]))

        v_ga = clamp(gacl[:, :], 16, [])
        v_gb = clamp(gbcl[:, :], 40, [("V", v_ga)])


        a1 = act(gc1[:, 0:4], negsA[:, :], AF.Abs, waits=[tk_al, ("V", v_gb)])
        act(gc1[:, 0:4], gc1[:, 0:4], AF.Exp, scale=-1.0)
        a3 = act(gc1[:, 0:4], gc1[:, 0:4], AF.Ln, bias=1.0)
        a4 = act(rep2[:, 0:4], negsA[:, :], AF.Relu)
        v5 = vop(lambda e: e.tensor_add(rep2[:, 0:4], rep2[:, 0:4], gc1[:, 0:4]),
                 waits=[("S", a4)])
        v_nsa = vop(lambda e: e.tensor_scalar_mul(negsA[:, :], rep2[:, 0:4], -1.0))
        v_rep2 = vop(lambda e: e.memset(rp4[:, :, :], 1.0e30))

        with ExitStack() as attn:
            an = attn.enter_context
            A = an(nc.sbuf_tensor("Ab", [128, 2, 512], F32))
            B = an(nc.sbuf_tensor("Bb", [128, 2, 512], F32))
            C = an(nc.sbuf_tensor("Cb", [128, 2, 512], F32))
            D = an(nc.sbuf_tensor("Db", [128, 2, 4096], F32))
            EqB = an(nc.sbuf_tensor("EqB", [128, 2, 2, 512], F32))
            FqB = an(nc.sbuf_tensor("FqB", [128, 2, 2, 512], F32))
            h2r = an(nc.sbuf_tensor("h2r", [128, 2, 512], F32))
            xt = an(nc.sbuf_tensor("xt", [128, 2, 8, 512], F32))
            p1 = [xt[:, s, 0:2, :].rearrange("p a t -> p (a t)") for s in range(2)]
            p1s = [xt[:, s, 2:4, :].rearrange("p a t -> p (a t)") for s in range(2)]
            tr_last = [None, None]
            ev_last = [None] * 32
            ld32 = [None] * 32
            st32 = [None] * 32
            for jt in range(32):
                sl = jt % 2
                w = [("T", tr_last[sl])] if jt >= 2 else []
                ld32[jt] = dma("sync", p1[sl], x_in.ap()[jt * 128:(jt + 1) * 128, :],
                               f"x{sl}", waits=w)
                s_tok = v_tok = None
                pv = st32[jt - 2] if jt >= 2 else None
                for c in range(8):
                    bank = 4 + sl
                    t = tr(pb[bank][:, 0:128], p1[sl][:, c * 128:(c + 1) * 128], ident,
                           waits=[ld32[jt], tk_cst] + bank_waits(bank),
                           inc="T")
                    if c == 7:
                        tr_last[sl] = t
                    if c % 2 == 0:
                        s_tok = act(p1s[sl][:, c * 128:(c + 1) * 128], pb[bank][:, 0:128],
                                    AF.Copy, waits=[("T", t), pv])
                        note_bank(bank, "S", s_tok)
                    else:
                        v_tok = vop(lambda e, sl=sl, c=c, bank=bank:
                                    e.tensor_copy(p1s[sl][:, c * 128:(c + 1) * 128],
                                                  pb[bank][:, 0:128]),
                                    waits=[("T", t), pv])
                        note_bank(bank, "V", v_tok)
                ev_last[jt] = (s_tok, v_tok)
                st32[jt] = dma("gpsimd",
                    scr_xfm.ap()[:, :, jt * 128:(jt + 1) * 128].rearrange("c p t -> p c t"),
                    p1s[sl].rearrange("p (c t) -> p c t", t=128), f"st{sl}",
                    waits=[("S", s_tok), ("V", v_tok)])
            tk_p1 = [st32[30], st32[31]]

            h1 = an(nc.sbuf_tensor("h1", [128, 8, 512], F32))
            sq = an(nc.sbuf_tensor("sq", [128, 2, 512], F32))
            rb2 = an(nc.sbuf_tensor("rb2", [128, 2, 512], F32))
            wdsb = an(nc.sbuf_tensor("wdsb", [128, 8, 2, 128], F32))
            wisb = an(nc.sbuf_tensor("wisb", [128, 8, 2, 128], F32))
            wdsb2 = an(nc.sbuf_tensor("wdsb2", [128, 8, 2, 128], F32))
            wisb2 = an(nc.sbuf_tensor("wisb2", [128, 8, 2, 128], F32))
            wosb = an(nc.sbuf_tensor("wosb", [128, 2, 8, 128], F32))
            wssb = an(nc.sbuf_tensor("wssb", [128, 2, 1024], F32))
            rows = an(nc.sbuf_tensor("rows", [1, 2, 2, 512], F32))
            bs_row = an(nc.sbuf_tensor("bs_row", [1, 2 * DM], F32))
            csrow = an(nc.sbuf_tensor("csrow", [1, DM], F32))
            g8row = an(nc.sbuf_tensor("g8row", [8, 128], F32))

            tk_bs = dma("sync", bs_row[0:1, :],
                        bs_d.ap().rearrange("l f -> (l f)")[None, :], "wt")
            t = tr(pb[2][0:8, 0:128], gbcl[:, :], ident[:, :],
                   waits=[("V", v_gb)] + bank_waits(2), inc="T")
            s_g8 = act(g8row[0:8, :], pb[2][0:8, 0:128], AF.Copy, waits=[("T", t)])
            note_bank(2, "S", s_g8)
            tk_g1 = dma("gpsimd", scr_row.ap()[0:DM].rearrange("(a b) -> a b", b=128),
                        g8row[0:8, :], "gx", waits=[("S", s_g8)])
            tk_gbrow = dma("gpsimd", csrow[0:1, :], scr_row.ap()[0:DM][None, :],
                           "gx", waits=[tk_g1])
            for fh in range(2):
                t = mm(pb[3][:, :], onesb[0:1, 0:128], csrow[0:1, fh * 512:(fh + 1) * 512],
                       True, True, waits=[tk_gbrow] + bank_waits(3), inc="T")
                s_gbbc = act(gb_bc[:, fh, :], pb[3][:, :], AF.Copy, waits=[("T", t)])
                note_bank(3, "S", s_gbbc)

            pg = list(tk_p1)
            tkwd = dma("sync", wdsb[:, :, :, :],
                       wd_d.ap()[0].rearrange("(k p) (m n) -> p k m n", p=128, n=128),
                       "wt", waits=pg)
            tkwi = dma("sync", wisb[:, :, :, :],
                       wi_d.ap()[0].rearrange("(k p) (m n) -> p k m n", p=128, n=128),
                       "wt", waits=pg)
            tkwd2 = dma("sync", wdsb2[:, :, :, :],
                        wd_d.ap()[1].rearrange("(k p) (m n) -> p k m n", p=128, n=128),
                        "wt", waits=pg)
            tkwi2 = dma("sync", wisb2[:, :, :, :],
                        wi_d.ap()[1].rearrange("(k p) (m n) -> p k m n", p=128, n=128),
                        "wt", waits=pg)
            tkwo = dma("sync", wosb[:, :, :, :],
                       wo_d.ap()[0].rearrange("(k p) (m n) -> p k m n", p=128, n=128),
                       "wt", waits=pg)
            tkws = dma("sync", wssb[:, :, :],
                       ws_d.ap()[0].rearrange("(c p) f -> p c f", p=128),
                       "wt", waits=pg)

            def bias_cols(dst2, w_sb, rhs_cols, add_cols, extra_waits):
                last_t = None
                for m in range(2):
                    bank = 5 + m
                    bw = bank_waits(bank)
                    for k in range(8):
                        last_t = mm(pb[bank][:, 0:1], w_sb[:, k, m, :], rhs_cols[:, k:k + 1],
                                    k == 0, k == 7,
                                    waits=extra_waits + (bw if k == 0 else []),
                                    inc="T" if k == 7 else None)
                    s_t = act(dst2[:, m:m + 1], pb[bank][:, 0:1], AF.Copy,
                              waits=[("T", last_t)])
                    note_bank(bank, "S", s_t)
                v_t = vop(lambda e, dst2=dst2, add_cols=add_cols:
                          e.tensor_add(dst2[:, 0:2], dst2[:, 0:2], add_cols),
                          waits=[("S", s_t)])
                return v_t, last_t

            v_b1d, t_bd = bias_cols(bias_d[:, 0:2], wdsb, mod_cols[:, 0:8], bd_cols[:, 0:2],
                                    [tkwd, ("S", s_modc), tk_bd])
            v_b1i, t_bi = bias_cols(bias_i[:, 0:2], wisb, mod_cols[:, 0:8], bi_cols[:, 0:2],
                                    [tkwi, ("S", s_modc), tk_bi])

            v_sc = None
            for k in range(8):
                v_sc = vop(lambda e, k=k: e.tensor_scalar_mul(
                    wdsb[:, k, :, :], wdsb[:, k, :, :], oneca[:, k:k + 1]),
                    waits=[("V", v_oneca), ("T", t_bd), ("T", t_bi)])
                v_sc = vop(lambda e, k=k: e.tensor_scalar_mul(
                    wisb[:, k, :, :], wisb[:, k, :, :], oneca[:, k:k + 1]))

            def colsums(dst_row, w_sb, prev_v):
                bw = bank_waits(7)
                last_t = None
                for k in range(8):
                    last_t = mm(pb[7][0:1, 0:256], onesb[:, 0:1],
                                w_sb[:, k, :, :].rearrange("p m n -> p (m n)"),
                                k == 0, k == 7,
                                waits=[("V", prev_v)] + (bw if k == 0 else []),
                                inc="T" if k == 7 else None)
                v_t = vop(lambda e, dst_row=dst_row:
                          e.tensor_scalar_mul(dst_row, pb[7][0:1, 0:256], -1.0),
                          waits=[("T", last_t)])
                note_bank(7, "V", v_t)
                return v_t

            v_nsd = colsums(nsd_row[0:1, :], wdsb, v_sc)
            v_nsi = colsums(nsi_row[0:1, :], wisb, v_sc)

            def stats_tile(src_tile, slot, negate_mu, src_waits, sq4=None):
                bw7 = bank_waits(7)
                for k in range(8):
                    t_mu = mm(pb[7][0:1, :], invd, src_tile[:, k, :], k == 0, k == 7,
                              waits=src_waits + (bw7 if k == 0 else []),
                              inc="T" if k == 7 else None)
                bw4 = bank_waits(4)
                slots = [sq[:, 0, :], sq[:, 1, :]]
                if sq4 is not None:
                    slots += [sq4[:, 0, :]]
                G = len(slots)
                sq_t = [None] * G
                for k in range(8):
                    g = k % G
                    row = g * 32 if G == 3 else 0
                    st_f = k < G if G == 3 else k == 0
                    sp_f = k >= 8 - G if G == 3 else k == 7
                    s_t = act(slots[g], src_tile[:, k, :], AF.Square,
                              waits=src_waits + ([("T", sq_t[g])] if sq_t[g] else []))
                    t_sq = mm(pb[4][row:row + 1, :], invd, slots[g],
                              st_f, sp_f,
                              waits=[("S", s_t)] + (bw4 if st_f else []),
                              inc="T")
                    sq_t[g] = t_sq
                mu_sl = rows[0:1, slot, 0, :]
                rstd_sl = rows[0:1, slot, 1, :]
                scr_sl = rowtmp[0:1, slot, :]
                v_mu = vop(lambda e, mu_sl=mu_sl, f=(-1.0 if negate_mu else 1.0):
                           e.tensor_scalar_mul(mu_sl, pb[7][0:1, :], f),
                           waits=[("T", t_mu)])
                note_bank(7, "V", v_mu)
                vop(lambda e, scr_sl=scr_sl, mu_sl=mu_sl: e.tensor_mul(scr_sl, mu_sl, mu_sl),
                    waits=src_waits)
                if G == 3:
                    vop(lambda e, rstd_sl=rstd_sl: e.tensor_copy(
                        rstd_sl, pb[4][0:1, :]), waits=[("T", t_sq)])
                    vop(lambda e, rstd_sl=rstd_sl: e.tensor_add(
                        rstd_sl, rstd_sl, pb[4][32:33, :]))
                    v_s4 = vop(lambda e, rstd_sl=rstd_sl: e.tensor_add(
                        rstd_sl, rstd_sl, pb[4][64:65, :]))
                    note_bank(4, "V", v_s4)
                    v_ve = vop(lambda e, scr_sl=scr_sl, rstd_sl=rstd_sl:
                               e.scalar_tensor_tensor(
                                   scr_sl, rstd_sl, EPS, scr_sl, ALU.add, ALU.subtract))
                else:
                    v_ve = vop(lambda e, scr_sl=scr_sl: e.scalar_tensor_tensor(
                        scr_sl, pb[4][0:1, :], EPS, scr_sl, ALU.add, ALU.subtract),
                        waits=[("T", t_sq)])
                    note_bank(4, "V", v_ve)
                act(scr_sl, scr_sl, AF.Ln, waits=[("V", v_ve)])
                s_rstd = act(rstd_sl, scr_sl, AF.Exp, scale=-0.5)
                return v_mu, s_rstd

            xt_tok = [None] * 8
            xt_lastT = [None] * 8
            pre1_spill = {}
            scan0_v = None

            scan_toks = [None] * 8
            zlv = [None] * 8

            xta_tok = {}

            def l0_front(j):
                sl = j % 2
                jsl = slice(j * 512, (j + 1) * 512)
                w = list(tk_p1) if j < 2 else [("T", xt_lastT[j - 2])] + list(tk_p1)
                xta = dma("sync", xt[:, sl, 0:4, :],
                          scr_xfm.ap()[0:4, :, jsl].rearrange("c p t -> p c t"),
                          f"xta{sl}", waits=w)
                xt_tok[j] = dma("sync", xt[:, sl, 4:8, :],
                                scr_xfm.ap()[4:8, :, jsl].rearrange("c p t -> p c t"),
                                f"xt{sl}", waits=w)
                xta_tok[j] = xta
                v_mu, s_rstd = stats_tile(xt[:, sl, :, :], sl, False,
                                          [xta, xt_tok[j], tk_modrow, tk_trt])
                t = mm(pb[4][:, :], onesb[0:1, 0:128], rows[0:1, sl, 1, :], True, True,
                       waits=[("S", s_rstd)] + bank_waits(4), inc="T")
                s_rb = act(rb2[:, sl, :], pb[4][:, :], AF.Copy,
                           waits=[("T", t), ("V", p.cnt["V"])])
                note_bank(4, "S", s_rb)
                z_last_v = None
                for (w_sb, ns_row, dstbuf, base, v_ns) in (
                        (wdsb, nsd_row, A, 0, v_nsd), (wisb, nsi_row, B, 2, v_nsi)):
                    for m in range(2):
                        bank = base + m
                        bw = bank_waits(bank)
                        for k in range(8):
                            mm(pb[bank][:, :], w_sb[:, k, m, :], xt[:, sl, k, :],
                               k == 0, False,
                               waits=([("V", v_ns), xt_tok[j], xta_tok[j]] +
                                      (bw if k == 0 else [])))
                        t = mm(pb[bank][:, :], ns_row[0:1, m * 128:(m + 1) * 128],
                               rows[0:1, sl, 0, :], False, True,
                               waits=[("V", v_mu)], inc="T")
                        v_z = vop(lambda e, bank=bank, dstbuf=dstbuf, m=m, sl=sl:
                                  e.tensor_mul(dstbuf[:, m, :],
                                               pb[bank][:, :], rb2[:, sl, :]),
                                  waits=[("T", t), ("S", s_rb)])
                        note_bank(bank, "V", v_z)
                        z_last_v = v_z
                xt_lastT[j] = p.cnt["T"]
                zlv[j] = z_last_v

                for c in range(2):
                    Ah, Bh, Ch = A[:, c, :], B[:, c, :], C[:, c, :]
                    bwt = [("V", z_last_v), ("V", v_b1d), ("V", v_b1i), ("V", v_nsa)]
                    act(Ch, Ah, AF.Abs, bias=bias_d[:, c:c + 1], waits=bwt)
                    act(Ch, Ch, AF.Exp, scale=-1.0)
                    s_ln = act(Ch, Ch, AF.Ln, bias=1.0)
                    s_rl = act(Ah, Ah, AF.Relu, bias=bias_d[:, c:c + 1])
                    v_dl = vop(lambda e, Ah=Ah, Ch=Ch: e.tensor_add(Ah, Ah, Ch),
                               waits=[("S", s_ln), ("S", s_rl)])
                    s_dc = act(Ch, Ah, AF.Exp, scale=negsA[:, c:c + 1],
                               waits=[("V", v_dl)])
                    vop(lambda e, Ah=Ah, Bh=Bh, c=c: e.scalar_tensor_tensor(
                        Bh, Bh, bias_i[:, c:c + 1], Ah, ALU.add, ALU.mult))
                    init = 0.0 if j == 0 else D[:, c, j * 512 - 1:j * 512]
                    scan_toks[j] = vop(lambda e, Ch=Ch, Bh=Bh, c=c, j=j, init=init:
                                       e.tensor_tensor_scan(
                                           D[:, c, j * 512:(j + 1) * 512],
                                           Ch, Bh, init, ALU.mult, ALU.add),
                                       waits=[("S", s_dc)])

            def l0_back(j):
                jsl = slice(j * 512, (j + 1) * 512)
                h1_s = [None] * 8
                for m in range(8):
                    bank = 5 + (m % 2)
                    bw = bank_waits(bank)
                    for k in range(2):
                        t = mm(pb[bank][:, :], wosb[:, k, m, :], D[:, k, jsl],
                               k == 0, k == 1,
                               waits=[("V", scan_toks[j]), tkwo] + (bw if k == 0 else []),
                               inc="T" if k == 1 else None)
                    h1_s[m] = act(h1[:, m, :], pb[bank][:, :], AF.Identity,
                                  bias=bo_cols[:, m:m + 1],
                                  waits=[("T", t), tk_bo])
                    note_bank(bank, "S", h1_s[m])
                for (w_sb2, stgbuf, scr, base, tk_w) in (
                        (wdsb2, EqB, scr_e, 0, tkwd2), (wisb2, FqB, scr_f, 2, tkwi2)):
                    sl = j % 2
                    pre_s = [None, None]
                    for m in range(2):
                        bank = base + m
                        bw = bank_waits(bank)
                        for k in range(8):
                            t = mm(pb[bank][:, :], w_sb2[:, k, m, :], h1[:, k, :],
                                   k == 0, k == 7,
                                   waits=[tk_w, ("S", h1_s[k]),
                                          pre1_spill.get((j - 2, base))] +
                                         (bw if k == 0 else []),
                                   inc="T" if k == 7 else None)
                        pre_s[m] = act(stgbuf[:, sl, m, :], pb[bank][:, :], AF.Copy,
                                       waits=[("T", t)])
                        note_bank(bank, "S", pre_s[m])
                    ch = ("se" if base == 0 else "sf") + str(sl)
                    pre1_spill[(j, base)] = dma(
                        "gpsimd", scr.ap()[:, :, jsl].rearrange("c p t -> p c t"),
                        stgbuf[:, sl, :, :], ch,
                        waits=[("S", pre_s[0]), ("S", pre_s[1])])

            for j in range(9):
                if j < 8:
                    l0_front(j)
                if j >= 1:
                    l0_back(j - 1)
            scan0_v = scan_toks[7]

            for fh in range(2):
                bw5 = bank_waits(5)
                for c in range(2):
                    t = mm(pb[5][0:1, :],
                           D[:, c, 4095:4096], wssb[:, c, fh * 512:(fh + 1) * 512],
                           c == 0, False,
                           waits=[("V", scan0_v), tkws] + (bw5 if c == 0 else []))
                t = mm(pb[5][0:1, :], onesb[0:1, 0:1], bs_row[0:1, fh * 512:(fh + 1) * 512],
                       False, True, waits=[tk_bs], inc="T")
                s_cs = act(csrow[0:1, fh * 512:(fh + 1) * 512], pb[5][0:1, :], AF.Copy,
                           waits=[("T", t), tk_gbrow])
                note_bank(5, "S", s_cs)
            tk_c0a = dma("gpsimd", scr_row.ap()[0:DM][None, :], csrow[0:1, :],
                         "gx", waits=[("S", s_cs)])
            tk_cst0 = dma("gpsimd", trtmp[0:8, :],
                          scr_row.ap()[0:DM].rearrange("(a b) -> a b", b=128),
                          "gx", waits=[tk_c0a, ("S", s_modc)])
            t = tr(pb[6][:, 0:8], trtmp[0:8, :], ident[0:8, 0:8],
                   waits=[tk_cst0] + bank_waits(6), inc="T")
            s_csc = act(cs_cols[:, :], pb[6][:, 0:8], AF.Copy, waits=[("T", t)])
            note_bank(6, "S", s_csc)

            v_b2d, _ = bias_cols(bias_d[:, 2:4], wdsb2, cs_cols[:, :], bd_cols[:, 2:4],
                                 [tkwd2, ("S", s_csc)])
            v_b2i, _ = bias_cols(bias_i[:, 2:4], wisb2, cs_cols[:, :], bi_cols[:, 2:4],
                                 [tkwi2, ("S", s_csc)])

            tkwo1 = dma("sync", wosb[:, :, :, :],
                        wo_d.ap()[1].rearrange("(k p) (m n) -> p k m n", p=128, n=128),
                        "wt", waits=[("T", p.cnt["T"])])
            tkws1 = dma("sync", wssb[:, :, :],
                        ws_d.ap()[1].rearrange("(c p) f -> p c f", p=128),
                        "wt", waits=[("T", p.cnt["T"])])

            y0_all_T = p.cnt["T"]
            eq_tok = [None] * 8
            fq_tok = [None] * 8
            q_done_v = [None] * 8
            xt_wb = [None] * 8
            scan1_v = None
            for j8 in range(8):
                sl = j8 % 2
                qsl = slice(j8 * 512, (j8 + 1) * 512)
                wv = [pre1_spill[(6 + sl, 0)], pre1_spill[(6 + sl, 2)]]
                if j8 >= 2 and q_done_v[j8 - 2] is not None:
                    wv.append(("V", q_done_v[j8 - 2]))
                eq_tok[j8] = dma("sync", EqB[:, sl, :, :],
                                 scr_e.ap()[:, :, qsl].rearrange("c p t -> p c t"),
                                 f"eq{sl}", waits=wv)
                fq_tok[j8] = dma("sync", FqB[:, sl, :, :],
                                 scr_f.ap()[:, :, qsl].rearrange("c p t -> p c t"),
                                 f"fq{sl}", waits=wv)
                for c in range(2):
                    Ehh = EqB[:, sl, c, :]
                    Fhh = FqB[:, sl, c, :]
                    Ch = C[:, c, :]
                    bwt = [("V", v_b2d), ("V", v_b2i), eq_tok[j8], fq_tok[j8]]
                    act(Ch, Ehh, AF.Abs, bias=bias_d[:, 2 + c:3 + c], waits=bwt)
                    act(Ch, Ch, AF.Exp, scale=-1.0)
                    s_ln = act(Ch, Ch, AF.Ln, bias=1.0)
                    s_rl = act(Ehh, Ehh, AF.Relu, bias=bias_d[:, 2 + c:3 + c])
                    v_dl = vop(lambda e, Ehh=Ehh, Ch=Ch: e.tensor_add(Ehh, Ehh, Ch),
                               waits=[("S", s_ln), ("S", s_rl)])
                    s_dc = act(Ch, Ehh, AF.Exp, scale=negsA[:, 2 + c:3 + c],
                               waits=[("V", v_dl)])
                    vop(lambda e, Ehh=Ehh, Fhh=Fhh, c=c: e.scalar_tensor_tensor(
                        Fhh, Fhh, bias_i[:, 2 + c:3 + c], Ehh, ALU.add, ALU.mult))
                    init = 0.0 if j8 == 0 else D[:, c, j8 * 512 - 1:j8 * 512]
                    scan1_v = vop(lambda e, Ch=Ch, Fhh=Fhh, c=c, j8=j8, init=init:
                                  e.tensor_tensor_scan(D[:, c, j8 * 512:(j8 + 1) * 512],
                                                       Ch, Fhh, init, ALU.mult, ALU.add),
                                  waits=[("S", s_dc), ("T", y0_all_T)])
                q_done_v[j8] = scan1_v
                w = [("T", xt_lastT[6]), ("T", xt_lastT[7])] if j8 < 2 else                     [xt_wb[j8 - 2]]
                xta = dma("sync", xt[:, sl, 0:4, :],
                          scr_xfm.ap()[0:4, :, qsl].rearrange("c p t -> p c t"),
                          f"xta{sl}", waits=w + list(tk_p1))
                xt_tok[j8] = dma("sync", xt[:, sl, 4:8, :],
                                 scr_xfm.ap()[4:8, :, qsl].rearrange("c p t -> p c t"),
                                 f"xt{sl}", waits=w)
                xta_tok[j8] = xta
                v_u1 = None
                for m in range(8):
                    bank = 5 + (m % 2)
                    bw = bank_waits(bank)
                    for k in range(2):
                        t = mm(pb[bank][:, :], wosb[:, k, m, :], D[:, k, qsl],
                               k == 0, k == 1,
                               waits=[("V", scan1_v), tkwo1] + (bw if k == 0 else []),
                               inc="T" if k == 1 else None)
                    v_u1 = vop(lambda e, bank=bank, m=m, sl=sl: e.scalar_tensor_tensor(
                        xt[:, sl, m, :], pb[bank][:, :], gacl[:, m:m + 1], xt[:, sl, m, :],
                        ALU.mult, ALU.add),
                        waits=[("T", t), xt_tok[j8], xta_tok[j8], ("V", v_ga)])
                    note_bank(bank, "V", v_u1)
                xt_wb[j8] = dma("gpsimd",
                                scr_xfm.ap()[:, :, qsl].rearrange("c p t -> p c t"),
                                xt[:, sl, :, :], f"xw{sl}", waits=[("V", v_u1)])

            for fh in range(2):
                bw5 = bank_waits(5)
                for c in range(2):
                    t = mm(pb[5][0:1, :], D[:, c, 4095:4096],
                           wssb[:, c, fh * 512:(fh + 1) * 512], c == 0, False,
                           waits=[("V", scan1_v), tkws1] + (bw5 if c == 0 else []))
                t = mm(pb[5][0:1, :], onesb[0:1, 0:1], bs_row[0:1, DM + fh * 512:DM + (fh + 1) * 512],
                       False, True, inc="T")
                s_cs1 = act(csrow[0:1, fh * 512:(fh + 1) * 512], pb[5][0:1, :], AF.Copy,
                            waits=[("T", t), tk_cst0])
                note_bank(5, "S", s_cs1)
            tk_c1a = dma("gpsimd", scr_row.ap()[0:DM][None, :], csrow[0:1, :],
                         "gx", waits=[("S", s_cs1)])
            tk_cst1 = dma("gpsimd", trtmp[0:8, :],
                          scr_row.ap()[0:DM].rearrange("(a b) -> a b", b=128),
                          "gx", waits=[tk_c1a, ("S", s_csc)])
            t = tr(pb[6][:, 0:8], trtmp[0:8, :], ident[0:8, 0:8],
                   waits=[tk_cst1] + bank_waits(6), inc="T")
            s_cs1c = act(cs_cols[:, :], pb[6][:, 0:8], AF.Copy, waits=[("T", t)])
            note_bank(6, "S", s_cs1c)
            vop(lambda e: e.tensor_add(gc1[:, :], cs_cols[:, :], bo_cols[:, 8:16]),
                waits=[("S", s_cs1c), tk_bo])
            v_gc1 = vop(lambda e: e.tensor_mul(gc1[:, :], gacl[:, :], gc1[:, :]),
                        waits=[("V", v_ga)])

            p4_lastT = [None] * 8
            h2_st_tok = [None] * 8
            h2prev_l = [[None] * 8]
            xtm_st = {}
            v_up_toks = [None] * 8
            stats_out = [None] * 8

            def p4_b1(j):
                sl = j % 2
                jsl = slice(j * 512, (j + 1) * 512)
                if j < 2:
                    w = [xt_wb[6 + sl]]
                else:
                    w = [("T", p4_lastT[j - 2]), h2_st_tok[j - 2]]
                xta = dma("sync", xt[:, sl, 0:4, :],
                          scr_xfm.ap()[0:4, :, jsl].rearrange("c p t -> p c t"),
                          f"xta{sl}", waits=w)
                xt_tok[j] = dma("sync", xt[:, sl, 4:8, :],
                                scr_xfm.ap()[4:8, :, jsl].rearrange("c p t -> p c t"),
                                f"xt{sl}", waits=w)
                xta_tok[j] = xta
                v_up = vop(lambda e, sl=sl: e.tensor_tensor(
                    xt[:, sl, :, :], xt[:, sl, :, :],
                    gc1[:, :].rearrange("p (e o) -> p e o", o=1)
                    .broadcast_to([128, 8, 512]), ALU.add),
                    waits=[("V", v_gc1), xt_tok[j], xta_tok[j]])
                v_up_toks[j] = v_up
                v_mu, s_rstd = stats_tile(xt[:, sl, :, :], sl, True, [("V", v_up)],
                                          sq4=C)
                v_m2r = vop(lambda e, sl=sl: e.tensor_mul(
                    rowtmp[0:1, sl, :], rows[0:1, sl, 0, :], rows[0:1, sl, 1, :]),
                    waits=[("S", s_rstd)])
                stats_out[j] = (v_mu, s_rstd, v_m2r)

            def p4_b2(j):
                sl = j % 2
                jsl = slice(j * 512, (j + 1) * 512)
                v_up = v_up_toks[j]
                s_rstd, v_m2r = stats_out[j][1], stats_out[j][2]
                t_r = mm(pb[0][:, :], onesb[0:1, 0:128], rows[0:1, sl, 1, :], True, True,
                         waits=[("S", s_rstd)] + bank_waits(0), inc="T")
                t_m = mm(pb[1][:, :], onesb[0:1, 0:128], rowtmp[0:1, sl, :], True, True,
                         waits=[("V", v_m2r)] + bank_waits(1), inc="T")
                for s4 in range(4):
                    stg = h1[:, :, :].rearrange("p m t -> p (m t)")[:, s4 * 1024:(s4 + 1) * 1024]
                    s_tok = v_tok = None
                    for c in range(8):
                        bank = 5 + (c % 2)
                        bw = bank_waits(bank)
                        t = tr(pb[bank][:, 0:128], xt[:, sl, c, s4 * 128:(s4 + 1) * 128],
                               ident,
                               waits=[("V", v_up)] + xtm_st.get((j - 1, s4), []) + bw,
                               inc="T")
                        if c % 2 == 0:
                            s_tok = act(stg[:, c * 128:(c + 1) * 128], pb[bank][:, 0:128],
                                        AF.Copy, waits=[("T", t)])
                            note_bank(bank, "S", s_tok)
                        else:
                            v_tok = vop(lambda e, stg=stg, c=c, bank=bank: e.tensor_copy(
                                stg[:, c * 128:(c + 1) * 128], pb[bank][:, 0:128]),
                                waits=[("T", t)])
                            note_bank(bank, "V", v_tok)
                    rsl = slice((j * 4 + s4) * 128, (j * 4 + s4 + 1) * 128)
                    xtm_st[(j, s4)] = [dma("gpsimd", scr_xtm.ap()[rsl, :], stg,
                                           f"xm{s4}",
                                           waits=[("S", s_tok), ("V", v_tok)])]
                    if stage == "p4":
                        xtm_st[(j, s4)].append(
                            dma("gpsimd", out_d.ap()[rsl, :], stg, f"om{s4}",
                                waits=[("S", s_tok), ("V", v_tok)]))
                h2st_c = [None] * 8
                lbank = (2, 3, 5, 6)
                lg_tok = [None] * 4
                lgT = [None] * 8
                for c in range(8):
                    hsl = h2r[:, c % 2, :]
                    wv1 = [("T", t_r)]
                    if c >= 2:
                        wv1 += [h2st_c[c - 2], ("T", lgT[c - 2])]
                    elif j > 0:
                        wv1 += [h2prev_l[0][6 + c]]
                    v1 = vop(lambda e, hsl=hsl, c=c, sl=sl: e.tensor_mul(
                        hsl, xt[:, sl, c, :], pb[0][:, :]), waits=wv1)
                    note_bank(0, "V", v1)
                    v2 = vop(lambda e, hsl=hsl: e.tensor_add(hsl, hsl, pb[1][:, :]),
                             waits=[("T", t_m)])
                    note_bank(1, "V", v2)
                    v_h2 = vop(lambda e, hsl=hsl, c=c: e.tensor_scalar(
                        hsl, hsl, onecb[:, c:c + 1],
                        mod_cols[:, 24 + c:25 + c], ALU.mult, ALU.add),
                        waits=[("V", v_onecb)])
                    for s4 in range(4):
                        bw = bank_waits(lbank[s4]) if c == 0 else []
                        lg_tok[s4] = mm(pb[lbank[s4]][:, 0:8],
                                        hsl[:, s4 * 128:(s4 + 1) * 128],
                                        gw_sb[:, c, :], c == 0, c == 7,
                                        waits=[("V", v_h2), tk_gw] + bw,
                                        inc="T" if s4 == 3 else None)
                    lgT[c] = lg_tok[3]
                    h2st_c[c] = dma("gpsimd",
                                    scr_h2.ap()[c, :, jsl], hsl, f"h2{c % 2}",
                                    waits=[("V", v_h2)])
                h2_st_tok[j] = h2st_c[7]
                h2prev_l[0] = h2st_c
                R = rt4
                for s4 in range(4):
                    s_sg = act(R[:, s4, 0:8], pb[lbank[s4]][:, 0:8], AF.Exp, scale=-1.0,
                               waits=[("T", lgT[7]), ("V", p.cnt["V"])])
                    note_bank(lbank[s4], "S", s_sg)
                vop(lambda e: e.tensor_scalar_add(R[:, :, 0:8], R[:, :, 0:8], 1.0),
                    waits=[("S", s_sg)])
                vop(lambda e: e.reciprocal(R[:, :, 8:16], R[:, :, 0:8]))
                vop(lambda e: e.tensor_add(R[:, :, 8:16], R[:, :, 8:16], gbias_bc[:, :, :]),
                    waits=[("S", s_gbias)])
                vop(lambda e: e.tensor_add(R[:, :, 16:24:2], R[:, :, 8:16:2], R[:, :, 9:16:2]))
                vop(lambda e: e.tensor_copy(R[:, :, 17:24:2], R[:, :, 16:24:2]))
                vop(lambda e: e.tensor_max(R[:, :, 24:25], R[:, :, 16:17], R[:, :, 20:21]))
                vop(lambda e: e.tensor_tensor(R[:, :, 25:26], R[:, :, 16:17], R[:, :, 20:21], ALU.min))
                vop(lambda e: e.tensor_max(R[:, :, 26:27], R[:, :, 18:19], R[:, :, 22:23]))
                vop(lambda e: e.tensor_tensor(R[:, :, 27:28], R[:, :, 18:19], R[:, :, 22:23], ALU.min))
                vop(lambda e: e.tensor_tensor(R[:, :, 28:29], R[:, :, 24:25], R[:, :, 26:27], ALU.is_ge))
                vop(lambda e: e.tensor_mul(R[:, :, 29:30], R[:, :, 28:29], R[:, :, 25:26]))
                vop(lambda e: e.tensor_scalar(R[:, :, 28:29], R[:, :, 28:29], -1.0, 1.0,
                                              ALU.mult, ALU.add))
                vop(lambda e: e.tensor_mul(R[:, :, 28:29], R[:, :, 28:29], R[:, :, 27:28]))
                vop(lambda e: e.tensor_add(R[:, :, 29:30], R[:, :, 29:30], R[:, :, 28:29]))
                vop(lambda e: e.tensor_tensor(R[:, :, 30:31], R[:, :, 24:25], R[:, :, 26:27], ALU.min))
                vop(lambda e: e.tensor_max(R[:, :, 31:32], R[:, :, 30:31], R[:, :, 29:30]))
                vop(lambda e: e.tensor_tensor(
                    R[:, :, 32:40], R[:, :, 16:24],
                    R[:, :, 31:32].broadcast_to([128, 4, 8]), ALU.is_ge))
                vop(lambda e: e.tensor_mul(R[:, :, 40:48], R[:, :, 8:16], R[:, :, 32:40]))
                for s4 in range(4):
                    vop(lambda e, s4=s4: e.max(m8b4[:, s4, :], R[:, s4, 40:48]))
                vop(lambda e: e.tensor_copy(rp4[:, :, 0:2], m8b4[:, :, 0:2]),
                    waits=[("V", v_rep2)])
                for s4 in range(4):
                    vop(lambda e, s4=s4: e.match_replace(
                        R[:, s4, 32:40], rp4[:, s4, :], R[:, s4, 40:48], -1.0e30))
                vop(lambda e: e.tensor_single_scalar(R[:, :, 32:40], R[:, :, 32:40],
                                                     -1.0e29, ALU.is_le))
                vop(lambda e: e.tensor_mul(R[:, :, 32:40], R[:, :, 32:40], R[:, :, 8:16]))
                vop(lambda e: e.reduce_sum(R[:, :, 24:25], R[:, :, 32:40], AXX))
                vop(lambda e: e.tensor_scalar_add(R[:, :, 24:25], R[:, :, 24:25], 1.0e-20))
                vop(lambda e: e.reciprocal(R[:, :, 25:26], R[:, :, 24:25]))
                vop(lambda e: e.tensor_mul(
                    R[:, :, 40:48], R[:, :, 32:40],
                    R[:, :, 25:26].broadcast_to([128, 4, 8])))
                v_w = vop(lambda e, j=j: e.tensor_scalar_mul(
                    wtok[:, j * 4:(j + 1) * 4, :], R[:, :, 40:48], 2.5))
                p4_lastT[j] = p.cnt["T"]

            for j in range(9):
                if j < 8:
                    p4_b1(j)
                if j >= 1:
                    p4_b2(j - 1)

            attn_bar = p.all_waits()

        if stage == "p4":
            pass
        else:
            with ExitStack() as ffn:
                fn_ = ffn.enter_context
                stgb = fn_(nc.sbuf_tensor("stgb", [128, 8, 1024], F32))
                h2hh = fn_(nc.sbuf_tensor("h2hh", [128, 8, 1024], BF16))
                h2hl = fn_(nc.sbuf_tensor("h2hl", [128, 8, 1024], BF16))
                h2st = fn_(nc.sbuf_tensor("h2st", [128, 2, 1024], F32))
                wbgh = fn_(nc.sbuf_tensor("wbgh", [128, 2, 8, 512], BF16))
                wbgl = fn_(nc.sbuf_tensor("wbgl", [128, 2, 8, 512], BF16))
                wbuh = fn_(nc.sbuf_tensor("wbuh", [128, 2, 8, 512], BF16))
                wbul = fn_(nc.sbuf_tensor("wbul", [128, 2, 8, 512], BF16))
                wbdh = fn_(nc.sbuf_tensor("wbdh", [128, 2, 4, 2, 512], BF16))
                wbdl = fn_(nc.sbuf_tensor("wbdl", [128, 2, 4, 2, 512], BF16))
                wst = fn_(nc.sbuf_tensor("wst", [128, 3, 512], F32))
                mbuf = fn_(nc.sbuf_tensor("mbuf", [128, 512], F32))
                mh = fn_(nc.sbuf_tensor("mh", [128, 4, 512], BF16))
                ml = fn_(nc.sbuf_tensor("ml", [128, 4, 512], BF16))
                sgb = fn_(nc.sbuf_tensor("sgb", [128, 2, 512], F32))
                wrh = fn_(nc.sbuf_tensor("wrh", [8, 256], F32))
                wrow = fn_(nc.sbuf_tensor("wrow", [1, 2, 512], F32))

                first = {e: True for e in ENGINES}

                def bw_first(eng):
                    if first[eng]:
                        first[eng] = False
                        return list(attn_bar)
                    return []

                out_toks = []
                out_by_slot = {}
                h2_split = {0: [], 1: []}
                wr_state = {"tk_w8": None, "wrow_tok": None}

                def emit_wrows(hq, bank):
                    for g2 in range(4):
                        bk = bank if g2 % 2 == 0 else 6
                        bw = bank_waits(bk)
                        t = None
                        for ss4 in range(2):
                            ss = g2 * 2 + ss4
                            t = tr(pb[bk][0:8, ss4 * 128:(ss4 + 1) * 128],
                                   wtok[:, hq * 8 + ss, :], ident,
                                   waits=bw_first("tensor") + bw, inc="T")
                            bw = []
                        s_wr = act(wrh[0:8, :], pb[bk][0:8, 0:256],
                                   AF.Copy, waits=bw_first("scalar") +
                                   [("T", t), wr_state["tk_w8"]])
                        note_bank(bk, "S", s_wr)
                        wr_state["tk_w8"] = dma(
                            "gpsimd", scr_w8.ap()[:, g2 * 256:(g2 + 1) * 256],
                            wrh[0:8, :], "w8",
                            waits=bw_first("gpsimd") + [("S", s_wr)] +
                                  ([wr_state["wrow_tok"]] if g2 == 0 else []))

                for h in range(4):
                    emit_wrows(h, 3)
                    tk_w8 = wr_state["tk_w8"]
                    stg_tok = []
                    for jj in range(2):
                        wx = bw_first("sync") + ([out_by_slot[jj]] if jj in out_by_slot else [])
                        stg_tok.append(dma(
                            "sync", stgb[:, jj * 4:(jj + 1) * 4, :],
                            scr_xtm.ap()[h * 1024 + jj * 512: h * 1024 + (jj + 1) * 512, :]
                            .rearrange("(s p) f -> p s f", p=128), f"sgl{jj}", waits=wx))
                    t_snap = p.cnt["T"]
                    h2S = h2V = None
                    for c in range(8):
                        slc = c % 2
                        tkc = dma("sync", h2st[:, slc, :],
                                  scr_h2.ap()[c, :, h * 1024:(h + 1) * 1024],
                                  f"hh{slc}", waits=bw_first("sync") + h2_split[slc])
                        h2S = act(h2hh[:, c, :], h2st[:, slc, :], AF.Copy,
                                  waits=[tkc, ("T", t_snap)])
                        h2V = vop(lambda e, c=c, slc=slc: e.tensor_tensor(
                            h2hl[:, c, :], h2st[:, slc, :], h2hh[:, c, :], ALU.subtract),
                            waits=bw_first("vector") + [tkc, ("S", h2S)])
                        h2_split[slc] = [("S", h2S), ("V", h2V)]

                    def esrc(e10):
                        if e10 < 8:
                            return (wg_d.ap()[e10].rearrange("(k p) q -> p k q", p=128),
                                    wu_d.ap()[e10].rearrange("(k p) q -> p k q", p=128),
                                    wn_d.ap()[e10].rearrange("(k p) q -> p k q", p=128))
                        ih = e10 - 8
                        return (sg_d.ap()[:, ih * 512:(ih + 1) * 512].rearrange(
                                    "(k p) q -> p k q", p=128),
                                su_d.ap()[:, ih * 512:(ih + 1) * 512].rearrange(
                                    "(k p) q -> p k q", p=128),
                                sd_d.ap()[ih * 512:(ih + 1) * 512, :].rearrange(
                                    "(k p) q -> p k q", p=128))

                    def preload(e10):
                        slw = e10 % 2
                        wprev = [("T", slot_T[slw])] if slot_T[slw] else []
                        toks = {}
                        if h == 0:
                            po = list(pair_out[slw])
                            g_src, u_src, d_src = esrc(e10)
                            qctr = [0]

                            def mk_stage(nm, q, srcap, dsth, dstl, sh, sl_,
                                         wprev=wprev, po=po, e10=e10, slw=slw,
                                         toks=toks):
                                fold = nm == "d"
                                if not fold:
                                    sap = srcap[:, q, :]
                                    dh = dsth[:, slw, q, :]
                                    dl = dstl[:, slw, q, :]
                                    oh = sh.ap()[e10][:, q, :]
                                    ol = sl_.ap()[e10][:, q, :]
                                else:
                                    kq, fhh = q // 2, q % 2
                                    sap = srcap[:, kq, fhh * 512:(fhh + 1) * 512]
                                    dh = dsth[:, slw, kq, fhh, :]
                                    dl = dstl[:, slw, kq, fhh, :]
                                    oh = sh.ap()[e10][:, kq, fhh, :]
                                    ol = sl_.ap()[e10][:, kq, fhh, :]

                                def emit():
                                    qs = qctr[0] % 3
                                    qctr[0] += 1
                                    tk = dma("sync", wst[:, qs, :], sap, f"ws{qs}",
                                             waits=bw_first("sync") + stage_guard[qs])
                                    vf = None
                                    if fold:
                                        vf = vop(lambda e, qs=qs, fhh=q % 2:
                                                 e.tensor_mul(wst[:, qs, :],
                                                              wst[:, qs, :],
                                                              gb_bc[:, fhh, :]),
                                                 waits=bw_first("vector") + [tk])
                                    sS = act(dh, wst[:, qs, :], AF.Copy,
                                             waits=[tk, ("V", vf)] + wprev + po)
                                    sV = vop(lambda e, dl=dl, qs=qs, dh=dh:
                                             e.tensor_tensor(dl, wst[:, qs, :], dh,
                                                             ALU.subtract),
                                             waits=[tk, ("S", sS)] + po)
                                    oh_t = dma("gpsimd", oh, dh, "pwoh",
                                               waits=bw_first("gpsimd") + [("S", sS)])
                                    ol_t = dma("gpsimd", ol, dl, "pwol",
                                               waits=[("V", sV)])
                                    stage_guard[qs] = [("S", sS), ("V", sV)]
                                    if q == 7:
                                        toks[nm] = [("S", sS), ("V", sV)]
                                        if nm == "d":
                                            pair_out[slw] = [oh_t, ol_t]
                                return emit

                            for nm, srcap, dsth, dstl, sh, sl_ in (
                                    ("g", g_src, wbgh, wbgl, scr_gh, scr_gl),
                                    ("u", u_src, wbuh, wbul, scr_uh, scr_ul),
                                    ("d", d_src, wbdh, wbdl, scr_dh, scr_dl)):
                                for q in range(8):
                                    pend.append(mk_stage(nm, q, srcap, dsth,
                                                         dstl, sh, sl_))
                        else:
                            pw = [("pwoh", p.cnt["pwoh"]), ("pwol", p.cnt["pwol"])]
                            for nm, dh, dl, sh, sl_ in (
                                    ("g", wbgh, wbgl, scr_gh, scr_gl),
                                    ("u", wbuh, wbul, scr_uh, scr_ul),
                                    ("d", wbdh, wbdl, scr_dh, scr_dl)):
                                dst = dh[:, slw] if nm == "d" else dh[:, slw, :, :]
                                dstl = dl[:, slw] if nm == "d" else dl[:, slw, :, :]
                                dma("sync", dst, sh.ap()[e10], f"w{nm}{slw}",
                                    waits=bw_first("sync") + wprev + pw)
                                tl = dma("sync", dstl, sl_.ap()[e10], f"w{nm}{slw}",
                                         waits=wprev)
                                toks[nm] = [tl]
                        return toks

                    if h == 0:
                        slot_T = [None, None]
                        stage_guard = {0: [], 1: [], 2: []}
                        pair_out = {0: [], 1: []}
                        sgb_rd = {0: None, 1: None}
                        mh_S = [None]
                        wrow_T = None

                    def wrow_load(e10):
                        wr_state["wrow_tok"] = dma(
                            "sync", wrow[0:1, :, :],
                            scr_w8.ap()[e10][None, :].rearrange(
                                "a (j t) -> a j t", j=2), "wr",
                            waits=[tk_w8, ("T", wrow_T)])
                        return wr_state["wrow_tok"]

                    wrow_tok = wrow_load(0)
                    pend = []

                    def drain(n):
                        for _ in range(min(n, len(pend))):
                            pend.pop(0)()

                    pre = {0: preload(0)}
                    ev_tok = {}
                    for e10 in range(10):
                        routed = e10 < 8
                        slw = e10 % 2
                        drain(len(pend))
                        gtk, utk, dtk = pre[e10]["g"], pre[e10]["u"], pre[e10]["d"]
                        if e10 + 1 < 10:
                            pre[e10 + 1] = preload(e10 + 1)
                        for jj in range(2):
                            jsl = slice(jj * 512, (jj + 1) * 512)
                            wbc_T = None
                            if routed:
                                bw = bank_waits(6)
                                wbc_T = mm(pb[6][:, :], onesb[0:1, 0:128],
                                           wrow[0:1, jj, :], True, True,
                                           waits=[wrow_tok] + bw, inc="T")
                                if jj == 1:
                                    wrow_T = wbc_T
                                    if e10 + 1 < 8:
                                        wrow_tok = wrow_load(e10 + 1)
                            s_mh = [None] * 4
                            v_ml = [None] * 4
                            u_T = [None] * 4
                            for m in range(4):
                                gbank = m % 2
                                bw = bank_waits(gbank)
                                fw = gtk + [("S", h2S), ("V", h2V)] + bw
                                for pi, (lw, rh) in enumerate(
                                        ((wbgh, h2hh), (wbgh, h2hl), (wbgl, h2hh))):
                                    for k in range(8):
                                        st = pi == 0 and k == 0
                                        sp = pi == 2 and k == 7
                                        t = mm(pb[gbank][:, :],
                                               lw[:, slw, k, m * 128:(m + 1) * 128],
                                               rh[:, k, jsl], st, sp,
                                               waits=fw if st else [],
                                               inc="T" if sp else None)
                                g_T = t
                                ubank = 2 + m % 2
                                bw = bank_waits(ubank)
                                fw = utk + bw
                                for pi, (lw, rh) in enumerate(
                                        ((wbuh, h2hh), (wbuh, h2hl), (wbul, h2hh))):
                                    for k in range(8):
                                        st = pi == 0 and k == 0
                                        sp = pi == 2 and k == 7
                                        t = mm(pb[ubank][:, :],
                                               lw[:, slw, k, m * 128:(m + 1) * 128],
                                               rh[:, k, jsl], st, sp,
                                               waits=fw if st else [],
                                               inc="T" if sp else None)
                                u_T[m] = t
                                s_si = act(sgb[:, m % 2, :], pb[gbank][:, :], AF.Silu,
                                           waits=[("T", g_T), ("V", sgb_rd[m % 2])])
                                note_bank(gbank, "S", s_si)
                                v1 = vop(lambda e, m=m, ubank=ubank: e.tensor_mul(
                                    mbuf[:, :], sgb[:, m % 2, :], pb[ubank][:, :]),
                                    waits=[("S", s_si), ("T", u_T[m]),
                                           ("S", mh_S[0])])
                                sgb_rd[m % 2] = v1
                                note_bank(ubank, "V", v1)
                                if routed:
                                    v1 = vop(lambda e: e.tensor_mul(
                                        mbuf[:, :], mbuf[:, :], pb[6][:, :]),
                                        waits=[("T", wbc_T)])
                                    note_bank(6, "V", v1)
                                s_mh[m] = act(mh[:, m, :], mbuf[:, :], AF.Copy,
                                              waits=[("V", v1)])
                                mh_S[0] = s_mh[m]
                                v_ml[m] = vop(lambda e, m=m: e.tensor_tensor(
                                    ml[:, m, :], mbuf[:, :], mh[:, m, :], ALU.subtract),
                                    waits=[("S", s_mh[m])])
                                drain(2)
                            for ts in range(4):
                                for fhh in range(2):
                                    bank = 4 + (ts * 2 + fhh) % 2
                                    bw = bank_waits(bank)
                                    fw = [("S", s_mh[3]), ("V", v_ml[3])] + dtk + bw
                                    for pi, (lh, rh) in enumerate(
                                            ((mh, wbdh), (mh, wbdl), (ml, wbdh))):
                                        for k in range(4):
                                            st = pi == 0 and k == 0
                                            sp = pi == 2 and k == 3
                                            t = mm(pb[bank][:, :],
                                                   lh[:, k, ts * 128:(ts + 1) * 128],
                                                   rh[:, slw, k, fhh, :], st, sp,
                                                   waits=fw if st else [],
                                                   inc="T" if sp else None)
                                    v_ev = vop(lambda e, jj=jj, ts=ts, fhh=fhh, bank=bank:
                                               e.tensor_add(
                                                   stgb[:, jj * 4 + ts, fhh * 512:(fhh + 1) * 512],
                                                   stgb[:, jj * 4 + ts, fhh * 512:(fhh + 1) * 512],
                                                   pb[bank][:, :]),
                                               waits=[("T", t), stg_tok[jj]])
                                    note_bank(bank, "V", v_ev)
                                    ev_tok[(jj, ts, fhh)] = v_ev
                                    drain(1)
                        slot_T[slw] = p.cnt["T"]
                    for jj in range(2):
                        wv = [("V", ev_tok[(jj, ts, fhh)]) for ts in range(4) for fhh in range(2)]
                        out_toks.append(dma(
                            "gpsimd",
                            out_d.ap()[h * 1024 + jj * 512: h * 1024 + (jj + 1) * 512, :]
                            .rearrange("(s p) f -> p s f", p=128),
                            stgb[:, jj * 4:(jj + 1) * 4, :], f"ob{jj}",
                            waits=bw_first("gpsimd") + wv))
                        out_by_slot[jj] = out_toks[-1]

        finals = [(s, v) for s, v in p.all_waits() if s not in ("T", "V", "S") and v > 0]

        def _final_wait(e, fs=tuple(finals)):
            for s, v in fs:
                e.wait_ge(p.hand[s], v)
        p.th["gpsimd"].append(_final_wait)

        with nc.Block() as block:

            @block.tensor
            def _(eng):
                for th in p.th["tensor"]:
                    th(eng)

            @block.vector
            def _(eng):
                for th in p.th["vector"]:
                    th(eng)

            @block.scalar
            def _(eng):
                for th in p.th["scalar"]:
                    th(eng)

            @block.sync
            def _(eng):
                for th in p.th["sync"]:
                    th(eng)

            @block.gpsimd
            def _(eng):
                for th in p.th["gpsimd"]:
                    th(eng)

    return nc


def _host_consts():
    cst = np.zeros((128, 257), dtype=np.float32)
    cst[:, 0:128] = np.eye(128, dtype=np.float32)
    cst[:, 128:256] = 1.0
    cst[:, 256] = 1.0 / 1024.0
    return cst


def kernel(**inputs):
    g = lambda k: np.ascontiguousarray(np.asarray(inputs[k], dtype=np.float32))
    xall = g("L0_attn_shot_embeddings")
    cst = _host_consts()
    shared = dict(
        wmod=g("L0_wmod"), wd=g("L0_attn_Wd"), bd=g("L0_attn_bd"),
        wi=g("L0_attn_Wi"), bi=g("L0_attn_bi"), alog=g("L0_attn_A_log"),
        wo=g("L0_attn_Wo"), bo=g("L0_attn_bo"), wsm=g("L0_attn_Ws"),
        bsv=g("L0_attn_bs"), gw=g("L0_ffn_gate_w"),
        gbias=g("L0_ffn_gate_bias").reshape(1, 8),
        wg=g("L0_ffn_w_gate"), wu=g("L0_ffn_w_up"), wdn=g("L0_ffn_w_down"),
        swg=g("L0_ffn_sw_gate"), swu=g("L0_ffn_sw_up"), swd=g("L0_ffn_sw_down"),
        cst_hbm=cst,
    )
    call = g("L0_c")
    nc = build_nc()
    in_maps = [dict(x_in=np.ascontiguousarray(xall[b]),
                    cvec=np.ascontiguousarray(call[b]), **shared)
               for b in range(8)]
    res = run_bass_kernel_spmd(nc, in_maps, core_ids=list(range(8)))
    return np.stack([res.results[b]["out"] for b in range(8)], axis=0)


if __name__ == "__main__":
    import reference as R
    ins = {k: np.asarray(v) for k, v in R.setup_inputs().items()}
    out = kernel(**ins)
    ref = np.asarray(R.reference(**{k: v for k, v in ins.items()}))
    err = np.linalg.norm(out - ref) / np.linalg.norm(ref)
    print("rel err:", err)


# version 27 (final)
# speedup vs baseline: 1.0589x; 1.0004x over previous
import numpy as np
from contextlib import ExitStack

import concourse.bass as bass
import concourse.mybir as mybir
from concourse.bass_utils import run_bass_kernel_spmd

F32 = mybir.dt.float32
BF16 = mybir.dt.bfloat16
AF = mybir.ActivationFunctionType
ALU = mybir.AluOpType
AXX = mybir.AxisListType.X

SEQ, DM, SD = 4096, 1024, 256
NT, TT = 8, 512
EPS = 1e-5
ENGINES = ("tensor", "vector", "scalar", "sync", "gpsimd")
STAGE = "full"


class P:

    def __init__(self, nc):
        self.nc = nc
        self.th = {e: [] for e in ENGINES}
        self.cnt = {}
        self.hand = {}
        self.waited = {e: {} for e in ENGINES}
        self.own = {"T": "tensor"}

    def mksem(self, stack, *names):
        self.stack = stack
        for n in names:
            self.hand[n] = stack.enter_context(self.nc.semaphore(n))
            self.cnt[n] = 0

    def chan(self, n):
        if n not in self.hand:
            self.hand[n] = self.stack.enter_context(self.nc.semaphore(n))
            self.cnt[n] = 0

    def emit(self, eng, fn, waits=(), inc=None, n=1):
        if eng == "scalar":
            waits = list(waits) + [("S", self.cnt["S"])]
        elif eng == "vector":
            waits = list(waits) + [("V", self.cnt["V"])]
        clean = []
        for wv in waits:
            if wv is None:
                continue
            s, v = wv
            if v is None or v <= 0:
                continue
            if self.own.get(s) == eng:
                continue
            assert v <= self.cnt[s], (eng, s, v, self.cnt[s])
            clean.append((s, v))
        tok = None
        if inc is not None:
            self.cnt[inc] += n
            tok = self.cnt[inc]
        wd = self.waited[eng]
        real = []
        for s, v in clean:
            if wd.get(s, 0) < v:
                real.append((s, v))
                wd[s] = v
        hand = self.hand

        def thunk(e, fn=fn, real=tuple(real), inc=inc, n=n):
            for s, v in real:
                e.wait_ge(hand[s], v)
            inst = fn(e)
            if inc is not None:
                inst.then_inc(hand[inc], n)

        self.th[eng].append(thunk)
        return tok

    def all_waits(self):
        return [(s, self.cnt[s]) for s in self.cnt]


def build_nc(stage=STAGE):
    nc = bass.Bass("TRN2", target_bir_lowering=False)

    x_in = nc.dram_tensor("x_in", [SEQ, DM], F32, kind="ExternalInput")
    cvec = nc.dram_tensor("cvec", [DM], F32, kind="ExternalInput")
    wmod_d = nc.dram_tensor("wmod", [DM, 6 * DM], F32, kind="ExternalInput")
    wd_d = nc.dram_tensor("wd", [2, DM, SD], F32, kind="ExternalInput")
    bd_d = nc.dram_tensor("bd", [2, SD], F32, kind="ExternalInput")
    wi_d = nc.dram_tensor("wi", [2, DM, SD], F32, kind="ExternalInput")
    bi_d = nc.dram_tensor("bi", [2, SD], F32, kind="ExternalInput")
    al_d = nc.dram_tensor("alog", [2, SD], F32, kind="ExternalInput")
    wo_d = nc.dram_tensor("wo", [2, SD, DM], F32, kind="ExternalInput")
    bo_d = nc.dram_tensor("bo", [2, DM], F32, kind="ExternalInput")
    ws_d = nc.dram_tensor("wsm", [2, SD, DM], F32, kind="ExternalInput")
    bs_d = nc.dram_tensor("bsv", [2, DM], F32, kind="ExternalInput")
    gw_d = nc.dram_tensor("gw", [DM, 8], F32, kind="ExternalInput")
    gbias_d = nc.dram_tensor("gbias", [1, 8], F32, kind="ExternalInput")
    wg_d = nc.dram_tensor("wg", [8, DM, 512], F32, kind="ExternalInput")
    wu_d = nc.dram_tensor("wu", [8, DM, 512], F32, kind="ExternalInput")
    wn_d = nc.dram_tensor("wdn", [8, 512, DM], F32, kind="ExternalInput")
    sg_d = nc.dram_tensor("swg", [DM, DM], F32, kind="ExternalInput")
    su_d = nc.dram_tensor("swu", [DM, DM], F32, kind="ExternalInput")
    sd_d = nc.dram_tensor("swd", [DM, DM], F32, kind="ExternalInput")
    cst_d = nc.dram_tensor("cst_hbm", [128, 257], F32, kind="ExternalInput")

    scr_row = nc.dram_tensor("scr_row", [6 * DM], F32)
    scr_w8 = nc.dram_tensor("scr_w8", [8, 1024], F32)
    scr_e = nc.dram_tensor("scr_e", [2, 128, SEQ], F32)
    scr_f = nc.dram_tensor("scr_f", [2, 128, SEQ], F32)
    scr_xfm = nc.dram_tensor("scr_xfm", [8, 128, SEQ], F32)
    scr_h2 = nc.dram_tensor("scr_h2", [8, 128, SEQ], F32)
    scr_xtm = nc.dram_tensor("scr_xtm", [SEQ, DM], F32)
    scr_gh = nc.dram_tensor("scr_gh", [10, 128, 8, 512], BF16)
    scr_gl = nc.dram_tensor("scr_gl", [10, 128, 8, 512], BF16)
    scr_uh = nc.dram_tensor("scr_uh", [10, 128, 8, 512], BF16)
    scr_ul = nc.dram_tensor("scr_ul", [10, 128, 8, 512], BF16)
    scr_dh = nc.dram_tensor("scr_dh", [10, 128, 4, 2, 512], BF16)
    scr_dl = nc.dram_tensor("scr_dl", [10, 128, 4, 2, 512], BF16)
    out_d = nc.dram_tensor("out", [SEQ, DM], F32, kind="ExternalOutput")

    p = P(nc)

    with ExitStack() as top:
        en = top.enter_context
        p.mksem(top, "T", "V", "S")

        cst = en(nc.sbuf_tensor("cst", [128, 257], F32))
        ident = cst[:, 0:128]
        onesb = cst[:, 128:256]
        invd = cst[:, 256:257]
        rowtmp = en(nc.sbuf_tensor("rowtmp", [1, 2, 512], F32))
        mod_cols = en(nc.sbuf_tensor("mod_cols", [128, 48], F32))
        oneca = en(nc.sbuf_tensor("oneca", [128, 8], F32))
        onecb = en(nc.sbuf_tensor("onecb", [128, 8], F32))
        gacl = en(nc.sbuf_tensor("gacl", [128, 8], F32))
        gbcl = en(nc.sbuf_tensor("gbcl", [128, 8], F32))
        gb_bc = en(nc.sbuf_tensor("gb_bc", [128, 2, 512], F32))
        negsA = en(nc.sbuf_tensor("negsA", [128, 4], F32))
        bo_cols = en(nc.sbuf_tensor("bo_cols", [128, 16], F32))
        bd_cols = en(nc.sbuf_tensor("bd_cols", [128, 4], F32))
        bi_cols = en(nc.sbuf_tensor("bi_cols", [128, 4], F32))
        bias_d = en(nc.sbuf_tensor("bias_d", [128, 4], F32))
        bias_i = en(nc.sbuf_tensor("bias_i", [128, 4], F32))
        nsd_row = en(nc.sbuf_tensor("nsd_row", [1, SD], F32))
        nsi_row = en(nc.sbuf_tensor("nsi_row", [1, SD], F32))
        cs_cols = en(nc.sbuf_tensor("cs_cols", [128, 8], F32))
        gc1 = en(nc.sbuf_tensor("gc1", [128, 8], F32))
        gw_sb = en(nc.sbuf_tensor("gw_sb", [128, 8, 8], F32))
        gbias_row = en(nc.sbuf_tensor("gbias_row", [1, 8], F32))
        gbias_bc = en(nc.sbuf_tensor("gbias_bc", [128, 4, 8], F32))
        wtok = en(nc.sbuf_tensor("wtok", [128, 32, 8], F32))
        rep2 = en(nc.sbuf_tensor("rep2", [128, 8], F32))
        ccols = en(nc.sbuf_tensor("ccols", [128, 8], F32))
        scc = en(nc.sbuf_tensor("scc", [128, 8], F32))
        trtmp = en(nc.sbuf_tensor("trtmp", [48, 128], F32))
        rt4 = en(nc.sbuf_tensor("rt4", [128, 4, 48], F32))
        m8b4 = en(nc.sbuf_tensor("m8b4", [128, 4, 8], F32))
        rp4 = en(nc.sbuf_tensor("rp4", [128, 4, 8], F32))

        pb = [en(nc.psum_tensor(f"pb{i}", [128, 512], F32)) for i in range(8)]
        bank_cons = [[] for _ in range(8)]

        def bank_waits(i):
            w = bank_cons[i]
            bank_cons[i] = []
            return w

        def note_bank(i, sem, tok):
            bank_cons[i].append((sem, tok))

        def mm(out_ap, lhsT, rhs, start, stop, waits=(), inc=None):
            return p.emit("tensor",
                          lambda e, o=out_ap, l=lhsT, r=rhs, a=start, b=stop:
                          e.matmul(o, l, r, start=a, stop=b),
                          waits=waits, inc=inc)

        def tr(out_ap, in_ap, id_ap, waits=(), inc=None):
            return p.emit("tensor",
                          lambda e, o=out_ap, i=in_ap, d=id_ap: e.transpose(o, i, d),
                          waits=waits, inc=inc)

        def act(out_ap, in_ap, func, bias=0.0, scale=1.0, waits=(), inc="S"):
            return p.emit("scalar",
                          lambda e, o=out_ap, i=in_ap, f=func, b=bias, s=scale:
                          e.activation(o, i, f, bias=b, scale=s),
                          waits=waits, inc=inc)

        def vop(fn, waits=(), inc="V"):
            return p.emit("vector", fn, waits=waits, inc=inc)

        def dma(eng, out_ap, in_ap, ch, waits=(), slow=False):
            p.chan(ch)
            swait = [(ch, p.cnt[ch])]
            tok = p.emit(eng,
                         lambda e, o=out_ap, i=in_ap, sl=slow:
                         e.dma_start(out=o, in_=i, allow_slow_non_contiguous=sl),
                         waits=list(waits) + swait, inc=ch, n=16)
            return (ch, tok)

        tk_cst = dma("sync", cst[:, :], cst_d.ap(), "c_cst")
        tk_c = dma("sync", ccols[:, :], cvec.ap().rearrange("(k p) -> p k", p=128), "c_c", slow=True)
        tk_gw = dma("sync", gw_sb[:, :, :], gw_d.ap().rearrange("(k p) e -> p k e", p=128), "c_gw", slow=True)
        tk_gbr = dma("sync", gbias_row[0:1, :], gbias_d.ap(), "c_gbr")
        tk_al = dma("sync", negsA[:, :], al_d.ap().rearrange("l (k p) -> p (l k)", p=128), "c_al", slow=True)
        tk_bo = dma("sync", bo_cols[:, :], bo_d.ap().rearrange("l (k p) -> p (l k)", p=128), "c_bo", slow=True)
        tk_bd = dma("sync", bd_cols[:, :], bd_d.ap().rearrange("l (m p) -> p (l m)", p=128), "c_bd", slow=True)
        tk_bi = dma("sync", bi_cols[:, :], bi_d.ap().rearrange("l (m p) -> p (l m)", p=128), "c_bi", slow=True)

        for s4 in range(4):
            t = mm(pb[0][:, s4 * 8:(s4 + 1) * 8], onesb[0:1, 0:128],
                   gbias_row[0:1, :], s4 == 0, s4 == 3,
                   waits=[tk_gbr, tk_cst], inc="T" if s4 == 3 else None)
        s_gbias = act(gbias_bc[:, :, :], pb[0][:, 0:32].rearrange(
            "p (s e) -> p s e", e=8), AF.Copy, waits=[("T", t)])
        note_bank(0, "S", s_gbias)

        s_sc = act(scc[:, :], ccols[:, :], AF.Silu, waits=[tk_c])

        with nc.sbuf_tensor("wmodb", [128, 2, 8, 512], F32) as wmodb:
            mm_toks = [None] * 12
            row_st = [None] * 12
            for j in range(12):
                sl = j % 2
                w = [("T", mm_toks[j - 2])] if j >= 2 else []
                wjt = dma("sync", wmodb[:, sl, :, :],
                          wmod_d.ap()[:, j * 512:(j + 1) * 512].rearrange(
                              "(k p) n -> p k n", p=128), f"wm{sl}", waits=w)
                bw = bank_waits(1)
                for k in range(8):
                    t = mm(pb[1][0:1, :], scc[:, k:k + 1], wmodb[:, sl, k, :],
                           k == 0, k == 7,
                           waits=[wjt, ("S", s_sc)] + (bw if k == 0 else []),
                           inc="T" if k == 7 else None)
                mm_toks[j] = t
                s_t = act(rowtmp[0:1, sl, :], pb[1][0:1, :], AF.Copy,
                          waits=[("T", t), row_st[j - 2]] if j >= 2
                          else [("T", t)])
                note_bank(1, "S", s_t)
                row_st[j] = dma("gpsimd", scr_row.ap()[j * 512:(j + 1) * 512][None, :],
                                rowtmp[0:1, sl, :], "gx", waits=[("S", s_t)])
            tk_modrow = row_st[11]

        tk_trt = dma("gpsimd", trtmp[0:48, :],
                     scr_row.ap().rearrange("(a b) -> a b", b=128), "gx",
                     waits=[tk_modrow])
        t = tr(pb[2][:, 0:48], trtmp[0:48, :], ident[0:48, 0:48],
               waits=[tk_trt] + bank_waits(2), inc="T")
        s_modc = act(mod_cols[:, :], pb[2][:, 0:48], AF.Copy, waits=[("T", t)])
        note_bank(2, "S", s_modc)

        v_oneca = vop(lambda e: e.tensor_scalar_add(oneca[:, :], mod_cols[:, 8:16], 1.0),
                      waits=[("S", s_modc)])
        v_onecb = vop(lambda e: e.tensor_scalar_add(onecb[:, :], mod_cols[:, 32:40], 1.0))

        def clamp(dst, lo_col, extra):
            sl = slice(lo_col, lo_col + 8)
            s_a = act(gc1[:, :], mod_cols[:, sl], AF.Abs,
                      waits=[("S", s_modc)] + extra)
            s_s = act(rep2[:, :], mod_cols[:, sl], AF.Sign)
            vop(lambda e: e.tensor_scalar_max(gc1[:, :], gc1[:, :], 0.25),
                waits=[("S", s_a), ("S", s_s)])
            return vop(lambda e, dst=dst: e.tensor_mul(dst, rep2[:, :], gc1[:, :]))

        v_ga = clamp(gacl[:, :], 16, [])
        v_gb = clamp(gbcl[:, :], 40, [("V", v_ga)])


        a1 = act(gc1[:, 0:4], negsA[:, :], AF.Abs, waits=[tk_al, ("V", v_gb)])
        act(gc1[:, 0:4], gc1[:, 0:4], AF.Exp, scale=-1.0)
        a3 = act(gc1[:, 0:4], gc1[:, 0:4], AF.Ln, bias=1.0)
        a4 = act(rep2[:, 0:4], negsA[:, :], AF.Relu)
        v5 = vop(lambda e: e.tensor_add(rep2[:, 0:4], rep2[:, 0:4], gc1[:, 0:4]),
                 waits=[("S", a4)])
        v_nsa = vop(lambda e: e.tensor_scalar_mul(negsA[:, :], rep2[:, 0:4], -1.0))
        v_rep2 = vop(lambda e: e.memset(rp4[:, :, :], 1.0e30))

        with ExitStack() as attn:
            an = attn.enter_context
            A = an(nc.sbuf_tensor("Ab", [128, 2, 512], F32))
            B = an(nc.sbuf_tensor("Bb", [128, 2, 512], F32))
            C = an(nc.sbuf_tensor("Cb", [128, 2, 512], F32))
            D = an(nc.sbuf_tensor("Db", [128, 2, 4096], F32))
            EqB = an(nc.sbuf_tensor("EqB", [128, 2, 2, 512], F32))
            FqB = an(nc.sbuf_tensor("FqB", [128, 2, 2, 512], F32))
            h2r = an(nc.sbuf_tensor("h2r", [128, 2, 512], F32))
            xt = an(nc.sbuf_tensor("xt", [128, 2, 8, 512], F32))
            p1 = [xt[:, s, 0:2, :].rearrange("p a t -> p (a t)") for s in range(2)]
            p1s = [xt[:, s, 2:4, :].rearrange("p a t -> p (a t)") for s in range(2)]
            tr_last = [None, None]
            ev_last = [None] * 32
            ld32 = [None] * 32
            st32 = [None] * 32
            for jt in range(32):
                sl = jt % 2
                w = [("T", tr_last[sl])] if jt >= 2 else []
                ld32[jt] = dma("sync", p1[sl], x_in.ap()[jt * 128:(jt + 1) * 128, :],
                               f"x{sl}", waits=w)
                s_tok = v_tok = None
                pv = st32[jt - 2] if jt >= 2 else None
                for c in range(8):
                    bank = 4 + sl
                    t = tr(pb[bank][:, 0:128], p1[sl][:, c * 128:(c + 1) * 128], ident,
                           waits=[ld32[jt], tk_cst] + bank_waits(bank),
                           inc="T")
                    if c == 7:
                        tr_last[sl] = t
                    if c % 2 == 0:
                        s_tok = act(p1s[sl][:, c * 128:(c + 1) * 128], pb[bank][:, 0:128],
                                    AF.Copy, waits=[("T", t), pv])
                        note_bank(bank, "S", s_tok)
                    else:
                        v_tok = vop(lambda e, sl=sl, c=c, bank=bank:
                                    e.tensor_copy(p1s[sl][:, c * 128:(c + 1) * 128],
                                                  pb[bank][:, 0:128]),
                                    waits=[("T", t), pv])
                        note_bank(bank, "V", v_tok)
                ev_last[jt] = (s_tok, v_tok)
                st32[jt] = dma("gpsimd",
                    scr_xfm.ap()[:, :, jt * 128:(jt + 1) * 128].rearrange("c p t -> p c t"),
                    p1s[sl].rearrange("p (c t) -> p c t", t=128), f"st{sl}",
                    waits=[("S", s_tok), ("V", v_tok)])
            tk_p1 = [st32[30], st32[31]]

            h1 = an(nc.sbuf_tensor("h1", [128, 8, 512], F32))
            sq = an(nc.sbuf_tensor("sq", [128, 2, 512], F32))
            rb2 = an(nc.sbuf_tensor("rb2", [128, 2, 512], F32))
            wdsb = an(nc.sbuf_tensor("wdsb", [128, 8, 2, 128], F32))
            wisb = an(nc.sbuf_tensor("wisb", [128, 8, 2, 128], F32))
            wdsb2 = an(nc.sbuf_tensor("wdsb2", [128, 8, 2, 128], F32))
            wisb2 = an(nc.sbuf_tensor("wisb2", [128, 8, 2, 128], F32))
            wosb = an(nc.sbuf_tensor("wosb", [128, 2, 8, 128], F32))
            wssb = an(nc.sbuf_tensor("wssb", [128, 2, 1024], F32))
            rows = an(nc.sbuf_tensor("rows", [1, 2, 2, 512], F32))
            bs_row = an(nc.sbuf_tensor("bs_row", [1, 2 * DM], F32))
            csrow = an(nc.sbuf_tensor("csrow", [1, DM], F32))
            g8row = an(nc.sbuf_tensor("g8row", [8, 128], F32))

            tk_bs = dma("sync", bs_row[0:1, :],
                        bs_d.ap().rearrange("l f -> (l f)")[None, :], "wt")
            t = tr(pb[2][0:8, 0:128], gbcl[:, :], ident[:, :],
                   waits=[("V", v_gb)] + bank_waits(2), inc="T")
            s_g8 = act(g8row[0:8, :], pb[2][0:8, 0:128], AF.Copy, waits=[("T", t)])
            note_bank(2, "S", s_g8)
            tk_g1 = dma("gpsimd", scr_row.ap()[0:DM].rearrange("(a b) -> a b", b=128),
                        g8row[0:8, :], "gx", waits=[("S", s_g8)])
            tk_gbrow = dma("gpsimd", csrow[0:1, :], scr_row.ap()[0:DM][None, :],
                           "gx", waits=[tk_g1])
            for fh in range(2):
                t = mm(pb[3][:, :], onesb[0:1, 0:128], csrow[0:1, fh * 512:(fh + 1) * 512],
                       True, True, waits=[tk_gbrow] + bank_waits(3), inc="T")
                s_gbbc = act(gb_bc[:, fh, :], pb[3][:, :], AF.Copy, waits=[("T", t)])
                note_bank(3, "S", s_gbbc)

            pg = list(tk_p1)
            tkwd = dma("sync", wdsb[:, :, :, :],
                       wd_d.ap()[0].rearrange("(k p) (m n) -> p k m n", p=128, n=128),
                       "wt", waits=pg)
            tkwi = dma("sync", wisb[:, :, :, :],
                       wi_d.ap()[0].rearrange("(k p) (m n) -> p k m n", p=128, n=128),
                       "wt", waits=pg)
            tkwd2 = dma("sync", wdsb2[:, :, :, :],
                        wd_d.ap()[1].rearrange("(k p) (m n) -> p k m n", p=128, n=128),
                        "wt", waits=pg)
            tkwi2 = dma("sync", wisb2[:, :, :, :],
                        wi_d.ap()[1].rearrange("(k p) (m n) -> p k m n", p=128, n=128),
                        "wt", waits=pg)
            tkwo = dma("sync", wosb[:, :, :, :],
                       wo_d.ap()[0].rearrange("(k p) (m n) -> p k m n", p=128, n=128),
                       "wt", waits=pg)
            tkws = dma("sync", wssb[:, :, :],
                       ws_d.ap()[0].rearrange("(c p) f -> p c f", p=128),
                       "wt", waits=pg)

            def bias_cols(dst2, w_sb, rhs_cols, add_cols, extra_waits):
                last_t = None
                for m in range(2):
                    bank = 5 + m
                    bw = bank_waits(bank)
                    for k in range(8):
                        last_t = mm(pb[bank][:, 0:1], w_sb[:, k, m, :], rhs_cols[:, k:k + 1],
                                    k == 0, k == 7,
                                    waits=extra_waits + (bw if k == 0 else []),
                                    inc="T" if k == 7 else None)
                    s_t = act(dst2[:, m:m + 1], pb[bank][:, 0:1], AF.Copy,
                              waits=[("T", last_t)])
                    note_bank(bank, "S", s_t)
                v_t = vop(lambda e, dst2=dst2, add_cols=add_cols:
                          e.tensor_add(dst2[:, 0:2], dst2[:, 0:2], add_cols),
                          waits=[("S", s_t)])
                return v_t, last_t

            v_b1d, t_bd = bias_cols(bias_d[:, 0:2], wdsb, mod_cols[:, 0:8], bd_cols[:, 0:2],
                                    [tkwd, ("S", s_modc), tk_bd])
            v_b1i, t_bi = bias_cols(bias_i[:, 0:2], wisb, mod_cols[:, 0:8], bi_cols[:, 0:2],
                                    [tkwi, ("S", s_modc), tk_bi])

            v_sc = None
            for k in range(8):
                v_sc = vop(lambda e, k=k: e.tensor_scalar_mul(
                    wdsb[:, k, :, :], wdsb[:, k, :, :], oneca[:, k:k + 1]),
                    waits=[("V", v_oneca), ("T", t_bd), ("T", t_bi)])
                v_sc = vop(lambda e, k=k: e.tensor_scalar_mul(
                    wisb[:, k, :, :], wisb[:, k, :, :], oneca[:, k:k + 1]))

            def colsums(dst_row, w_sb, prev_v):
                bw = bank_waits(7)
                last_t = None
                for k in range(8):
                    last_t = mm(pb[7][0:1, 0:256], onesb[:, 0:1],
                                w_sb[:, k, :, :].rearrange("p m n -> p (m n)"),
                                k == 0, k == 7,
                                waits=[("V", prev_v)] + (bw if k == 0 else []),
                                inc="T" if k == 7 else None)
                v_t = vop(lambda e, dst_row=dst_row:
                          e.tensor_scalar_mul(dst_row, pb[7][0:1, 0:256], -1.0),
                          waits=[("T", last_t)])
                note_bank(7, "V", v_t)
                return v_t

            v_nsd = colsums(nsd_row[0:1, :], wdsb, v_sc)
            v_nsi = colsums(nsi_row[0:1, :], wisb, v_sc)

            def stats_tile(src_tile, slot, negate_mu, src_waits, sq4=None):
                bw7 = bank_waits(7)
                for k in range(8):
                    t_mu = mm(pb[7][0:1, :], invd, src_tile[:, k, :], k == 0, k == 7,
                              waits=src_waits + (bw7 if k == 0 else []),
                              inc="T" if k == 7 else None)
                bw4 = bank_waits(4)
                slots = [sq[:, 0, :], sq[:, 1, :]]
                if sq4 is not None:
                    slots += [sq4[:, 0, :]]
                G = len(slots)
                sq_t = [None] * G
                for k in range(8):
                    g = k % G
                    row = g * 32 if G == 3 else 0
                    st_f = k < G if G == 3 else k == 0
                    sp_f = k >= 8 - G if G == 3 else k == 7
                    s_t = act(slots[g], src_tile[:, k, :], AF.Square,
                              waits=src_waits + ([("T", sq_t[g])] if sq_t[g] else []))
                    t_sq = mm(pb[4][row:row + 1, :], invd, slots[g],
                              st_f, sp_f,
                              waits=[("S", s_t)] + (bw4 if st_f else []),
                              inc="T")
                    sq_t[g] = t_sq
                mu_sl = rows[0:1, slot, 0, :]
                rstd_sl = rows[0:1, slot, 1, :]
                scr_sl = rowtmp[0:1, slot, :]
                v_mu = vop(lambda e, mu_sl=mu_sl, f=(-1.0 if negate_mu else 1.0):
                           e.tensor_scalar_mul(mu_sl, pb[7][0:1, :], f),
                           waits=[("T", t_mu)])
                note_bank(7, "V", v_mu)
                vop(lambda e, scr_sl=scr_sl, mu_sl=mu_sl: e.tensor_mul(scr_sl, mu_sl, mu_sl),
                    waits=src_waits)
                if G == 3:
                    vop(lambda e, rstd_sl=rstd_sl: e.tensor_copy(
                        rstd_sl, pb[4][0:1, :]), waits=[("T", t_sq)])
                    vop(lambda e, rstd_sl=rstd_sl: e.tensor_add(
                        rstd_sl, rstd_sl, pb[4][32:33, :]))
                    v_s4 = vop(lambda e, rstd_sl=rstd_sl: e.tensor_add(
                        rstd_sl, rstd_sl, pb[4][64:65, :]))
                    note_bank(4, "V", v_s4)
                    v_ve = vop(lambda e, scr_sl=scr_sl, rstd_sl=rstd_sl:
                               e.scalar_tensor_tensor(
                                   scr_sl, rstd_sl, EPS, scr_sl, ALU.add, ALU.subtract))
                else:
                    v_ve = vop(lambda e, scr_sl=scr_sl: e.scalar_tensor_tensor(
                        scr_sl, pb[4][0:1, :], EPS, scr_sl, ALU.add, ALU.subtract),
                        waits=[("T", t_sq)])
                    note_bank(4, "V", v_ve)
                act(scr_sl, scr_sl, AF.Ln, waits=[("V", v_ve)])
                s_rstd = act(rstd_sl, scr_sl, AF.Exp, scale=-0.5)
                return v_mu, s_rstd

            xt_tok = [None] * 8
            xt_lastT = [None] * 8
            pre1_spill = {}
            scan0_v = None

            scan_toks = [None] * 8
            zlv = [None] * 8

            def l0_front(j):
                sl = j % 2
                jsl = slice(j * 512, (j + 1) * 512)
                w = list(tk_p1) if j < 2 else [("T", xt_lastT[j - 2])] + list(tk_p1)
                xt_tok[j] = dma("sync", xt[:, sl, :, :],
                                scr_xfm.ap()[:, :, jsl].rearrange("c p t -> p c t"),
                                f"xt{sl}", waits=w)
                v_mu, s_rstd = stats_tile(xt[:, sl, :, :], sl, False,
                                          [xt_tok[j], tk_modrow, tk_trt])
                t = mm(pb[4][:, :], onesb[0:1, 0:128], rows[0:1, sl, 1, :], True, True,
                       waits=[("S", s_rstd)] + bank_waits(4), inc="T")
                s_rb = act(rb2[:, sl, :], pb[4][:, :], AF.Copy,
                           waits=[("T", t), ("V", p.cnt["V"])])
                note_bank(4, "S", s_rb)
                z_last_v = None
                for (w_sb, ns_row, dstbuf, base, v_ns) in (
                        (wdsb, nsd_row, A, 0, v_nsd), (wisb, nsi_row, B, 2, v_nsi)):
                    for m in range(2):
                        bank = base + m
                        bw = bank_waits(bank)
                        for k in range(8):
                            mm(pb[bank][:, :], w_sb[:, k, m, :], xt[:, sl, k, :],
                               k == 0, False,
                               waits=([("V", v_ns), xt_tok[j]] +
                                      (bw if k == 0 else [])))
                        t = mm(pb[bank][:, :], ns_row[0:1, m * 128:(m + 1) * 128],
                               rows[0:1, sl, 0, :], False, True,
                               waits=[("V", v_mu)], inc="T")
                        v_z = vop(lambda e, bank=bank, dstbuf=dstbuf, m=m, sl=sl:
                                  e.tensor_mul(dstbuf[:, m, :],
                                               pb[bank][:, :], rb2[:, sl, :]),
                                  waits=[("T", t), ("S", s_rb)])
                        note_bank(bank, "V", v_z)
                        z_last_v = v_z
                xt_lastT[j] = p.cnt["T"]
                zlv[j] = z_last_v

                for c in range(2):
                    Ah, Bh, Ch = A[:, c, :], B[:, c, :], C[:, c, :]
                    bwt = [("V", z_last_v), ("V", v_b1d), ("V", v_b1i), ("V", v_nsa)]
                    act(Ch, Ah, AF.Abs, bias=bias_d[:, c:c + 1], waits=bwt)
                    act(Ch, Ch, AF.Exp, scale=-1.0)
                    s_ln = act(Ch, Ch, AF.Ln, bias=1.0)
                    s_rl = act(Ah, Ah, AF.Relu, bias=bias_d[:, c:c + 1])
                    v_dl = vop(lambda e, Ah=Ah, Ch=Ch: e.tensor_add(Ah, Ah, Ch),
                               waits=[("S", s_ln), ("S", s_rl)])
                    s_dc = act(Ch, Ah, AF.Exp, scale=negsA[:, c:c + 1],
                               waits=[("V", v_dl)])
                    vop(lambda e, Ah=Ah, Bh=Bh, c=c: e.scalar_tensor_tensor(
                        Bh, Bh, bias_i[:, c:c + 1], Ah, ALU.add, ALU.mult))
                    init = 0.0 if j == 0 else D[:, c, j * 512 - 1:j * 512]
                    scan_toks[j] = vop(lambda e, Ch=Ch, Bh=Bh, c=c, j=j, init=init:
                                       e.tensor_tensor_scan(
                                           D[:, c, j * 512:(j + 1) * 512],
                                           Ch, Bh, init, ALU.mult, ALU.add),
                                       waits=[("S", s_dc)])

            def l0_back(j):
                jsl = slice(j * 512, (j + 1) * 512)
                h1_s = [None] * 8
                for m in range(8):
                    bank = 5 + (m % 2)
                    bw = bank_waits(bank)
                    for k in range(2):
                        t = mm(pb[bank][:, :], wosb[:, k, m, :], D[:, k, jsl],
                               k == 0, k == 1,
                               waits=[("V", scan_toks[j]), tkwo] + (bw if k == 0 else []),
                               inc="T" if k == 1 else None)
                    h1_s[m] = act(h1[:, m, :], pb[bank][:, :], AF.Identity,
                                  bias=bo_cols[:, m:m + 1],
                                  waits=[("T", t), tk_bo])
                    note_bank(bank, "S", h1_s[m])
                for (w_sb2, stgbuf, scr, base, tk_w) in (
                        (wdsb2, EqB, scr_e, 0, tkwd2), (wisb2, FqB, scr_f, 2, tkwi2)):
                    sl = j % 2
                    pre_s = [None, None]
                    for m in range(2):
                        bank = base + m
                        bw = bank_waits(bank)
                        for k in range(8):
                            t = mm(pb[bank][:, :], w_sb2[:, k, m, :], h1[:, k, :],
                                   k == 0, k == 7,
                                   waits=[tk_w, ("S", h1_s[k]),
                                          pre1_spill.get((j - 2, base))] +
                                         (bw if k == 0 else []),
                                   inc="T" if k == 7 else None)
                        pre_s[m] = act(stgbuf[:, sl, m, :], pb[bank][:, :], AF.Copy,
                                       waits=[("T", t)])
                        note_bank(bank, "S", pre_s[m])
                    ch = ("se" if base == 0 else "sf") + str(sl)
                    pre1_spill[(j, base)] = dma(
                        "gpsimd", scr.ap()[:, :, jsl].rearrange("c p t -> p c t"),
                        stgbuf[:, sl, :, :], ch,
                        waits=[("S", pre_s[0]), ("S", pre_s[1])])

            for j in range(9):
                if j < 8:
                    l0_front(j)
                if j >= 1:
                    l0_back(j - 1)
            scan0_v = scan_toks[7]

            for fh in range(2):
                bw5 = bank_waits(5)
                for c in range(2):
                    t = mm(pb[5][0:1, :],
                           D[:, c, 4095:4096], wssb[:, c, fh * 512:(fh + 1) * 512],
                           c == 0, False,
                           waits=[("V", scan0_v), tkws] + (bw5 if c == 0 else []))
                t = mm(pb[5][0:1, :], onesb[0:1, 0:1], bs_row[0:1, fh * 512:(fh + 1) * 512],
                       False, True, waits=[tk_bs], inc="T")
                s_cs = act(csrow[0:1, fh * 512:(fh + 1) * 512], pb[5][0:1, :], AF.Copy,
                           waits=[("T", t), tk_gbrow])
                note_bank(5, "S", s_cs)
            tk_c0a = dma("gpsimd", scr_row.ap()[0:DM][None, :], csrow[0:1, :],
                         "gx", waits=[("S", s_cs)])
            tk_cst0 = dma("gpsimd", trtmp[0:8, :],
                          scr_row.ap()[0:DM].rearrange("(a b) -> a b", b=128),
                          "gx", waits=[tk_c0a, ("S", s_modc)])
            t = tr(pb[6][:, 0:8], trtmp[0:8, :], ident[0:8, 0:8],
                   waits=[tk_cst0] + bank_waits(6), inc="T")
            s_csc = act(cs_cols[:, :], pb[6][:, 0:8], AF.Copy, waits=[("T", t)])
            note_bank(6, "S", s_csc)

            v_b2d, _ = bias_cols(bias_d[:, 2:4], wdsb2, cs_cols[:, :], bd_cols[:, 2:4],
                                 [tkwd2, ("S", s_csc)])
            v_b2i, _ = bias_cols(bias_i[:, 2:4], wisb2, cs_cols[:, :], bi_cols[:, 2:4],
                                 [tkwi2, ("S", s_csc)])

            tkwo1 = dma("sync", wosb[:, :, :, :],
                        wo_d.ap()[1].rearrange("(k p) (m n) -> p k m n", p=128, n=128),
                        "wt", waits=[("T", p.cnt["T"])])
            tkws1 = dma("sync", wssb[:, :, :],
                        ws_d.ap()[1].rearrange("(c p) f -> p c f", p=128),
                        "wt", waits=[("T", p.cnt["T"])])

            y0_all_T = p.cnt["T"]
            eq_tok = [None] * 8
            fq_tok = [None] * 8
            q_done_v = [None] * 8
            xt_wb = [None] * 8
            scan1_v = None
            for j8 in range(8):
                sl = j8 % 2
                qsl = slice(j8 * 512, (j8 + 1) * 512)
                wv = [pre1_spill[(6 + sl, 0)], pre1_spill[(6 + sl, 2)]]
                if j8 >= 2 and q_done_v[j8 - 2] is not None:
                    wv.append(("V", q_done_v[j8 - 2]))
                eq_tok[j8] = dma("sync", EqB[:, sl, :, :],
                                 scr_e.ap()[:, :, qsl].rearrange("c p t -> p c t"),
                                 f"eq{sl}", waits=wv)
                fq_tok[j8] = dma("sync", FqB[:, sl, :, :],
                                 scr_f.ap()[:, :, qsl].rearrange("c p t -> p c t"),
                                 f"fq{sl}", waits=wv)
                for c in range(2):
                    Ehh = EqB[:, sl, c, :]
                    Fhh = FqB[:, sl, c, :]
                    Ch = C[:, c, :]
                    bwt = [("V", v_b2d), ("V", v_b2i), eq_tok[j8], fq_tok[j8]]
                    act(Ch, Ehh, AF.Abs, bias=bias_d[:, 2 + c:3 + c], waits=bwt)
                    act(Ch, Ch, AF.Exp, scale=-1.0)
                    s_ln = act(Ch, Ch, AF.Ln, bias=1.0)
                    s_rl = act(Ehh, Ehh, AF.Relu, bias=bias_d[:, 2 + c:3 + c])
                    v_dl = vop(lambda e, Ehh=Ehh, Ch=Ch: e.tensor_add(Ehh, Ehh, Ch),
                               waits=[("S", s_ln), ("S", s_rl)])
                    s_dc = act(Ch, Ehh, AF.Exp, scale=negsA[:, 2 + c:3 + c],
                               waits=[("V", v_dl)])
                    vop(lambda e, Ehh=Ehh, Fhh=Fhh, c=c: e.scalar_tensor_tensor(
                        Fhh, Fhh, bias_i[:, 2 + c:3 + c], Ehh, ALU.add, ALU.mult))
                    init = 0.0 if j8 == 0 else D[:, c, j8 * 512 - 1:j8 * 512]
                    scan1_v = vop(lambda e, Ch=Ch, Fhh=Fhh, c=c, j8=j8, init=init:
                                  e.tensor_tensor_scan(D[:, c, j8 * 512:(j8 + 1) * 512],
                                                       Ch, Fhh, init, ALU.mult, ALU.add),
                                  waits=[("S", s_dc), ("T", y0_all_T)])
                q_done_v[j8] = scan1_v
                w = [("T", xt_lastT[6]), ("T", xt_lastT[7])] if j8 < 2 else                     [xt_wb[j8 - 2]]
                xt_tok[j8] = dma("sync", xt[:, sl, :, :],
                                 scr_xfm.ap()[:, :, qsl].rearrange("c p t -> p c t"),
                                 f"xt{sl}", waits=w + list(tk_p1))
                v_u1 = None
                for m in range(8):
                    bank = 5 + (m % 2)
                    bw = bank_waits(bank)
                    for k in range(2):
                        t = mm(pb[bank][:, :], wosb[:, k, m, :], D[:, k, qsl],
                               k == 0, k == 1,
                               waits=[("V", scan1_v), tkwo1] + (bw if k == 0 else []),
                               inc="T" if k == 1 else None)
                    v_u1 = vop(lambda e, bank=bank, m=m, sl=sl: e.scalar_tensor_tensor(
                        xt[:, sl, m, :], pb[bank][:, :], gacl[:, m:m + 1], xt[:, sl, m, :],
                        ALU.mult, ALU.add),
                        waits=[("T", t), xt_tok[j8], ("V", v_ga)])
                    note_bank(bank, "V", v_u1)
                xt_wb[j8] = dma("gpsimd",
                                scr_xfm.ap()[:, :, qsl].rearrange("c p t -> p c t"),
                                xt[:, sl, :, :], f"xw{sl}", waits=[("V", v_u1)])

            for fh in range(2):
                bw5 = bank_waits(5)
                for c in range(2):
                    t = mm(pb[5][0:1, :], D[:, c, 4095:4096],
                           wssb[:, c, fh * 512:(fh + 1) * 512], c == 0, False,
                           waits=[("V", scan1_v), tkws1] + (bw5 if c == 0 else []))
                t = mm(pb[5][0:1, :], onesb[0:1, 0:1], bs_row[0:1, DM + fh * 512:DM + (fh + 1) * 512],
                       False, True, inc="T")
                s_cs1 = act(csrow[0:1, fh * 512:(fh + 1) * 512], pb[5][0:1, :], AF.Copy,
                            waits=[("T", t), tk_cst0])
                note_bank(5, "S", s_cs1)
            tk_c1a = dma("gpsimd", scr_row.ap()[0:DM][None, :], csrow[0:1, :],
                         "gx", waits=[("S", s_cs1)])
            tk_cst1 = dma("gpsimd", trtmp[0:8, :],
                          scr_row.ap()[0:DM].rearrange("(a b) -> a b", b=128),
                          "gx", waits=[tk_c1a, ("S", s_csc)])
            t = tr(pb[6][:, 0:8], trtmp[0:8, :], ident[0:8, 0:8],
                   waits=[tk_cst1] + bank_waits(6), inc="T")
            s_cs1c = act(cs_cols[:, :], pb[6][:, 0:8], AF.Copy, waits=[("T", t)])
            note_bank(6, "S", s_cs1c)
            vop(lambda e: e.tensor_add(gc1[:, :], cs_cols[:, :], bo_cols[:, 8:16]),
                waits=[("S", s_cs1c), tk_bo])
            v_gc1 = vop(lambda e: e.tensor_mul(gc1[:, :], gacl[:, :], gc1[:, :]),
                        waits=[("V", v_ga)])

            p4_lastT = [None] * 8
            h2_st_tok = [None] * 8
            h2prev_l = [[None] * 8]
            xtm_st = {}
            v_up_toks = [None] * 8
            stats_out = [None] * 8

            def p4_b1(j):
                sl = j % 2
                jsl = slice(j * 512, (j + 1) * 512)
                if j < 2:
                    w = [xt_wb[6 + sl]]
                else:
                    w = [("T", p4_lastT[j - 2]), h2_st_tok[j - 2]]
                xpa = dma("sync", xt[:, sl, 0:4, :],
                          scr_xfm.ap()[0:4, :, jsl].rearrange("c p t -> p c t"),
                          f"xp{sl}", waits=w)
                xt_tok[j] = dma("sync", xt[:, sl, 4:8, :],
                                scr_xfm.ap()[4:8, :, jsl].rearrange("c p t -> p c t"),
                                f"xq{sl}", waits=w)
                v_up = vop(lambda e, sl=sl, xpa=xpa: e.tensor_tensor(
                    xt[:, sl, :, :], xt[:, sl, :, :],
                    gc1[:, :].rearrange("p (e o) -> p e o", o=1)
                    .broadcast_to([128, 8, 512]), ALU.add),
                    waits=[("V", v_gc1), xt_tok[j], xpa])
                v_up_toks[j] = v_up
                v_mu, s_rstd = stats_tile(xt[:, sl, :, :], sl, True, [("V", v_up)],
                                          sq4=C)
                v_m2r = vop(lambda e, sl=sl: e.tensor_mul(
                    rowtmp[0:1, sl, :], rows[0:1, sl, 0, :], rows[0:1, sl, 1, :]),
                    waits=[("S", s_rstd)])
                stats_out[j] = (v_mu, s_rstd, v_m2r)

            def p4_b2(j):
                sl = j % 2
                jsl = slice(j * 512, (j + 1) * 512)
                v_up = v_up_toks[j]
                s_rstd, v_m2r = stats_out[j][1], stats_out[j][2]
                t_r = mm(pb[0][:, :], onesb[0:1, 0:128], rows[0:1, sl, 1, :], True, True,
                         waits=[("S", s_rstd)] + bank_waits(0), inc="T")
                t_m = mm(pb[1][:, :], onesb[0:1, 0:128], rowtmp[0:1, sl, :], True, True,
                         waits=[("V", v_m2r)] + bank_waits(1), inc="T")
                for s4 in range(4):
                    stg = h1[:, :, :].rearrange("p m t -> p (m t)")[:, s4 * 1024:(s4 + 1) * 1024]
                    s_tok = v_tok = None
                    for c in range(8):
                        bank = 5 + (c % 2)
                        bw = bank_waits(bank)
                        t = tr(pb[bank][:, 0:128], xt[:, sl, c, s4 * 128:(s4 + 1) * 128],
                               ident,
                               waits=[("V", v_up)] + xtm_st.get((j - 1, s4), []) + bw,
                               inc="T")
                        if c % 2 == 0:
                            s_tok = act(stg[:, c * 128:(c + 1) * 128], pb[bank][:, 0:128],
                                        AF.Copy, waits=[("T", t)])
                            note_bank(bank, "S", s_tok)
                        else:
                            v_tok = vop(lambda e, stg=stg, c=c, bank=bank: e.tensor_copy(
                                stg[:, c * 128:(c + 1) * 128], pb[bank][:, 0:128]),
                                waits=[("T", t)])
                            note_bank(bank, "V", v_tok)
                    rsl = slice((j * 4 + s4) * 128, (j * 4 + s4 + 1) * 128)
                    xtm_st[(j, s4)] = [dma("gpsimd", scr_xtm.ap()[rsl, :], stg,
                                           f"xm{s4}",
                                           waits=[("S", s_tok), ("V", v_tok)])]
                    if stage == "p4":
                        xtm_st[(j, s4)].append(
                            dma("gpsimd", out_d.ap()[rsl, :], stg, f"om{s4}",
                                waits=[("S", s_tok), ("V", v_tok)]))
                h2st_c = [None] * 8
                lbank = (2, 3, 5, 6)
                lg_tok = [None] * 4
                lgT = [None] * 8
                for c in range(8):
                    hsl = h2r[:, c % 2, :]
                    wv1 = [("T", t_r)]
                    if c >= 2:
                        wv1 += [h2st_c[c - 2], ("T", lgT[c - 2])]
                    elif j > 0:
                        wv1 += [h2prev_l[0][6 + c]]
                    v1 = vop(lambda e, hsl=hsl, c=c, sl=sl: e.tensor_mul(
                        hsl, xt[:, sl, c, :], pb[0][:, :]), waits=wv1)
                    note_bank(0, "V", v1)
                    v2 = vop(lambda e, hsl=hsl: e.tensor_add(hsl, hsl, pb[1][:, :]),
                             waits=[("T", t_m)])
                    note_bank(1, "V", v2)
                    v_h2 = vop(lambda e, hsl=hsl, c=c: e.tensor_scalar(
                        hsl, hsl, onecb[:, c:c + 1],
                        mod_cols[:, 24 + c:25 + c], ALU.mult, ALU.add),
                        waits=[("V", v_onecb)])
                    for s4 in range(4):
                        bw = bank_waits(lbank[s4]) if c == 0 else []
                        lg_tok[s4] = mm(pb[lbank[s4]][:, 0:8],
                                        hsl[:, s4 * 128:(s4 + 1) * 128],
                                        gw_sb[:, c, :], c == 0, c == 7,
                                        waits=[("V", v_h2), tk_gw] + bw,
                                        inc="T" if s4 == 3 else None)
                    lgT[c] = lg_tok[3]
                    h2st_c[c] = dma("gpsimd",
                                    scr_h2.ap()[c, :, jsl], hsl, f"h2{c % 2}",
                                    waits=[("V", v_h2)])
                h2_st_tok[j] = h2st_c[7]
                h2prev_l[0] = h2st_c
                R = rt4
                for s4 in range(4):
                    s_sg = act(R[:, s4, 0:8], pb[lbank[s4]][:, 0:8], AF.Exp, scale=-1.0,
                               waits=[("T", lgT[7]), ("V", p.cnt["V"])])
                    note_bank(lbank[s4], "S", s_sg)
                vop(lambda e: e.tensor_scalar_add(R[:, :, 0:8], R[:, :, 0:8], 1.0),
                    waits=[("S", s_sg)])
                vop(lambda e: e.reciprocal(R[:, :, 8:16], R[:, :, 0:8]))
                vop(lambda e: e.tensor_add(R[:, :, 8:16], R[:, :, 8:16], gbias_bc[:, :, :]),
                    waits=[("S", s_gbias)])
                vop(lambda e: e.tensor_add(R[:, :, 16:24:2], R[:, :, 8:16:2], R[:, :, 9:16:2]))
                vop(lambda e: e.tensor_copy(R[:, :, 17:24:2], R[:, :, 16:24:2]))
                vop(lambda e: e.tensor_max(R[:, :, 24:25], R[:, :, 16:17], R[:, :, 20:21]))
                vop(lambda e: e.tensor_tensor(R[:, :, 25:26], R[:, :, 16:17], R[:, :, 20:21], ALU.min))
                vop(lambda e: e.tensor_max(R[:, :, 26:27], R[:, :, 18:19], R[:, :, 22:23]))
                vop(lambda e: e.tensor_tensor(R[:, :, 27:28], R[:, :, 18:19], R[:, :, 22:23], ALU.min))
                vop(lambda e: e.tensor_tensor(R[:, :, 28:29], R[:, :, 24:25], R[:, :, 26:27], ALU.is_ge))
                vop(lambda e: e.tensor_mul(R[:, :, 29:30], R[:, :, 28:29], R[:, :, 25:26]))
                vop(lambda e: e.tensor_scalar(R[:, :, 28:29], R[:, :, 28:29], -1.0, 1.0,
                                              ALU.mult, ALU.add))
                vop(lambda e: e.tensor_mul(R[:, :, 28:29], R[:, :, 28:29], R[:, :, 27:28]))
                vop(lambda e: e.tensor_add(R[:, :, 29:30], R[:, :, 29:30], R[:, :, 28:29]))
                vop(lambda e: e.tensor_tensor(R[:, :, 30:31], R[:, :, 24:25], R[:, :, 26:27], ALU.min))
                vop(lambda e: e.tensor_max(R[:, :, 31:32], R[:, :, 30:31], R[:, :, 29:30]))
                vop(lambda e: e.tensor_tensor(
                    R[:, :, 32:40], R[:, :, 16:24],
                    R[:, :, 31:32].broadcast_to([128, 4, 8]), ALU.is_ge))
                vop(lambda e: e.tensor_mul(R[:, :, 40:48], R[:, :, 8:16], R[:, :, 32:40]))
                for s4 in range(4):
                    vop(lambda e, s4=s4: e.max(m8b4[:, s4, :], R[:, s4, 40:48]))
                vop(lambda e: e.tensor_copy(rp4[:, :, 0:2], m8b4[:, :, 0:2]),
                    waits=[("V", v_rep2)])
                for s4 in range(4):
                    vop(lambda e, s4=s4: e.match_replace(
                        R[:, s4, 32:40], rp4[:, s4, :], R[:, s4, 40:48], -1.0e30))
                vop(lambda e: e.tensor_single_scalar(R[:, :, 32:40], R[:, :, 32:40],
                                                     -1.0e29, ALU.is_le))
                vop(lambda e: e.tensor_mul(R[:, :, 32:40], R[:, :, 32:40], R[:, :, 8:16]))
                vop(lambda e: e.reduce_sum(R[:, :, 24:25], R[:, :, 32:40], AXX))
                vop(lambda e: e.tensor_scalar_add(R[:, :, 24:25], R[:, :, 24:25], 1.0e-20))
                vop(lambda e: e.reciprocal(R[:, :, 25:26], R[:, :, 24:25]))
                vop(lambda e: e.tensor_mul(
                    R[:, :, 40:48], R[:, :, 32:40],
                    R[:, :, 25:26].broadcast_to([128, 4, 8])))
                v_w = vop(lambda e, j=j: e.tensor_scalar_mul(
                    wtok[:, j * 4:(j + 1) * 4, :], R[:, :, 40:48], 2.5))
                p4_lastT[j] = p.cnt["T"]

            for j in range(9):
                if j < 8:
                    p4_b1(j)
                if j >= 1:
                    p4_b2(j - 1)

            attn_bar = p.all_waits()

        if stage == "p4":
            pass
        else:
            with ExitStack() as ffn:
                fn_ = ffn.enter_context
                stgb = fn_(nc.sbuf_tensor("stgb", [128, 8, 1024], F32))
                h2hh = fn_(nc.sbuf_tensor("h2hh", [128, 8, 1024], BF16))
                h2hl = fn_(nc.sbuf_tensor("h2hl", [128, 8, 1024], BF16))
                h2st = fn_(nc.sbuf_tensor("h2st", [128, 2, 1024], F32))
                wbgh = fn_(nc.sbuf_tensor("wbgh", [128, 2, 8, 512], BF16))
                wbgl = fn_(nc.sbuf_tensor("wbgl", [128, 2, 8, 512], BF16))
                wbuh = fn_(nc.sbuf_tensor("wbuh", [128, 2, 8, 512], BF16))
                wbul = fn_(nc.sbuf_tensor("wbul", [128, 2, 8, 512], BF16))
                wbdh = fn_(nc.sbuf_tensor("wbdh", [128, 2, 4, 2, 512], BF16))
                wbdl = fn_(nc.sbuf_tensor("wbdl", [128, 2, 4, 2, 512], BF16))
                wst = fn_(nc.sbuf_tensor("wst", [128, 3, 512], F32))
                mbuf = fn_(nc.sbuf_tensor("mbuf", [128, 512], F32))
                mh = fn_(nc.sbuf_tensor("mh", [128, 4, 512], BF16))
                ml = fn_(nc.sbuf_tensor("ml", [128, 4, 512], BF16))
                sgb = fn_(nc.sbuf_tensor("sgb", [128, 2, 512], F32))
                wrh = fn_(nc.sbuf_tensor("wrh", [8, 256], F32))
                wrow = fn_(nc.sbuf_tensor("wrow", [1, 2, 512], F32))

                first = {e: True for e in ENGINES}

                def bw_first(eng):
                    if first[eng]:
                        first[eng] = False
                        return list(attn_bar)
                    return []

                out_toks = []
                out_by_slot = {}
                h2_split = {0: [], 1: []}
                wr_state = {"tk_w8": None, "wrow_tok": None}

                def emit_wrows(hq, bank):
                    for g2 in range(4):
                        bk = bank if g2 % 2 == 0 else 6
                        bw = bank_waits(bk)
                        t = None
                        for ss4 in range(2):
                            ss = g2 * 2 + ss4
                            t = tr(pb[bk][0:8, ss4 * 128:(ss4 + 1) * 128],
                                   wtok[:, hq * 8 + ss, :], ident,
                                   waits=bw_first("tensor") + bw, inc="T")
                            bw = []
                        s_wr = act(wrh[0:8, :], pb[bk][0:8, 0:256],
                                   AF.Copy, waits=bw_first("scalar") +
                                   [("T", t), wr_state["tk_w8"]])
                        note_bank(bk, "S", s_wr)
                        wr_state["tk_w8"] = dma(
                            "gpsimd", scr_w8.ap()[:, g2 * 256:(g2 + 1) * 256],
                            wrh[0:8, :], "w8",
                            waits=bw_first("gpsimd") + [("S", s_wr)] +
                                  ([wr_state["wrow_tok"]] if g2 == 0 else []))

                for h in range(4):
                    emit_wrows(h, 3)
                    tk_w8 = wr_state["tk_w8"]
                    stg_tok = []
                    for jj in range(2):
                        wx = bw_first("sync") + ([out_by_slot[jj]] if jj in out_by_slot else [])
                        stg_tok.append(dma(
                            "sync", stgb[:, jj * 4:(jj + 1) * 4, :],
                            scr_xtm.ap()[h * 1024 + jj * 512: h * 1024 + (jj + 1) * 512, :]
                            .rearrange("(s p) f -> p s f", p=128), f"sgl{jj}", waits=wx))
                    t_snap = p.cnt["T"]
                    h2S = h2V = None
                    for c in range(8):
                        slc = c % 2
                        tkc = dma("sync", h2st[:, slc, :],
                                  scr_h2.ap()[c, :, h * 1024:(h + 1) * 1024],
                                  f"hh{slc}", waits=bw_first("sync") + h2_split[slc])
                        h2S = act(h2hh[:, c, :], h2st[:, slc, :], AF.Copy,
                                  waits=[tkc, ("T", t_snap)])
                        h2V = vop(lambda e, c=c, slc=slc: e.tensor_tensor(
                            h2hl[:, c, :], h2st[:, slc, :], h2hh[:, c, :], ALU.subtract),
                            waits=bw_first("vector") + [tkc, ("S", h2S)])
                        h2_split[slc] = [("S", h2S), ("V", h2V)]

                    def esrc(e10):
                        if e10 < 8:
                            return (wg_d.ap()[e10].rearrange("(k p) q -> p k q", p=128),
                                    wu_d.ap()[e10].rearrange("(k p) q -> p k q", p=128),
                                    wn_d.ap()[e10].rearrange("(k p) q -> p k q", p=128))
                        ih = e10 - 8
                        return (sg_d.ap()[:, ih * 512:(ih + 1) * 512].rearrange(
                                    "(k p) q -> p k q", p=128),
                                su_d.ap()[:, ih * 512:(ih + 1) * 512].rearrange(
                                    "(k p) q -> p k q", p=128),
                                sd_d.ap()[ih * 512:(ih + 1) * 512, :].rearrange(
                                    "(k p) q -> p k q", p=128))

                    def preload(e10):
                        slw = e10 % 2
                        wprev = [("T", slot_T[slw])] if slot_T[slw] else []
                        toks = {}
                        if h == 0:
                            po = list(pair_out[slw])
                            g_src, u_src, d_src = esrc(e10)
                            qctr = [0]

                            def mk_stage(nm, q, srcap, dsth, dstl, sh, sl_,
                                         wprev=wprev, po=po, e10=e10, slw=slw,
                                         toks=toks):
                                fold = nm == "d"
                                if not fold:
                                    sap = srcap[:, q, :]
                                    dh = dsth[:, slw, q, :]
                                    dl = dstl[:, slw, q, :]
                                    oh = sh.ap()[e10][:, q, :]
                                    ol = sl_.ap()[e10][:, q, :]
                                else:
                                    kq, fhh = q // 2, q % 2
                                    sap = srcap[:, kq, fhh * 512:(fhh + 1) * 512]
                                    dh = dsth[:, slw, kq, fhh, :]
                                    dl = dstl[:, slw, kq, fhh, :]
                                    oh = sh.ap()[e10][:, kq, fhh, :]
                                    ol = sl_.ap()[e10][:, kq, fhh, :]

                                def emit():
                                    qs = qctr[0] % 3
                                    qctr[0] += 1
                                    tk = dma("sync", wst[:, qs, :], sap, f"ws{qs}",
                                             waits=bw_first("sync") + stage_guard[qs])
                                    vf = None
                                    if fold:
                                        vf = vop(lambda e, qs=qs, fhh=q % 2:
                                                 e.tensor_mul(wst[:, qs, :],
                                                              wst[:, qs, :],
                                                              gb_bc[:, fhh, :]),
                                                 waits=bw_first("vector") + [tk])
                                    sS = act(dh, wst[:, qs, :], AF.Copy,
                                             waits=[tk, ("V", vf)] + wprev + po)
                                    sV = vop(lambda e, dl=dl, qs=qs, dh=dh:
                                             e.tensor_tensor(dl, wst[:, qs, :], dh,
                                                             ALU.subtract),
                                             waits=[tk, ("S", sS)] + po)
                                    oh_t = dma("gpsimd", oh, dh, "pwoh",
                                               waits=bw_first("gpsimd") + [("S", sS)])
                                    ol_t = dma("gpsimd", ol, dl, "pwol",
                                               waits=[("V", sV)])
                                    stage_guard[qs] = [("S", sS), ("V", sV)]
                                    if q == 7:
                                        toks[nm] = [("S", sS), ("V", sV)]
                                        if nm == "d":
                                            pair_out[slw] = [oh_t, ol_t]
                                return emit

                            for nm, srcap, dsth, dstl, sh, sl_ in (
                                    ("g", g_src, wbgh, wbgl, scr_gh, scr_gl),
                                    ("u", u_src, wbuh, wbul, scr_uh, scr_ul),
                                    ("d", d_src, wbdh, wbdl, scr_dh, scr_dl)):
                                for q in range(8):
                                    pend.append(mk_stage(nm, q, srcap, dsth,
                                                         dstl, sh, sl_))
                        else:
                            pw = [("pwoh", p.cnt["pwoh"]), ("pwol", p.cnt["pwol"])]
                            for nm, dh, dl, sh, sl_ in (
                                    ("g", wbgh, wbgl, scr_gh, scr_gl),
                                    ("u", wbuh, wbul, scr_uh, scr_ul),
                                    ("d", wbdh, wbdl, scr_dh, scr_dl)):
                                dst = dh[:, slw] if nm == "d" else dh[:, slw, :, :]
                                dstl = dl[:, slw] if nm == "d" else dl[:, slw, :, :]
                                dma("sync", dst, sh.ap()[e10], f"w{nm}{slw}",
                                    waits=bw_first("sync") + wprev + pw)
                                tl = dma("sync", dstl, sl_.ap()[e10], f"w{nm}{slw}",
                                         waits=wprev)
                                toks[nm] = [tl]
                        return toks

                    if h == 0:
                        slot_T = [None, None]
                        stage_guard = {0: [], 1: [], 2: []}
                        pair_out = {0: [], 1: []}
                        sgb_rd = {0: None, 1: None}
                        mh_S = [None]
                        wrow_T = None

                    def wrow_load(e10):
                        wr_state["wrow_tok"] = dma(
                            "sync", wrow[0:1, :, :],
                            scr_w8.ap()[e10][None, :].rearrange(
                                "a (j t) -> a j t", j=2), "wr",
                            waits=[tk_w8, ("T", wrow_T)])
                        return wr_state["wrow_tok"]

                    wrow_tok = wrow_load(0)
                    pend = []

                    def drain(n):
                        for _ in range(min(n, len(pend))):
                            pend.pop(0)()

                    pre = {0: preload(0)}
                    ev_tok = {}
                    for e10 in range(10):
                        routed = e10 < 8
                        slw = e10 % 2
                        drain(len(pend))
                        gtk, utk, dtk = pre[e10]["g"], pre[e10]["u"], pre[e10]["d"]
                        if e10 + 1 < 10:
                            pre[e10 + 1] = preload(e10 + 1)
                        for jj in range(2):
                            jsl = slice(jj * 512, (jj + 1) * 512)
                            wbc_T = None
                            if routed:
                                bw = bank_waits(6)
                                wbc_T = mm(pb[6][:, :], onesb[0:1, 0:128],
                                           wrow[0:1, jj, :], True, True,
                                           waits=[wrow_tok] + bw, inc="T")
                                if jj == 1:
                                    wrow_T = wbc_T
                                    if e10 + 1 < 8:
                                        wrow_tok = wrow_load(e10 + 1)
                            s_mh = [None] * 4
                            v_ml = [None] * 4
                            u_T = [None] * 4
                            for m in range(4):
                                gbank = m % 2
                                bw = bank_waits(gbank)
                                fw = gtk + [("S", h2S), ("V", h2V)] + bw
                                for pi, (lw, rh) in enumerate(
                                        ((wbgh, h2hh), (wbgh, h2hl), (wbgl, h2hh))):
                                    for k in range(8):
                                        st = pi == 0 and k == 0
                                        sp = pi == 2 and k == 7
                                        t = mm(pb[gbank][:, :],
                                               lw[:, slw, k, m * 128:(m + 1) * 128],
                                               rh[:, k, jsl], st, sp,
                                               waits=fw if st else [],
                                               inc="T" if sp else None)
                                g_T = t
                                ubank = 2 + m % 2
                                bw = bank_waits(ubank)
                                fw = utk + bw
                                for pi, (lw, rh) in enumerate(
                                        ((wbuh, h2hh), (wbuh, h2hl), (wbul, h2hh))):
                                    for k in range(8):
                                        st = pi == 0 and k == 0
                                        sp = pi == 2 and k == 7
                                        t = mm(pb[ubank][:, :],
                                               lw[:, slw, k, m * 128:(m + 1) * 128],
                                               rh[:, k, jsl], st, sp,
                                               waits=fw if st else [],
                                               inc="T" if sp else None)
                                u_T[m] = t
                                s_si = act(sgb[:, m % 2, :], pb[gbank][:, :], AF.Silu,
                                           waits=[("T", g_T), ("V", sgb_rd[m % 2])])
                                note_bank(gbank, "S", s_si)
                                v1 = vop(lambda e, m=m, ubank=ubank: e.tensor_mul(
                                    mbuf[:, :], sgb[:, m % 2, :], pb[ubank][:, :]),
                                    waits=[("S", s_si), ("T", u_T[m]),
                                           ("S", mh_S[0])])
                                sgb_rd[m % 2] = v1
                                note_bank(ubank, "V", v1)
                                if routed:
                                    v1 = vop(lambda e: e.tensor_mul(
                                        mbuf[:, :], mbuf[:, :], pb[6][:, :]),
                                        waits=[("T", wbc_T)])
                                    note_bank(6, "V", v1)
                                s_mh[m] = act(mh[:, m, :], mbuf[:, :], AF.Copy,
                                              waits=[("V", v1)])
                                mh_S[0] = s_mh[m]
                                v_ml[m] = vop(lambda e, m=m: e.tensor_tensor(
                                    ml[:, m, :], mbuf[:, :], mh[:, m, :], ALU.subtract),
                                    waits=[("S", s_mh[m])])
                                drain(2)
                            for ts in range(4):
                                for fhh in range(2):
                                    bank = 4 + (ts * 2 + fhh) % 2
                                    bw = bank_waits(bank)
                                    fw = [("S", s_mh[3]), ("V", v_ml[3])] + dtk + bw
                                    for pi, (lh, rh) in enumerate(
                                            ((mh, wbdh), (mh, wbdl), (ml, wbdh))):
                                        for k in range(4):
                                            st = pi == 0 and k == 0
                                            sp = pi == 2 and k == 3
                                            t = mm(pb[bank][:, :],
                                                   lh[:, k, ts * 128:(ts + 1) * 128],
                                                   rh[:, slw, k, fhh, :], st, sp,
                                                   waits=fw if st else [],
                                                   inc="T" if sp else None)
                                    v_ev = vop(lambda e, jj=jj, ts=ts, fhh=fhh, bank=bank:
                                               e.tensor_add(
                                                   stgb[:, jj * 4 + ts, fhh * 512:(fhh + 1) * 512],
                                                   stgb[:, jj * 4 + ts, fhh * 512:(fhh + 1) * 512],
                                                   pb[bank][:, :]),
                                               waits=[("T", t), stg_tok[jj]])
                                    note_bank(bank, "V", v_ev)
                                    ev_tok[(jj, ts, fhh)] = v_ev
                                    drain(1)
                        slot_T[slw] = p.cnt["T"]
                    for jj in range(2):
                        wv = [("V", ev_tok[(jj, ts, fhh)]) for ts in range(4) for fhh in range(2)]
                        out_toks.append(dma(
                            "gpsimd",
                            out_d.ap()[h * 1024 + jj * 512: h * 1024 + (jj + 1) * 512, :]
                            .rearrange("(s p) f -> p s f", p=128),
                            stgb[:, jj * 4:(jj + 1) * 4, :], f"ob{jj}",
                            waits=bw_first("gpsimd") + wv))
                        out_by_slot[jj] = out_toks[-1]

        finals = [(s, v) for s, v in p.all_waits() if s not in ("T", "V", "S") and v > 0]

        def _final_wait(e, fs=tuple(finals)):
            for s, v in fs:
                e.wait_ge(p.hand[s], v)
        p.th["gpsimd"].append(_final_wait)

        with nc.Block() as block:

            @block.tensor
            def _(eng):
                for th in p.th["tensor"]:
                    th(eng)

            @block.vector
            def _(eng):
                for th in p.th["vector"]:
                    th(eng)

            @block.scalar
            def _(eng):
                for th in p.th["scalar"]:
                    th(eng)

            @block.sync
            def _(eng):
                for th in p.th["sync"]:
                    th(eng)

            @block.gpsimd
            def _(eng):
                for th in p.th["gpsimd"]:
                    th(eng)

    return nc


def _host_consts():
    cst = np.zeros((128, 257), dtype=np.float32)
    cst[:, 0:128] = np.eye(128, dtype=np.float32)
    cst[:, 128:256] = 1.0
    cst[:, 256] = 1.0 / 1024.0
    return cst


def kernel(**inputs):
    g = lambda k: np.ascontiguousarray(np.asarray(inputs[k], dtype=np.float32))
    xall = g("L0_attn_shot_embeddings")
    cst = _host_consts()
    shared = dict(
        wmod=g("L0_wmod"), wd=g("L0_attn_Wd"), bd=g("L0_attn_bd"),
        wi=g("L0_attn_Wi"), bi=g("L0_attn_bi"), alog=g("L0_attn_A_log"),
        wo=g("L0_attn_Wo"), bo=g("L0_attn_bo"), wsm=g("L0_attn_Ws"),
        bsv=g("L0_attn_bs"), gw=g("L0_ffn_gate_w"),
        gbias=g("L0_ffn_gate_bias").reshape(1, 8),
        wg=g("L0_ffn_w_gate"), wu=g("L0_ffn_w_up"), wdn=g("L0_ffn_w_down"),
        swg=g("L0_ffn_sw_gate"), swu=g("L0_ffn_sw_up"), swd=g("L0_ffn_sw_down"),
        cst_hbm=cst,
    )
    call = g("L0_c")
    nc = build_nc()
    in_maps = [dict(x_in=np.ascontiguousarray(xall[b]),
                    cvec=np.ascontiguousarray(call[b]), **shared)
               for b in range(8)]
    res = run_bass_kernel_spmd(nc, in_maps, core_ids=list(range(8)))
    return np.stack([res.results[b]["out"] for b in range(8)], axis=0)


if __name__ == "__main__":
    import reference as R
    ins = {k: np.asarray(v) for k, v in R.setup_inputs().items()}
    out = kernel(**ins)
    ref = np.asarray(R.reference(**{k: v for k, v in ins.items()}))
    err = np.linalg.norm(out - ref) / np.linalg.norm(ref)
    print("rel err:", err)
